# version 21
# baseline (speedup 1.0000x reference)
import math
from contextlib import ExitStack

import numpy as np
import ml_dtypes

import concourse.bass as bass
import concourse.mybir as mybir
from concourse.bass_utils import run_bass_kernel_spmd

F32 = mybir.dt.float32
BF16 = mybir.dt.bfloat16
I32 = mybir.dt.int32
AF = mybir.ActivationFunctionType
ALU = mybir.AluOpType

NCORE = 8
S_TOK = 4096
DM = 1024
NCH = 8
SPAN = 256
NSPAN = S_TOK // SPAN
TPS = SPAN // 128
NTILE = S_TOK // 128

LN2 = math.log(2.0)
SEED_SCALE = -LN2 / 256.0
SEED_BIAS = 127.0 * LN2 / 2.0
NEWTON_ITERS = 2

ENG_SEM = {"T": "t", "V": "v", "S": "s", "SY": "sy", "G": "g"}


class _AutoDrain:

    _WRAP = frozenset({"tensor_scalar", "scalar_tensor_tensor",
                       "tensor_tensor_scan", "tensor_tensor",
                       "tensor_tensor_reduce", "reduce_sum", "tensor_reduce",
                       "tensor_copy"})

    def __init__(self, e, junk):
        self._e = e
        self._junk = junk

    def __getattr__(self, n):
        f = getattr(self._e, n)
        if n in self._WRAP:
            def g(*a, **k):
                f(*a, **k)
                return self._e.drain()
            return g
        return f


class Prog:

    def __init__(self):
        self.ops = []
        self.cnt = {s: 0 for s in ("t", "v", "s", "sy", "g")}
        self.batch_end = {"sy": {}, "g": {}}
        self.junk = None

    def add(self, eng, fn, deps=(), inc=0):
        sem = ENG_SEM[eng]
        tok = None
        if inc:
            self.cnt[sem] += inc
            tok = (sem, self.cnt[sem])
        self.ops.append([eng, fn, tuple(d for d in deps if d is not None), inc, sem,
                         self.cnt[sem]])
        return tok

    def finalize(self):
        cur = {"sy": [], "g": []}

        def close(q):
            if cur[q]:
                endv = cur[q][-1]
                for v in cur[q]:
                    self.batch_end[q][v] = endv
                cur[q] = []

        for eng, fn, deps, inc, sem, val in self.ops:
            if eng not in ("SY", "G"):
                continue
            q = ENG_SEM[eng]
            if deps:
                close(q)
            if inc:
                cur[q].append(val)
        close("sy")
        close("g")

    def emit(self, eng_name, e, semh):
        own = ENG_SEM[eng_name]
        skip_own = own in ("t", "v", "s")
        is_q = own in ("sy", "g")
        wait_e = e
        if eng_name == "V":
            e = _AutoDrain(e, self.junk)
        waited = {}
        for name, fn, deps, inc, sem, val in self.ops:
            if name != eng_name:
                continue
            emitted = False
            rdeps = []
            for (s, v) in deps:
                if s in self.batch_end:
                    v = self.batch_end[s].get(v, v)
                if skip_own and s == own:
                    continue
                rdeps.append((s, v))
            for (s, v) in rdeps:
                if waited.get(s, -1) >= v:
                    continue
                wait_e.wait_ge(semh[s], v)
                waited[s] = v
                emitted = True
            if is_q and inc and deps and not emitted and rdeps:
                s, v = rdeps[0]
                wait_e.wait_ge(semh[s], v)
            inst = fn(e)
            if inc:
                inst.then_inc(semh[sem], inc)


def build_nc(nspan=NSPAN, debug=False):
    nc = bass.Bass("TRN2", target_bir_lowering=False,
                   detect_race_conditions=False)

    d_xtm = nc.dram_tensor("xtm", [S_TOK, DM], F32, kind="ExternalInput").ap()
    d_xt = nc.dram_tensor("xt", [NSPAN, 128, NCH * SPAN], F32,
                          kind="ExternalInput").ap()
    dw = {}
    for wname in ("wint", "wigt", "wrgt", "woutt", "wrt", "wkt", "wvt"):
        dw[wname] = nc.dram_tensor(wname, [DM, DM], F32, kind="ExternalInput").ap()
    dvec = {}
    for vname in ("lam", "big", "brg", "mur", "muk", "h0", "nw"):
        dvec[vname] = nc.dram_tensor(vname, [128, NCH], F32,
                                     kind="ExternalInput").ap()
    d_out = nc.dram_tensor("out", [S_TOK, DM], F32, kind="ExternalOutput").ap()

    ident_h = nc.inline_tensor(np.eye(128, dtype=np.float32), "identf32")
    ones_h = nc.inline_tensor(np.ones((128, 128), dtype=ml_dtypes.bfloat16),
                              "onesbf")

    P = Prog()

    with ExitStack() as ctx:
        E = ctx.enter_context
        w_sb = {n: E(nc.sbuf_tensor(f"sb_{n}", [128, NCH * DM], BF16))
                for n in ("wint", "wigt", "wrgt", "woutt", "wrt", "wkt", "wvt")}
        xt_bf = [E(nc.sbuf_tensor(f"xtbf{i}", [128, NCH * SPAN], BF16))
                 for i in range(2)]
        xtm = [E(nc.sbuf_tensor(f"xtm{i}", [128, DM], F32)) for i in range(2)]
        B_tr = E(nc.sbuf_tensor("Btr", [128, NCH * SPAN], BF16))
        B_ti = E(nc.sbuf_tensor("Bti", [128, NCH * SPAN], BF16))
        B_xp = E(nc.sbuf_tensor("Bxp", [128, NCH * SPAN], BF16))
        B_y = E(nc.sbuf_tensor("By", [128, NCH * SPAN], BF16))
        B_a = [E(nc.sbuf_tensor(f"Ba{i}", [128, NCH * SPAN], F32))
               for i in range(2)]
        B_o = E(nc.sbuf_tensor("Bo", [128, NCH * SPAN], F32))
        B_obf = E(nc.sbuf_tensor("Bobf", [128, NCH * SPAN], BF16))
        uT0 = E(nc.sbuf_tensor("uT0", [128, NCH * (SPAN + 1)], BF16))
        B_xrT = E(nc.sbuf_tensor("BxrT", [128, NCH * SPAN], BF16))
        B_xkT = E(nc.sbuf_tensor("BxkT", [128, NCH * SPAN], BF16))
        B_w = [E(nc.sbuf_tensor(f"Bw{i}", [128, DM], F32)) for i in range(2)]
        B_u = [E(nc.sbuf_tensor(f"Bu{i}", [128, DM], F32)) for i in range(2)]
        B_R = [E(nc.sbuf_tensor(f"BR{i}", [128, DM], BF16)) for i in range(2)]
        sv = {n: E(nc.sbuf_tensor(f"v_{n}", [128, NCH], F32))
              for n in ("lam", "big", "brg", "mur", "muk", "h0", "nw",
                        "q", "hcl", "hbig", "hbrg", "ts1", "ts2")}
        hcar = [E(nc.sbuf_tensor(f"hcar{i}", [128, NCH], F32)) for i in range(2)]
        t_ident = E(nc.sbuf_tensor("identsb", [128, 128], F32))
        t_ones = E(nc.sbuf_tensor("onessb", [128, 128], BF16))
        zf32 = E(nc.sbuf_tensor("zf32", [128, 2], F32))
        zi32 = E(nc.sbuf_tensor("zi32", [128, 2], I32))
        zcv = E(nc.sbuf_tensor("zcv", [128, 2], F32))
        zsd = E(nc.sbuf_tensor("zsd", [128, 2], F32))
        znt = E(nc.sbuf_tensor("znt", [128, 2], F32))
        rstd = E(nc.sbuf_tensor("rstd", [128, 2], F32))
        rstd1 = E(nc.sbuf_tensor("rstd1", [128, 1], F32))
        msum = E(nc.sbuf_tensor("msum", [128, 1], F32))
        mcol = E(nc.sbuf_tensor("mcol", [128, 1], F32))
        junk = E(nc.sbuf_tensor("junk", [128, 2], F32))
        P.junk = junk
        c_sqb = E(nc.sbuf_tensor("c_sqb", [128, 1], F32))
        c_sdb = E(nc.sbuf_tensor("c_sdb", [128, 1], F32))
        if debug:
            dbgw = E(nc.sbuf_tensor("dbgw", [128, DM], BF16))
            dbgu = E(nc.sbuf_tensor("dbgu", [128, DM], BF16))
            dbgr = E(nc.sbuf_tensor("dbgr", [128, 1], F32))

        kaT = [B_a[i][:].bitcast(BF16)[:, 0:NCH * SPAN] for i in range(2)]

        pg = [E(nc.psum_tensor(f"pg{i}", [128, 512], F32)) for i in range(2)]
        pa = [E(nc.psum_tensor(f"pa{i}", [128, 1024], F32)) for i in range(2)]
        pk = [E(nc.psum_tensor(f"pk{i}", [128, 512], F32)) for i in range(2)]

        semh = {s: E(nc.semaphore(s)) for s in ("t", "v", "s", "sy", "g")}
        blk = E(nc.Block())

        wdma = {}
        for n in ("wrgt", "wigt", "wint"):
            s3 = dw[n].rearrange("(dc p) h -> p dc h", p=128)
            d3 = w_sb[n][:].rearrange("p (dc h) -> p dc h", dc=NCH)
            wdma[n] = P.add("G", (lambda e, s=s3, d=d3: e.dma_start(out=d, in_=s)),
                            inc=16)
        xtdma = {}
        for k in range(2):
            xtdma[k] = P.add(
                "G", (lambda e, k=k: e.dma_start(out=xt_bf[k][:],
                                                 in_=d_xt[k, :, :])),
                inc=16)
        for n in ("woutt", "wrt", "wkt", "wvt"):
            s3 = dw[n].rearrange("(dc p) h -> p dc h", p=128)
            d3 = w_sb[n][:].rearrange("p (dc h) -> p dc h", dc=NCH)
            wdma[n] = P.add("G", (lambda e, s=s3, d=d3: e.dma_start(out=d, in_=s)),
                            deps=[("g", P.cnt["g"])], inc=16)

        vdma = {}
        for n in ("lam", "big", "brg", "mur", "muk", "h0", "nw"):
            vdma[n] = P.add(
                "SY", (lambda e, n=n: e.dma_start(out=sv[n][:], in_=dvec[n])),
                inc=16)
        vdma["ident"] = P.add(
            "SY", (lambda e: e.dma_start(out=t_ident[:], in_=ident_h.ap())), inc=16)
        vdma["ones"] = P.add(
            "SY", (lambda e: e.dma_start(out=t_ones[:], in_=ones_h.ap())), inc=16)
        xtmdma = {}
        for t in range(2):
            xtmdma[t] = P.add(
                "SY", (lambda e, t=t: e.dma_start(
                    out=xtm[t % 2][:], in_=d_xtm[t * 128:(t + 1) * 128, :])),
                inc=16)

        wout_scaled = []
        for hc in range(NCH):
            tok = P.add("V", (lambda e, hc=hc: e.tensor_scalar(
                out=w_sb["woutt"][:, hc * DM:(hc + 1) * DM],
                in0=w_sb["woutt"][:, hc * DM:(hc + 1) * DM],
                scalar1=sv["nw"][:, hc:hc + 1], scalar2=None, op0=ALU.mult)),
                deps=[wdma["woutt"], vdma["nw"]], inc=1)
            wout_scaled.append(tok)

        qtok = P.add("S", (lambda e: e.activation(
            out=sv["q"][:], in_=sv["lam"][:], func=AF.Exp, scale=-1.0)),
            deps=[vdma["lam"]], inc=1)
        q, t1, t2 = sv["q"], sv["ts1"], sv["ts2"]

        def _tay(e):
            e.tensor_scalar(out=t1[:], in0=q[:], scalar1=-0.25, scalar2=1.0 / 3.0,
                            op0=ALU.mult, op1=ALU.add)
            e.tensor_tensor(out=t2[:], in0=t1[:], in1=q[:], op=ALU.mult)
            e.tensor_scalar(out=t1[:], in0=t2[:], scalar1=-1.0, scalar2=0.5,
                            op0=ALU.mult, op1=ALU.add)
            e.tensor_tensor(out=t2[:], in0=t1[:], in1=q[:], op=ALU.mult)
            e.tensor_scalar(out=t1[:], in0=t2[:], scalar1=-1.0, scalar2=1.0,
                            op0=ALU.mult, op1=ALU.add)
            e.tensor_tensor(out=t2[:], in0=t1[:], in1=q[:], op=ALU.mult)
            e.tensor_scalar(out=t1[:], in0=t2[:], scalar1=-1.0, scalar2=1e-8,
                            op0=ALU.mult, op1=ALU.add)
            e.scalar_tensor_tensor(out=t2[:], in0=q[:], scalar=1e-8, in1=t1[:],
                                   op0=ALU.mult, op1=ALU.add)
            return e.tensor_scalar(out=sv["hcl"][:], in0=t2[:], scalar1=4.0,
                                   scalar2=None, op0=ALU.mult)

        hcl_tok = P.add("V", _tay, deps=[qtok], inc=1)
        hbig_tok = P.add("V", (lambda e: e.tensor_scalar(
            out=sv["hbig"][:], in0=sv["big"][:], scalar1=0.5, scalar2=None,
            op0=ALU.mult)), deps=[vdma["big"]], inc=1)
        hbrg_tok = P.add("V", (lambda e: e.tensor_scalar(
            out=sv["hbrg"][:], in0=sv["brg"][:], scalar1=0.5, scalar2=None,
            op0=ALU.mult)), deps=[vdma["brg"]], inc=1)
        sqb_tok = P.add("V", (lambda e: e.memset(c_sqb[:], 1.000001)), inc=1)
        sdb_tok = P.add("V", (lambda e: e.memset(c_sdb[:], SEED_BIAS)), inc=1)
        ut0z = P.add("V", (lambda e: e.memset(
            uT0[:].rearrange("p (hc c) -> p hc c", hc=NCH)[:, :, 0:1], 0.0)),
            inc=1)

        marks = {}

        def newton_rsqrt(zsrc_fn, ncols, zdeps):
            dst = rstd[:, 0:ncols] if ncols == 2 else rstd1[:, 0:1]
            zt = P.add("V", zsrc_fn, deps=zdeps, inc=1)

            def _prep(e):
                e.tensor_scalar(out=zi32[:, 0:ncols],
                                in0=zf32[:, 0:ncols].bitcast(I32), scalar1=16,
                                scalar2=None, op0=ALU.logical_shift_right)
                return e.tensor_copy(out=zcv[:, 0:ncols], in_=zi32[:, 0:ncols])

            cv = P.add("V", _prep, deps=[zt], inc=1)
            sd = P.add("S", (lambda e: e.activation(
                out=zsd[:, 0:ncols], in_=zcv[:, 0:ncols], func=AF.Exp,
                scale=SEED_SCALE, bias=c_sdb[:])), deps=[cv, sdb_tok], inc=1)

            def _newton(e):
                y = dst
                e.tensor_copy(out=y, in_=zsd[:, 0:ncols])
                for _ in range(NEWTON_ITERS):
                    e.tensor_tensor(out=znt[:, 0:ncols], in0=y, in1=y, op=ALU.mult)
                    e.tensor_tensor(out=znt[:, 0:ncols], in0=znt[:, 0:ncols],
                                    in1=zf32[:, 0:ncols], op=ALU.mult)
                    e.tensor_scalar(out=znt[:, 0:ncols], in0=znt[:, 0:ncols],
                                    scalar1=-0.5, scalar2=1.5, op0=ALU.mult,
                                    op1=ALU.add)
                    e.tensor_tensor(out=y, in0=y, in1=znt[:, 0:ncols], op=ALU.mult)
                return e.nop()

            return P.add("V", _newton, deps=[sd], inc=1)

        def gate_round(k, grot):
            proj = grot // NCH
            hc = grot % NCH
            rot = grot % 2
            pp = k % 2
            wn = ("wrgt", "wigt", "wint")[proj]
            deps = [xtdma[k], wdma[wn],
                    marks.get(("gev", k, grot - 2)),
                    hbrg_tok if proj == 0 else None,
                    hbig_tok if proj == 1 else None]
            if grot < 2:
                deps.append(marks.get(("gev", k - 1, 22 + grot)))

            def _gmm(e, wn=wn, hc=hc, rot=rot, k=k):
                for dc in range(NCH):
                    e.matmul(pg[rot][:, 0:SPAN],
                             w_sb[wn][:, dc * DM + hc * 128:
                                      dc * DM + hc * 128 + 128],
                             xt_bf[k % 2][:, dc * SPAN:(dc + 1) * SPAN],
                             start=(dc == 0), stop=(dc == NCH - 1))
                return e.drain()

            mmtok = P.add("T", _gmm, deps=deps, inc=1)
            marks[("gates", k)] = mmtok

            if proj == 0:
                bdeps = [mmtok, marks.get(("scan", k - 2)),
                         marks.get(("fmm", k - 2, 0)),
                         marks.get(("fmm", k - 2, 1))]

                def _ev(e, hc=hc, rot=rot, pp=pp):
                    return e.activation(
                        out=B_a[pp][:, hc * SPAN:(hc + 1) * SPAN],
                        in_=pg[rot][:, 0:SPAN], func=AF.Tanh, scale=0.5,
                        bias=sv["hbrg"][:, hc:hc + 1])
            elif proj == 1:
                bdeps = [mmtok, marks.get(("g1", k - 1))]

                def _ev(e, hc=hc, rot=rot):
                    return e.activation(
                        out=B_ti[:, hc * SPAN:(hc + 1) * SPAN],
                        in_=pg[rot][:, 0:SPAN], func=AF.Tanh, scale=0.5,
                        bias=sv["hbig"][:, hc:hc + 1])
            else:
                bdeps = [mmtok, marks.get(("scan", k - 1))]

                def _ev(e, hc=hc, rot=rot):
                    return e.activation(
                        out=B_xp[:, hc * SPAN:(hc + 1) * SPAN],
                        in_=pg[rot][:, 0:SPAN], func=AF.Copy)

            marks[("gev", k, grot)] = P.add("S", _ev, deps=bdeps, inc=1)

        def tail_and_prefetch(k):
            if k == 0:
                marks[("tail", 0)] = ut0z
            else:
                u3 = uT0[:].rearrange("p (hc c) -> p hc c", hc=NCH)
                marks[("tail", k)] = P.add(
                    "V", (lambda e, u3=u3: e.tensor_copy(
                        out=u3[:, :, 0:1], in_=u3[:, :, SPAN:SPAN + 1])),
                    deps=[marks.get(("uTev", k - 1, 0)),
                          marks.get(("uTev", k - 1, 1)),
                          marks.get(("mix", k - 1))],
                    inc=1)

        def gm_a(k):
            pp = k % 2
            ytoks = []
            for hc in range(NCH):
                ytoks.append(P.add("V", (lambda e, hc=hc, pp=pp: e.tensor_scalar(
                    out=B_y[:, hc * SPAN:(hc + 1) * SPAN],
                    in0=B_a[pp][:, hc * SPAN:(hc + 1) * SPAN],
                    scalar1=sv["hcl"][:, hc:hc + 1],
                    scalar2=sv["hcl"][:, hc:hc + 1],
                    op0=ALU.mult, op1=ALU.add)),
                    deps=[marks[("gev", k, hc)], hcl_tok], inc=1))
            g1 = P.add("V", (lambda e: e.scalar_tensor_tensor(
                out=B_xp[:], in0=B_ti[:], scalar=1.0, in1=B_xp[:],
                op0=ALU.add, op1=ALU.mult)),
                deps=[marks[("gev", k, 15)], marks[("gev", k, 23)]], inc=1)
            marks[("g1", k)] = g1
            atok = P.add("S", (lambda e, pp=pp: e.activation(
                out=B_a[pp][:], in_=B_y[:], func=AF.Exp)),
                deps=ytoks, inc=1)
            e2y = P.add("S", (lambda e: e.activation(
                out=B_o[:], in_=B_y[:], func=AF.Exp, scale=2.0)),
                deps=[marks.get(("mix", k - 1))], inc=1)
            sqt = P.add("S", (lambda e: e.activation(
                out=B_y[:], in_=B_o[:], func=AF.Sqrt, scale=-1.0,
                bias=c_sqb[:])), deps=[e2y, sqb_tok], inc=1)
            gt = P.add("V", (lambda e: e.scalar_tensor_tensor(
                out=B_xp[:], in0=B_xp[:], scalar=0.5, in1=B_y[:],
                op0=ALU.mult, op1=ALU.mult)), deps=[sqt, g1], inc=1)

            carry_src = sv["h0"] if k == 0 else hcar[(k - 1) % 2]
            scans = []
            for hc in range(NCH):
                scans.append(P.add(
                    "V", (lambda e, hc=hc, c=carry_src, pp=pp:
                          e.tensor_tensor_scan(
                        out=B_o[:, hc * SPAN:(hc + 1) * SPAN],
                        data0=B_a[pp][:, hc * SPAN:(hc + 1) * SPAN],
                        data1=B_xp[:, hc * SPAN:(hc + 1) * SPAN],
                        initial=c[:, hc:hc + 1], op0=ALU.mult, op1=ALU.add)),
                    deps=[atok, gt]
                    + ([vdma["h0"]] if k == 0 and hc == 0 else []),
                    inc=1))
            marks[("scan", k)] = scans[-1]
            o3 = B_o[:].rearrange("p (hc c) -> p hc c", hc=NCH)
            P.add("V", (lambda e, k=k: e.tensor_copy(
                out=hcar[k % 2][:].rearrange("p (h o) -> p h o", o=1),
                in_=o3[:, :, SPAN - 1:SPAN])), deps=[], inc=1)

        def gm_b(k):
            obf = P.add("V", (lambda e: e.tensor_copy(out=B_obf[:], in_=B_o[:])),
                        deps=[marks.get(("kaT", k - 1))], inc=1)
            marks[("obf", k)] = obf
            o2 = P.add("V", (lambda e: e.tensor_tensor(
                out=B_tr[:], in0=B_obf[:], in1=B_obf[:], op=ALU.mult)),
                deps=[marks.get(("ssum", k - 1)),
                      marks.get(("sq3red", k - 1, 1))], inc=1)
            marks[("o2", k)] = o2

        def ssum_phase(k):
            def _ssum(e):
                for tp in range(TPS):
                    for hc in range(NCH):
                        e.matmul(pk[0][:, 256 + tp * 128:256 + tp * 128 + 128],
                                 B_tr[:, hc * SPAN + tp * 128:
                                      hc * SPAN + tp * 128 + 128],
                                 t_ones[:], start=(hc == 0),
                                 stop=(hc == NCH - 1))
                    e.drain()
                return e.drain()

            ssum = P.add("T", _ssum,
                         deps=[marks[("o2", k)], vdma["ones"],
                               marks.get(("krelu", k - 1, 2)),
                               marks.get(("rmsz", k - 1))], inc=1)
            marks[("ssum", k)] = ssum
            rms_tok = newton_rsqrt(
                (lambda e: e.tensor_scalar(
                    out=zf32[:, 0:2],
                    in0=pk[0][:, 256:512].rearrange(
                        "p (t c) -> p t c", t=2)[:, :, 0:1],
                    scalar1=1.0 / DM,
                    scalar2=1e-6, op0=ALU.mult, op1=ALU.add)),
                2, [ssum])
            marks[("rmsz", k)] = rms_tok

        def wout_phase(k, tp):
            t = k * TPS + tp

            def _wout(e, tp=tp):
                for hc in range(NCH):
                    for nh in range(2):
                        e.matmul(pa[tp][:, nh * 512:(nh + 1) * 512],
                                 B_obf[:, hc * SPAN + tp * 128:
                                       hc * SPAN + tp * 128 + 128],
                                 w_sb["woutt"][:, hc * DM + nh * 512:
                                               hc * DM + nh * 512 + 512],
                                 start=(hc == 0), stop=(hc == NCH - 1))
                    if hc % 2 == 1:
                        e.drain()
                return e.drain()

            attn = P.add("T", _wout,
                         deps=[marks[("obf", k)],
                               marks.get(("f2", k - 1, tp))] +
                              (wout_scaled if k == 0 else []),
                         inc=1)
            marks[("attn", k, tp)] = attn

            wstt = P.add("V", (lambda e, tp=tp: e.scalar_tensor_tensor(
                out=B_w[tp][:], in0=pa[tp][:], scalar=rstd[:, tp:tp + 1],
                in1=xtm[tp][:], op0=ALU.mult, op1=ALU.add,
                accum_out=msum[:])),
                deps=[attn, marks[("rmsz", k)], xtmdma[t]], inc=1)
            marks[("wstt", t)] = wstt
            if debug and t == 30:
                P.add("V", (lambda e: e.tensor_copy(out=dbgw[:], in_=B_w[0][:])),
                      deps=[wstt], inc=1)
            if t + 2 < 2 * nspan:
                xtmdma[t + 2] = P.add("SY", (lambda e, t=t: e.dma_start(
                    out=xtm[t % 2][:],
                    in_=d_xtm[(t + 2) * 128:(t + 3) * 128, :])),
                    deps=[wstt], inc=16)
            P.add("V", (lambda e: e.tensor_scalar(
                out=mcol[:], in0=msum[:], scalar1=1.0 / DM, scalar2=None,
                op0=ALU.mult)), inc=1)
            xc = P.add("V", (lambda e, tp=tp: e.tensor_scalar(
                out=B_w[tp][:], in0=B_w[tp][:], scalar1=mcol[:],
                scalar2=None, op0=ALU.subtract)), inc=1)
            sqp = P.add("S", (lambda e, tp=tp: e.activation(
                out=B_tr[:, 0:DM], in_=B_w[tp][:], func=AF.Square)),
                deps=[xc, marks[("ssum", k)]],
                inc=1)
            sq2 = P.add("V", (lambda e: e.reduce_sum(
                out=msum[:], in_=B_tr[:, 0:DM], axis=mybir.AxisListType.X)),
                deps=[sqp], inc=1)
            marks[("sq2red", k, tp)] = sq2
            ln1 = newton_rsqrt(
                (lambda e: e.tensor_scalar(
                    out=zf32[:, 0:1], in0=msum[:], scalar1=1.0 / DM,
                    scalar2=1e-5, op0=ALU.mult, op1=ALU.add)),
                1, [sq2])
            u = P.add("S", (lambda e, tp=tp: e.activation(
                out=B_u[tp][:], in_=B_w[tp][:], func=AF.Copy,
                scale=rstd1[:, 0:1])),
                deps=[ln1, marks.get(("store", t - 2))], inc=1)
            marks[("u", k, tp)] = u
            if debug and t == 30:
                P.add("V", (lambda e: e.tensor_copy(out=dbgu[:], in_=B_u[0][:])),
                      deps=[], inc=1)
                P.add("V", (lambda e: e.tensor_copy(out=dbgr[:], in_=rstd1[:])),
                      deps=[], inc=1)

        def tr_phase(k, tp):
            def _transp(e, tp=tp):
                for hc in range(NCH):
                    e.transpose(pa[tp][:, hc * 128:(hc + 1) * 128],
                                B_u[tp][:, hc * 128:(hc + 1) * 128],
                                t_ident[:])
                return e.drain()

            tr = P.add("T", _transp,
                       deps=[marks[("u", k, tp)], marks[("wstt", k * TPS + tp)],
                             vdma["ident"]], inc=1)
            u3 = uT0[:].rearrange("p (hc c) -> p hc c", hc=NCH)
            pa3 = pa[tp][:].rearrange("p (hc c) -> p hc c", hc=NCH)
            uTev = P.add("S", (lambda e, u3=u3, pa3=pa3, tp=tp: e.activation(
                out=u3[:, :, 1 + tp * 128:1 + tp * 128 + 128],
                in_=pa3[:, :, 0:128], func=AF.Copy)),
                deps=[tr, marks[("tail", k)], marks.get(("mix", k - 1))], inc=1)
            marks[("uTev", k, tp)] = uTev

        def mix_phase(k):
            u3 = uT0[:].rearrange("p (hc c) -> p hc c", hc=NCH)
            d1 = B_o[:].bitcast(BF16)[:, 0:NCH * SPAN]
            d13 = d1.rearrange("p (hc c) -> p hc c", hc=NCH)
            xr3 = B_xrT[:].rearrange("p (hc c) -> p hc c", hc=NCH)
            xk3 = B_xkT[:].rearrange("p (hc c) -> p hc c", hc=NCH)
            mixdeps = [marks[("uTev", k, 0)], marks[("uTev", k, 1)],
                       marks[("tail", k)], vdma["mur"], vdma["muk"],
                       marks.get(("rmm", k - 1, 0)), marks.get(("rmm", k - 1, 1)),
                       marks.get(("kT", k - 1, 3)),
                       marks[("sq2red", k, 1)]]

            def _mix(e, u3=u3, d13=d13, xr3=xr3, xk3=xk3):
                for hc in range(NCH):
                    e.tensor_tensor(out=d13[:, hc, :],
                                    in0=u3[:, hc, 1:SPAN + 1],
                                    in1=u3[:, hc, 0:SPAN], op=ALU.subtract)
                for hc in range(NCH):
                    e.scalar_tensor_tensor(out=xr3[:, hc, :], in0=d13[:, hc, :],
                                           scalar=sv["mur"][:, hc:hc + 1],
                                           in1=u3[:, hc, 0:SPAN],
                                           op0=ALU.mult, op1=ALU.add)
                for hc in range(NCH):
                    e.scalar_tensor_tensor(out=xk3[:, hc, :], in0=d13[:, hc, :],
                                           scalar=sv["muk"][:, hc:hc + 1],
                                           in1=u3[:, hc, 0:SPAN],
                                           op0=ALU.mult, op1=ALU.add)
                return e.nop()

            marks[("mix", k)] = P.add("V", _mix, deps=mixdeps, inc=1)

        def kt_phase(k, rc):
            def _kt(e, rc=rc):
                for oi in range(2):
                    oc = 2 * rc + oi
                    for cc in range(NCH):
                        e.matmul(pk[rc % 2][:, oi * SPAN:(oi + 1) * SPAN],
                                 w_sb["wkt"][:, cc * DM + oc * 128:
                                             cc * DM + oc * 128 + 128],
                                 B_xkT[:, cc * SPAN:(cc + 1) * SPAN],
                                 start=(cc == 0), stop=(cc == NCH - 1))
                    e.drain()
                return e.drain()

            ktdeps = [marks[("mix", k)], wdma["wkt"]]
            if rc >= 2:
                ktdeps.append(marks.get(("krelu", k, rc - 2)))
            else:
                ktdeps.append(marks.get(("krelu", k - 1, rc + 2)))
            if rc == 0:
                ktdeps.append(marks[("rmsz", k)])
            kt = P.add("T", _kt, deps=ktdeps, inc=1)
            marks[("kT", k, rc)] = kt
            krelu = P.add("S", (lambda e, rc=rc: e.activation(
                out=B_obf[:, rc * 512:(rc + 1) * 512],
                in_=pk[rc % 2][:, 0:512], func=AF.Relu)),
                deps=[kt, marks[("attn", k, 0)], marks[("attn", k, 1)],
                      marks[("g1", k)]], inc=1)
            marks[("krelu", k, rc)] = krelu

        def kat_phase(k):
            pp = k % 2
            katok = P.add("V", (lambda e, pp=pp: e.tensor_tensor(
                out=kaT[pp], in0=B_obf[:], in1=B_obf[:], op=ALU.mult)),
                deps=[marks[("krelu", k, rc_)] for rc_ in range(4)]
                + [marks[("scan", k)]], inc=1)
            marks[("kaT", k)] = katok

        def rmm_phase(k, tp):
            def _rmm(e, tp=tp):
                for cc in range(NCH):
                    for nh in range(2):
                        e.matmul(pa[tp][:, nh * 512:(nh + 1) * 512],
                                 B_xrT[:, cc * SPAN + tp * 128:
                                       cc * SPAN + tp * 128 + 128],
                                 w_sb["wrt"][:, cc * DM + nh * 512:
                                             cc * DM + nh * 512 + 512],
                                 start=(cc == 0), stop=(cc == NCH - 1))
                    if cc % 2 == 1:
                        e.drain()
                return e.drain()

            rmm = P.add("T", _rmm,
                        deps=[marks[("mix", k)], wdma["wrt"],
                              marks[("uTev", k, tp)]], inc=1)
            marks[("rmm", k, tp)] = rmm
            rev = P.add("S", (lambda e, tp=tp: e.activation(
                out=B_R[tp][:], in_=pa[tp][:], func=AF.Tanh, scale=0.5)),
                deps=[rmm, marks.get(("f2", k - 1, tp))], inc=1)
            marks[("rev", k, tp)] = rev

        def fmm_phase(k, tp):
            t = k * TPS + tp
            pp = k % 2

            def _fmm(e, tp=tp, pp=pp):
                for cc in range(NCH):
                    for nh in range(2):
                        e.matmul(pa[tp][:, nh * 512:(nh + 1) * 512],
                                 kaT[pp][:, cc * SPAN + tp * 128:
                                         cc * SPAN + tp * 128 + 128],
                                 w_sb["wvt"][:, cc * DM + nh * 512:
                                             cc * DM + nh * 512 + 512],
                                 start=(cc == 0), stop=(cc == NCH - 1))
                    if cc % 2 == 1:
                        e.drain()
                return e.drain()

            fmm = P.add("T", _fmm, deps=[marks[("kaT", k)],
                                         marks[("rev", k, tp)], wdma["wvt"]],
                        inc=1)
            marks[("fmm", k, tp)] = fmm

            f2 = P.add("V", (lambda e, tp=tp: e.scalar_tensor_tensor(
                out=B_w[tp][:], in0=B_R[tp][:], scalar=1.0, in1=pa[tp][:],
                op0=ALU.add, op1=ALU.mult)), deps=[fmm], inc=1)
            marks[("f2", k, tp)] = f2
            P.add("V", (lambda e, tp=tp: e.scalar_tensor_tensor(
                out=B_w[tp][:], in0=B_w[tp][:], scalar=0.5, in1=B_u[tp][:],
                op0=ALU.mult, op1=ALU.add, accum_out=msum[:])), inc=1)
            P.add("V", (lambda e: e.tensor_scalar(
                out=mcol[:], in0=msum[:], scalar1=1.0 / DM, scalar2=None,
                op0=ALU.mult)), inc=1)
            xc2 = P.add("V", (lambda e, tp=tp: e.tensor_scalar(
                out=B_w[tp][:], in0=B_w[tp][:], scalar1=mcol[:],
                scalar2=None, op0=ALU.subtract)), inc=1)
            sqp3 = P.add("S", (lambda e, tp=tp: e.activation(
                out=B_tr[:, 0:DM], in_=B_w[tp][:], func=AF.Square)),
                deps=[xc2, f2], inc=1)
            sq3 = P.add("V", (lambda e: e.reduce_sum(
                out=msum[:], in_=B_tr[:, 0:DM], axis=mybir.AxisListType.X)),
                deps=[sqp3], inc=1)
            marks[("sq3red", k, tp)] = sq3
            ln2 = newton_rsqrt(
                (lambda e: e.tensor_scalar(
                    out=zf32[:, 0:1], in0=msum[:], scalar1=1.0 / DM,
                    scalar2=1e-5, op0=ALU.mult, op1=ALU.add)),
                1, [sq3])
            outts = P.add("S", (lambda e, tp=tp: e.activation(
                out=B_u[tp][:], in_=B_w[tp][:], func=AF.Copy,
                scale=rstd1[:, 0:1])), deps=[ln2], inc=1)
            marks[("store", t)] = P.add("SY", (lambda e, t=t, tp=tp: e.dma_start(
                out=d_out[t * 128:(t + 1) * 128, :], in_=B_u[tp][:])),
                deps=[outts], inc=16)

        for g in range(3 * NCH):
            gate_round(0, g)
        tail_and_prefetch(0)
        gm_a(0)
        gm_b(0)
        for k in range(nspan):
            ssum_phase(k)

            gq = (list(range(3 * NCH)) if k + 1 < nspan else [])

            def take(n, k=k, gq=gq):
                for _ in range(n):
                    if gq:
                        gate_round(k + 1, gq.pop(0))

            wout_phase(k, 0)
            take(6)
            tr_phase(k, 0)
            take(6)
            wout_phase(k, 1)
            take(6)
            tr_phase(k, 1)
            take(6)
            mix_phase(k)
            while gq:
                gate_round(k + 1, gq.pop(0))
            if k + 1 < nspan:
                tail_and_prefetch(k + 1)
                gm_a(k + 1)
            kt_phase(k, 0)
            kt_phase(k, 1)
            rmm_phase(k, 0)
            kt_phase(k, 2)
            kt_phase(k, 3)
            kat_phase(k)
            rmm_phase(k, 1)
            fmm_phase(k, 0)
            fmm_phase(k, 1)
            if k + 1 < nspan:
                gm_b(k + 1)
            if k + 2 < nspan:
                xtdma[k + 2] = P.add("G", (lambda e, k=k: e.dma_start(
                    out=xt_bf[k % 2][:], in_=d_xt[k + 2, :, :])),
                    deps=[marks[("gates", k + 1)]], inc=16)

        if debug:
            dbgmap = {"By": B_y, "Bxp": B_xp, "Ba0": B_a[0], "Ba1": B_a[1],
                      "Bo": B_o, "Bobf": B_obf, "Bti": B_ti, "Btr": B_tr,
                      "rstd": rstd, "rstd1": rstd1, "hcl": sv["hcl"],
                      "q": sv["q"], "zf32": zf32, "zcv": zcv, "zsd": zsd,
                      "hcar0": hcar[0], "uT0": uT0, "msum": msum,
                      "dw": dbgw, "du": dbgu, "dr": dbgr,
                      "Bw0": B_w[0], "Bu0": B_u[0]}
            enddeps = [("t", P.cnt["t"]), ("v", P.cnt["v"]), ("s", P.cnt["s"]),
                       ("g", P.cnt["g"])]
            for nm, buf in dbgmap.items():
                th = nc.dram_tensor(f"dbg_{nm}", list(np.asarray(buf[:].shape)),
                                    buf[:].dtype, kind="ExternalOutput").ap()
                P.add("SY", (lambda e, th=th, buf=buf: e.dma_start(
                    out=th, in_=buf[:])), deps=enddeps, inc=16)

        final_val = P.cnt["sy"]
        P.add("SY", (lambda e: e.nop()), deps=[("sy", final_val)], inc=0)

        P.finalize()

        @blk.tensor
        def _(e):
            P.emit("T", e, semh)

        @blk.vector
        def _(e):
            P.emit("V", e, semh)

        @blk.scalar
        def _(e):
            P.emit("S", e, semh)

        @blk.gpsimd
        def _(e):
            P.emit("G", e, semh)

        @blk.sync
        def _(e):
            P.emit("SY", e, semh)

    return nc


_NC_CACHE = None


def _get_nc():
    global _NC_CACHE
    if _NC_CACHE is None:
        _NC_CACHE = build_nc()
    return _NC_CACHE


def _fm(v):
    return np.ascontiguousarray(np.asarray(v, np.float32).reshape(NCH, 128).T)


def kernel(**inputs):
    x = np.asarray(inputs["L0_attn_x"], np.float32)
    shared = {
        "wint": np.ascontiguousarray(np.asarray(inputs["L0_attn_W_in"], np.float32).T),
        "wigt": np.ascontiguousarray(np.asarray(inputs["L0_attn_W_ig"], np.float32).T),
        "wrgt": np.ascontiguousarray(np.asarray(inputs["L0_attn_W_rg"], np.float32).T),
        "woutt": np.ascontiguousarray(np.asarray(inputs["L0_attn_W_out"], np.float32).T),
        "wrt": np.ascontiguousarray(np.asarray(inputs["L0_ffn_W_r"], np.float32).T),
        "wkt": np.ascontiguousarray(np.asarray(inputs["L0_ffn_W_k"], np.float32).T),
        "wvt": np.ascontiguousarray(np.asarray(inputs["L0_ffn_W_v"], np.float32).T),
        "lam": _fm(inputs["L0_attn_lam"]),
        "big": _fm(inputs["L0_attn_b_ig"]),
        "brg": _fm(inputs["L0_attn_b_rg"]),
        "mur": _fm(inputs["L0_ffn_mu_r"]),
        "muk": _fm(inputs["L0_ffn_mu_k"]),
        "h0": _fm(np.asarray(inputs["L0_attn_h0"], np.float32).reshape(DM)),
        "nw": _fm(inputs["L0_attn_norm_w"]),
    }
    in_maps = []
    for b in range(NCORE):
        xb = x[b]
        xt = np.ascontiguousarray(
            xb.reshape(NSPAN, SPAN, NCH, 128).transpose(0, 3, 2, 1)
        ).reshape(NSPAN, 128, NCH * SPAN)
        m = dict(shared)
        m["xtm"] = np.ascontiguousarray(xb)
        m["xt"] = xt
        in_maps.append(m)

    nc = _get_nc()
    res = run_bass_kernel_spmd(nc, in_maps, core_ids=list(range(NCORE)))
    out = np.stack([np.asarray(r["out"], np.float32) for r in res.results])
    return out


# version 22
# speedup vs baseline: 1.0485x; 1.0485x over previous
import math
from contextlib import ExitStack

import numpy as np
import ml_dtypes

import concourse.bass as bass
import concourse.mybir as mybir
from concourse.bass_utils import run_bass_kernel_spmd

F32 = mybir.dt.float32
BF16 = mybir.dt.bfloat16
I32 = mybir.dt.int32
AF = mybir.ActivationFunctionType
ALU = mybir.AluOpType

NCORE = 8
S_TOK = 4096
DM = 1024
NCH = 8
SPAN = 256
NSPAN = S_TOK // SPAN
TPS = SPAN // 128
NTILE = S_TOK // 128

LN2 = math.log(2.0)
SEED_SCALE = -LN2 / 256.0
SEED_BIAS = 127.0 * LN2 / 2.0
NEWTON_ITERS = 2

ENG_SEM = {"T": "t", "V": "v", "S": "s", "SY": "sy", "G": "g"}


class _AutoDrain:

    _WRAP = frozenset({"tensor_scalar", "scalar_tensor_tensor",
                       "tensor_tensor_scan", "tensor_tensor",
                       "tensor_tensor_reduce", "reduce_sum", "tensor_reduce",
                       "tensor_copy"})

    def __init__(self, e, junk):
        self._e = e
        self._junk = junk

    def __getattr__(self, n):
        f = getattr(self._e, n)
        if n in self._WRAP:
            def g(*a, **k):
                f(*a, **k)
                return self._e.drain()
            return g
        return f


class Prog:

    def __init__(self):
        self.ops = []
        self.cnt = {s: 0 for s in ("t", "v", "s", "sy", "g")}
        self.batch_end = {"sy": {}, "g": {}}
        self.junk = None

    def add(self, eng, fn, deps=(), inc=0):
        sem = ENG_SEM[eng]
        tok = None
        if inc:
            self.cnt[sem] += inc
            tok = (sem, self.cnt[sem])
        self.ops.append([eng, fn, tuple(d for d in deps if d is not None), inc, sem,
                         self.cnt[sem]])
        return tok

    def finalize(self):
        cur = {"sy": [], "g": []}

        def close(q):
            if cur[q]:
                endv = cur[q][-1]
                for v in cur[q]:
                    self.batch_end[q][v] = endv
                cur[q] = []

        for eng, fn, deps, inc, sem, val in self.ops:
            if eng not in ("SY", "G"):
                continue
            q = ENG_SEM[eng]
            if deps:
                close(q)
            if inc:
                cur[q].append(val)
        close("sy")
        close("g")

    def emit(self, eng_name, e, semh):
        own = ENG_SEM[eng_name]
        skip_own = own in ("t", "v", "s")
        is_q = own in ("sy", "g")
        wait_e = e
        if eng_name == "V":
            e = _AutoDrain(e, self.junk)
        waited = {}
        for name, fn, deps, inc, sem, val in self.ops:
            if name != eng_name:
                continue
            emitted = False
            rdeps = []
            for (s, v) in deps:
                if s in self.batch_end:
                    v = self.batch_end[s].get(v, v)
                if skip_own and s == own:
                    continue
                rdeps.append((s, v))
            for (s, v) in rdeps:
                if waited.get(s, -1) >= v:
                    continue
                wait_e.wait_ge(semh[s], v)
                waited[s] = v
                emitted = True
            if is_q and inc and deps and not emitted and rdeps:
                s, v = rdeps[0]
                wait_e.wait_ge(semh[s], v)
            inst = fn(e)
            if inc:
                inst.then_inc(semh[sem], inc)


def build_nc(nspan=NSPAN, debug=False):
    nc = bass.Bass("TRN2", target_bir_lowering=False,
                   detect_race_conditions=False)

    d_xtm = nc.dram_tensor("xtm", [S_TOK, DM], F32, kind="ExternalInput").ap()
    d_xt = nc.dram_tensor("xt", [NSPAN, 128, NCH * SPAN], F32,
                          kind="ExternalInput").ap()
    dw = {}
    for wname in ("wint", "wigt", "wrgt", "woutt", "wrt", "wkt", "wvt"):
        dw[wname] = nc.dram_tensor(wname, [DM, DM], F32, kind="ExternalInput").ap()
    dvec = {}
    for vname in ("lam", "big", "brg", "mur", "muk", "h0", "nw"):
        dvec[vname] = nc.dram_tensor(vname, [128, NCH], F32,
                                     kind="ExternalInput").ap()
    d_out = nc.dram_tensor("out", [S_TOK, DM], F32, kind="ExternalOutput").ap()

    ident_h = nc.inline_tensor(np.eye(128, dtype=np.float32), "identf32")
    ones_h = nc.inline_tensor(np.ones((128, 128), dtype=ml_dtypes.bfloat16),
                              "onesbf")

    P = Prog()

    with ExitStack() as ctx:
        E = ctx.enter_context
        w_sb = {n: E(nc.sbuf_tensor(f"sb_{n}", [128, NCH * DM], BF16))
                for n in ("wint", "wigt", "wrgt", "woutt", "wrt", "wkt", "wvt")}
        xt_bf = [E(nc.sbuf_tensor(f"xtbf{i}", [128, NCH * SPAN], BF16))
                 for i in range(2)]
        xtm = [E(nc.sbuf_tensor(f"xtm{i}", [128, DM], F32)) for i in range(2)]
        B_tr = E(nc.sbuf_tensor("Btr", [128, NCH * SPAN], BF16))
        B_ti = E(nc.sbuf_tensor("Bti", [128, NCH * SPAN], BF16))
        B_xp = E(nc.sbuf_tensor("Bxp", [128, NCH * SPAN], BF16))
        B_y = E(nc.sbuf_tensor("By", [128, NCH * SPAN], BF16))
        B_a = [E(nc.sbuf_tensor(f"Ba{i}", [128, NCH * SPAN], F32))
               for i in range(2)]
        B_o = E(nc.sbuf_tensor("Bo", [128, NCH * SPAN], F32))
        B_obf = E(nc.sbuf_tensor("Bobf", [128, NCH * SPAN], BF16))
        uT0 = E(nc.sbuf_tensor("uT0", [128, NCH * (SPAN + 1)], BF16))
        B_xrT = E(nc.sbuf_tensor("BxrT", [128, NCH * SPAN], BF16))
        B_xkT = E(nc.sbuf_tensor("BxkT", [128, NCH * SPAN], BF16))
        B_w = [E(nc.sbuf_tensor(f"Bw{i}", [128, DM], F32)) for i in range(2)]
        B_u = [E(nc.sbuf_tensor(f"Bu{i}", [128, DM], F32)) for i in range(2)]
        B_R = [E(nc.sbuf_tensor(f"BR{i}", [128, DM], BF16)) for i in range(2)]
        sv = {n: E(nc.sbuf_tensor(f"v_{n}", [128, NCH], F32))
              for n in ("lam", "big", "brg", "mur", "muk", "h0", "nw",
                        "q", "hcl", "hbig", "hbrg", "ts1", "ts2")}
        hcar = [E(nc.sbuf_tensor(f"hcar{i}", [128, NCH], F32)) for i in range(2)]
        t_ident = E(nc.sbuf_tensor("identsb", [128, 128], F32))
        t_ones = E(nc.sbuf_tensor("onessb", [128, 128], BF16))
        zf32 = E(nc.sbuf_tensor("zf32", [128, 2], F32))
        zi32 = E(nc.sbuf_tensor("zi32", [128, 2], I32))
        zcv = E(nc.sbuf_tensor("zcv", [128, 2], F32))
        zsd = E(nc.sbuf_tensor("zsd", [128, 2], F32))
        znt = E(nc.sbuf_tensor("znt", [128, 2], F32))
        rstd = E(nc.sbuf_tensor("rstd", [128, 2], F32))
        rstd1 = E(nc.sbuf_tensor("rstd1", [128, 1], F32))
        msum = E(nc.sbuf_tensor("msum", [128, 1], F32))
        mcol = E(nc.sbuf_tensor("mcol", [128, 1], F32))
        junk = E(nc.sbuf_tensor("junk", [128, 2], F32))
        P.junk = junk
        c_sqb = E(nc.sbuf_tensor("c_sqb", [128, 1], F32))
        c_sdb = E(nc.sbuf_tensor("c_sdb", [128, 1], F32))
        if debug:
            dbgw = E(nc.sbuf_tensor("dbgw", [128, DM], BF16))
            dbgu = E(nc.sbuf_tensor("dbgu", [128, DM], BF16))
            dbgr = E(nc.sbuf_tensor("dbgr", [128, 1], F32))

        kaT = [B_a[i][:].bitcast(BF16)[:, 0:NCH * SPAN] for i in range(2)]

        pg = [E(nc.psum_tensor(f"pg{i}", [128, 512], F32)) for i in range(2)]
        pa = [E(nc.psum_tensor(f"pa{i}", [128, 1024], F32)) for i in range(2)]
        pk = [E(nc.psum_tensor(f"pk{i}", [128, 512], F32)) for i in range(2)]

        semh = {s: E(nc.semaphore(s)) for s in ("t", "v", "s", "sy", "g")}
        blk = E(nc.Block())

        wdma = {}
        for n in ("wrgt", "wigt", "wint"):
            s3 = dw[n].rearrange("(dc p) h -> p dc h", p=128)
            d3 = w_sb[n][:].rearrange("p (dc h) -> p dc h", dc=NCH)
            wdma[n] = P.add("G", (lambda e, s=s3, d=d3: e.dma_start(out=d, in_=s)),
                            inc=16)
        xtdma = {}
        for k in range(2):
            xtdma[k] = P.add(
                "G", (lambda e, k=k: e.dma_start(out=xt_bf[k][:],
                                                 in_=d_xt[k, :, :])),
                inc=16)
        for n in ("woutt", "wrt", "wkt", "wvt"):
            s3 = dw[n].rearrange("(dc p) h -> p dc h", p=128)
            d3 = w_sb[n][:].rearrange("p (dc h) -> p dc h", dc=NCH)
            wdma[n] = P.add("G", (lambda e, s=s3, d=d3: e.dma_start(out=d, in_=s)),
                            deps=[("g", P.cnt["g"])], inc=16)

        vdma = {}
        for n in ("lam", "big", "brg", "mur", "muk", "h0", "nw"):
            vdma[n] = P.add(
                "SY", (lambda e, n=n: e.dma_start(out=sv[n][:], in_=dvec[n])),
                inc=16)
        vdma["ident"] = P.add(
            "SY", (lambda e: e.dma_start(out=t_ident[:], in_=ident_h.ap())), inc=16)
        vdma["ones"] = P.add(
            "SY", (lambda e: e.dma_start(out=t_ones[:], in_=ones_h.ap())), inc=16)
        xtmdma = {}
        for t in range(2):
            xtmdma[t] = P.add(
                "SY", (lambda e, t=t: e.dma_start(
                    out=xtm[t % 2][:], in_=d_xtm[t * 128:(t + 1) * 128, :])),
                inc=16)

        wout_scaled = []
        for hc in range(NCH):
            tok = P.add("V", (lambda e, hc=hc: e.tensor_scalar(
                out=w_sb["woutt"][:, hc * DM:(hc + 1) * DM],
                in0=w_sb["woutt"][:, hc * DM:(hc + 1) * DM],
                scalar1=sv["nw"][:, hc:hc + 1], scalar2=None, op0=ALU.mult)),
                deps=[wdma["woutt"], vdma["nw"]], inc=1)
            wout_scaled.append(tok)

        qtok = P.add("S", (lambda e: e.activation(
            out=sv["q"][:], in_=sv["lam"][:], func=AF.Exp, scale=-1.0)),
            deps=[vdma["lam"]], inc=1)
        q, t1, t2 = sv["q"], sv["ts1"], sv["ts2"]

        def _tay(e):
            e.tensor_scalar(out=t1[:], in0=q[:], scalar1=-0.25, scalar2=1.0 / 3.0,
                            op0=ALU.mult, op1=ALU.add)
            e.tensor_tensor(out=t2[:], in0=t1[:], in1=q[:], op=ALU.mult)
            e.tensor_scalar(out=t1[:], in0=t2[:], scalar1=-1.0, scalar2=0.5,
                            op0=ALU.mult, op1=ALU.add)
            e.tensor_tensor(out=t2[:], in0=t1[:], in1=q[:], op=ALU.mult)
            e.tensor_scalar(out=t1[:], in0=t2[:], scalar1=-1.0, scalar2=1.0,
                            op0=ALU.mult, op1=ALU.add)
            e.tensor_tensor(out=t2[:], in0=t1[:], in1=q[:], op=ALU.mult)
            e.tensor_scalar(out=t1[:], in0=t2[:], scalar1=-1.0, scalar2=1e-8,
                            op0=ALU.mult, op1=ALU.add)
            e.scalar_tensor_tensor(out=t2[:], in0=q[:], scalar=1e-8, in1=t1[:],
                                   op0=ALU.mult, op1=ALU.add)
            return e.tensor_scalar(out=sv["hcl"][:], in0=t2[:], scalar1=4.0,
                                   scalar2=None, op0=ALU.mult)

        hcl_tok = P.add("V", _tay, deps=[qtok], inc=1)
        hbig_tok = P.add("V", (lambda e: e.tensor_scalar(
            out=sv["hbig"][:], in0=sv["big"][:], scalar1=0.5, scalar2=None,
            op0=ALU.mult)), deps=[vdma["big"]], inc=1)
        hbrg_tok = P.add("V", (lambda e: e.tensor_scalar(
            out=sv["hbrg"][:], in0=sv["brg"][:], scalar1=0.5, scalar2=None,
            op0=ALU.mult)), deps=[vdma["brg"]], inc=1)
        sqb_tok = P.add("V", (lambda e: e.memset(c_sqb[:], 1.000001)), inc=1)
        sdb_tok = P.add("V", (lambda e: e.memset(c_sdb[:], SEED_BIAS)), inc=1)
        ut0z = P.add("V", (lambda e: e.memset(
            uT0[:].rearrange("p (hc c) -> p hc c", hc=NCH)[:, :, 0:1], 0.0)),
            inc=1)

        marks = {}

        def newton_rsqrt(zsrc_fn, ncols, zdeps):
            dst = rstd[:, 0:ncols] if ncols == 2 else rstd1[:, 0:1]
            zt = P.add("V", zsrc_fn, deps=zdeps, inc=1)

            def _prep(e):
                e.tensor_scalar(out=zi32[:, 0:ncols],
                                in0=zf32[:, 0:ncols].bitcast(I32), scalar1=16,
                                scalar2=None, op0=ALU.logical_shift_right)
                return e.tensor_copy(out=zcv[:, 0:ncols], in_=zi32[:, 0:ncols])

            cv = P.add("V", _prep, deps=[zt], inc=1)
            sd = P.add("S", (lambda e: e.activation(
                out=zsd[:, 0:ncols], in_=zcv[:, 0:ncols], func=AF.Exp,
                scale=SEED_SCALE, bias=c_sdb[:])), deps=[cv, sdb_tok], inc=1)

            def _newton(e):
                y = dst
                e.tensor_copy(out=y, in_=zsd[:, 0:ncols])
                for _ in range(NEWTON_ITERS):
                    e.tensor_tensor(out=znt[:, 0:ncols], in0=y, in1=y, op=ALU.mult)
                    e.tensor_tensor(out=znt[:, 0:ncols], in0=znt[:, 0:ncols],
                                    in1=zf32[:, 0:ncols], op=ALU.mult)
                    e.tensor_scalar(out=znt[:, 0:ncols], in0=znt[:, 0:ncols],
                                    scalar1=-0.5, scalar2=1.5, op0=ALU.mult,
                                    op1=ALU.add)
                    e.tensor_tensor(out=y, in0=y, in1=znt[:, 0:ncols], op=ALU.mult)
                return e.nop()

            return P.add("V", _newton, deps=[sd], inc=1)

        def gate_round(k, grot):
            proj = grot // NCH
            hc = grot % NCH
            rot = grot % 2
            pp = k % 2
            wn = ("wrgt", "wigt", "wint")[proj]
            deps = [xtdma[k], wdma[wn],
                    marks.get(("gev", k, grot - 2)),
                    hbrg_tok if proj == 0 else None,
                    hbig_tok if proj == 1 else None]
            if grot < 2:
                deps.append(marks.get(("gev", k - 1, 22 + grot)))

            def _gmm(e, wn=wn, hc=hc, rot=rot, k=k):
                for dc in range(NCH):
                    e.matmul(pg[rot][:, 0:SPAN],
                             w_sb[wn][:, dc * DM + hc * 128:
                                      dc * DM + hc * 128 + 128],
                             xt_bf[k % 2][:, dc * SPAN:(dc + 1) * SPAN],
                             start=(dc == 0), stop=(dc == NCH - 1))
                return e.drain()

            mmtok = P.add("T", _gmm, deps=deps, inc=1)
            marks[("gates", k)] = mmtok

            if proj == 0:
                bdeps = [mmtok, marks.get(("scan", k - 2)),
                         marks.get(("fmm", k - 2, 0)),
                         marks.get(("fmm", k - 2, 1))]

                def _ev(e, hc=hc, rot=rot, pp=pp):
                    return e.activation(
                        out=B_a[pp][:, hc * SPAN:(hc + 1) * SPAN],
                        in_=pg[rot][:, 0:SPAN], func=AF.Tanh, scale=0.5,
                        bias=sv["hbrg"][:, hc:hc + 1])
            elif proj == 1:
                bdeps = [mmtok, marks.get(("g1", k - 1))]

                def _ev(e, hc=hc, rot=rot):
                    return e.activation(
                        out=B_ti[:, hc * SPAN:(hc + 1) * SPAN],
                        in_=pg[rot][:, 0:SPAN], func=AF.Tanh, scale=0.5,
                        bias=sv["hbig"][:, hc:hc + 1])
            else:
                bdeps = [mmtok, marks.get(("scan", k - 1))]

                def _ev(e, hc=hc, rot=rot):
                    return e.activation(
                        out=B_xp[:, hc * SPAN:(hc + 1) * SPAN],
                        in_=pg[rot][:, 0:SPAN], func=AF.Copy)

            marks[("gev", k, grot)] = P.add("S", _ev, deps=bdeps, inc=1)

        def tail_and_prefetch(k):
            if k == 0:
                marks[("tail", 0)] = ut0z
            else:
                u3 = uT0[:].rearrange("p (hc c) -> p hc c", hc=NCH)
                marks[("tail", k)] = P.add(
                    "V", (lambda e, u3=u3: e.tensor_copy(
                        out=u3[:, :, 0:1], in_=u3[:, :, SPAN:SPAN + 1])),
                    deps=[marks.get(("uTev", k - 1, 0)),
                          marks.get(("uTev", k - 1, 1)),
                          marks.get(("mix", k - 1))],
                    inc=1)

        def gm_a(k):
            pp = k % 2
            ytoks = []
            for hc in range(NCH):
                ytoks.append(P.add("V", (lambda e, hc=hc, pp=pp: e.tensor_scalar(
                    out=B_y[:, hc * SPAN:(hc + 1) * SPAN],
                    in0=B_a[pp][:, hc * SPAN:(hc + 1) * SPAN],
                    scalar1=sv["hcl"][:, hc:hc + 1],
                    scalar2=sv["hcl"][:, hc:hc + 1],
                    op0=ALU.mult, op1=ALU.add)),
                    deps=[marks[("gev", k, hc)], hcl_tok], inc=1))
            g1 = P.add("V", (lambda e: e.scalar_tensor_tensor(
                out=B_xp[:], in0=B_ti[:], scalar=1.0, in1=B_xp[:],
                op0=ALU.add, op1=ALU.mult)),
                deps=[marks[("gev", k, 15)], marks[("gev", k, 23)]], inc=1)
            marks[("g1", k)] = g1
            atok = P.add("S", (lambda e, pp=pp: e.activation(
                out=B_a[pp][:], in_=B_y[:], func=AF.Exp)),
                deps=ytoks, inc=1)
            e2y = P.add("S", (lambda e: e.activation(
                out=B_o[:], in_=B_y[:], func=AF.Exp, scale=2.0)),
                deps=[marks.get(("mix", k - 1))], inc=1)
            sqt = P.add("S", (lambda e: e.activation(
                out=B_y[:], in_=B_o[:], func=AF.Sqrt, scale=-1.0,
                bias=c_sqb[:])), deps=[e2y, sqb_tok], inc=1)
            gt = P.add("V", (lambda e: e.scalar_tensor_tensor(
                out=B_xp[:], in0=B_xp[:], scalar=0.5, in1=B_y[:],
                op0=ALU.mult, op1=ALU.mult)), deps=[sqt, g1], inc=1)

            carry_src = sv["h0"] if k == 0 else hcar[(k - 1) % 2]
            scans = []
            for hc in range(NCH):
                scans.append(P.add(
                    "V", (lambda e, hc=hc, c=carry_src, pp=pp:
                          e.tensor_tensor_scan(
                        out=B_o[:, hc * SPAN:(hc + 1) * SPAN],
                        data0=B_a[pp][:, hc * SPAN:(hc + 1) * SPAN],
                        data1=B_xp[:, hc * SPAN:(hc + 1) * SPAN],
                        initial=c[:, hc:hc + 1], op0=ALU.mult, op1=ALU.add)),
                    deps=[atok, gt]
                    + ([vdma["h0"]] if k == 0 and hc == 0 else []),
                    inc=1))
            marks[("scan", k)] = scans[-1]
            o3 = B_o[:].rearrange("p (hc c) -> p hc c", hc=NCH)
            P.add("V", (lambda e, k=k: e.tensor_copy(
                out=hcar[k % 2][:].rearrange("p (h o) -> p h o", o=1),
                in_=o3[:, :, SPAN - 1:SPAN])), deps=[], inc=1)

        def gm_b(k):
            obf = P.add("V", (lambda e: e.tensor_copy(out=B_obf[:], in_=B_o[:])),
                        deps=[marks.get(("kaT", k - 1))], inc=1)
            marks[("obf", k)] = obf
            o2 = P.add("V", (lambda e: e.tensor_tensor(
                out=B_tr[:], in0=B_obf[:], in1=B_obf[:], op=ALU.mult)),
                deps=[marks.get(("ssum", k - 1)),
                      marks.get(("sq3red", k - 1, 1))], inc=1)
            marks[("o2", k)] = o2

        def ssum_phase(k):
            def _ssum(e):
                for tp in range(TPS):
                    for hc in range(NCH):
                        e.matmul(pk[0][:, 256 + tp * 128:256 + tp * 128 + 128],
                                 B_tr[:, hc * SPAN + tp * 128:
                                      hc * SPAN + tp * 128 + 128],
                                 t_ones[:], start=(hc == 0),
                                 stop=(hc == NCH - 1))
                    e.drain()
                return e.drain()

            ssum = P.add("T", _ssum,
                         deps=[marks[("o2", k)], vdma["ones"],
                               marks.get(("krelu", k - 1, 2)),
                               marks.get(("rmsz", k - 1))], inc=1)
            marks[("ssum", k)] = ssum
            rms_tok = newton_rsqrt(
                (lambda e: e.tensor_scalar(
                    out=zf32[:, 0:2],
                    in0=pk[0][:, 256:512].rearrange(
                        "p (t c) -> p t c", t=2)[:, :, 0:1],
                    scalar1=1.0 / DM,
                    scalar2=1e-6, op0=ALU.mult, op1=ALU.add)),
                2, [ssum])
            marks[("rmsz", k)] = rms_tok

        def wout_phase(k, tp):
            t = k * TPS + tp

            def _wout(e, tp=tp):
                for hc in range(NCH):
                    for nh in range(2):
                        e.matmul(pa[tp][:, nh * 512:(nh + 1) * 512],
                                 B_obf[:, hc * SPAN + tp * 128:
                                       hc * SPAN + tp * 128 + 128],
                                 w_sb["woutt"][:, hc * DM + nh * 512:
                                               hc * DM + nh * 512 + 512],
                                 start=(hc == 0), stop=(hc == NCH - 1))
                    if hc % 2 == 1:
                        e.drain()
                return e.drain()

            attn = P.add("T", _wout,
                         deps=[marks[("obf", k)],
                               marks.get(("f2", k - 1, tp))] +
                              (wout_scaled if k == 0 else []),
                         inc=1)
            marks[("attn", k, tp)] = attn

            wstt = P.add("V", (lambda e, tp=tp: e.scalar_tensor_tensor(
                out=B_w[tp][:], in0=pa[tp][:], scalar=rstd[:, tp:tp + 1],
                in1=xtm[tp][:], op0=ALU.mult, op1=ALU.add,
                accum_out=msum[:])),
                deps=[attn, marks[("rmsz", k)], xtmdma[t]], inc=1)
            marks[("wstt", t)] = wstt
            if debug and t == 30:
                P.add("V", (lambda e: e.tensor_copy(out=dbgw[:], in_=B_w[0][:])),
                      deps=[wstt], inc=1)
            if t + 2 < 2 * nspan:
                xtmdma[t + 2] = P.add("SY", (lambda e, t=t: e.dma_start(
                    out=xtm[t % 2][:],
                    in_=d_xtm[(t + 2) * 128:(t + 3) * 128, :])),
                    deps=[wstt], inc=16)
            P.add("V", (lambda e: e.tensor_scalar(
                out=mcol[:], in0=msum[:], scalar1=1.0 / DM, scalar2=None,
                op0=ALU.mult)), inc=1)
            xc = P.add("V", (lambda e, tp=tp: e.tensor_scalar(
                out=B_w[tp][:], in0=B_w[tp][:], scalar1=mcol[:],
                scalar2=None, op0=ALU.subtract)), inc=1)
            sqp = P.add("V", (lambda e, tp=tp: e.tensor_tensor(
                out=B_tr[:, 0:DM], in0=B_w[tp][:], in1=B_w[tp][:],
                op=ALU.mult)),
                deps=[xc, marks[("ssum", k)]],
                inc=1)
            sq2 = P.add("V", (lambda e: e.reduce_sum(
                out=msum[:], in_=B_tr[:, 0:DM], axis=mybir.AxisListType.X)),
                deps=[sqp], inc=1)
            marks[("sq2red", k, tp)] = sq2
            ln1 = newton_rsqrt(
                (lambda e: e.tensor_scalar(
                    out=zf32[:, 0:1], in0=msum[:], scalar1=1.0 / DM,
                    scalar2=1e-5, op0=ALU.mult, op1=ALU.add)),
                1, [sq2])
            u = P.add("V", (lambda e, tp=tp: e.tensor_scalar(
                out=B_u[tp][:], in0=B_w[tp][:], scalar1=rstd1[:, 0:1],
                scalar2=None, op0=ALU.mult)),
                deps=[ln1, marks.get(("store", t - 2))], inc=1)
            marks[("u", k, tp)] = u
            if debug and t == 30:
                P.add("V", (lambda e: e.tensor_copy(out=dbgu[:], in_=B_u[0][:])),
                      deps=[], inc=1)
                P.add("V", (lambda e: e.tensor_copy(out=dbgr[:], in_=rstd1[:])),
                      deps=[], inc=1)

        def tr_phase(k, tp):
            def _transp(e, tp=tp):
                for hc in range(NCH):
                    e.transpose(pa[tp][:, hc * 128:(hc + 1) * 128],
                                B_u[tp][:, hc * 128:(hc + 1) * 128],
                                t_ident[:])
                return e.drain()

            tr = P.add("T", _transp,
                       deps=[marks[("u", k, tp)], marks[("wstt", k * TPS + tp)],
                             vdma["ident"]], inc=1)
            u3 = uT0[:].rearrange("p (hc c) -> p hc c", hc=NCH)
            pa3 = pa[tp][:].rearrange("p (hc c) -> p hc c", hc=NCH)
            uTev = P.add("S", (lambda e, u3=u3, pa3=pa3, tp=tp: e.activation(
                out=u3[:, :, 1 + tp * 128:1 + tp * 128 + 128],
                in_=pa3[:, :, 0:128], func=AF.Copy)),
                deps=[tr, marks[("tail", k)], marks.get(("mix", k - 1))], inc=1)
            marks[("uTev", k, tp)] = uTev

        def mix_phase(k):
            u3 = uT0[:].rearrange("p (hc c) -> p hc c", hc=NCH)
            d1 = B_o[:].bitcast(BF16)[:, 0:NCH * SPAN]
            d13 = d1.rearrange("p (hc c) -> p hc c", hc=NCH)
            xr3 = B_xrT[:].rearrange("p (hc c) -> p hc c", hc=NCH)
            xk3 = B_xkT[:].rearrange("p (hc c) -> p hc c", hc=NCH)
            mixdeps = [marks[("uTev", k, 0)], marks[("uTev", k, 1)],
                       marks[("tail", k)], vdma["mur"], vdma["muk"],
                       marks.get(("rmm", k - 1, 0)), marks.get(("rmm", k - 1, 1)),
                       marks.get(("kT", k - 1, 3)),
                       marks[("sq2red", k, 1)]]

            def _mix(e, u3=u3, d13=d13, xr3=xr3, xk3=xk3):
                for hc in range(NCH):
                    e.tensor_tensor(out=d13[:, hc, :],
                                    in0=u3[:, hc, 1:SPAN + 1],
                                    in1=u3[:, hc, 0:SPAN], op=ALU.subtract)
                for hc in range(NCH):
                    e.scalar_tensor_tensor(out=xr3[:, hc, :], in0=d13[:, hc, :],
                                           scalar=sv["mur"][:, hc:hc + 1],
                                           in1=u3[:, hc, 0:SPAN],
                                           op0=ALU.mult, op1=ALU.add)
                for hc in range(NCH):
                    e.scalar_tensor_tensor(out=xk3[:, hc, :], in0=d13[:, hc, :],
                                           scalar=sv["muk"][:, hc:hc + 1],
                                           in1=u3[:, hc, 0:SPAN],
                                           op0=ALU.mult, op1=ALU.add)
                return e.nop()

            marks[("mix", k)] = P.add("V", _mix, deps=mixdeps, inc=1)

        def kt_phase(k, rc):
            def _kt(e, rc=rc):
                for oi in range(2):
                    oc = 2 * rc + oi
                    for cc in range(NCH):
                        e.matmul(pk[rc % 2][:, oi * SPAN:(oi + 1) * SPAN],
                                 w_sb["wkt"][:, cc * DM + oc * 128:
                                             cc * DM + oc * 128 + 128],
                                 B_xkT[:, cc * SPAN:(cc + 1) * SPAN],
                                 start=(cc == 0), stop=(cc == NCH - 1))
                    e.drain()
                return e.drain()

            ktdeps = [marks[("mix", k)], wdma["wkt"]]
            if rc >= 2:
                ktdeps.append(marks.get(("krelu", k, rc - 2)))
            else:
                ktdeps.append(marks.get(("krelu", k - 1, rc + 2)))
            if rc == 0:
                ktdeps.append(marks[("rmsz", k)])
            kt = P.add("T", _kt, deps=ktdeps, inc=1)
            marks[("kT", k, rc)] = kt
            krelu = P.add("S", (lambda e, rc=rc: e.activation(
                out=B_obf[:, rc * 512:(rc + 1) * 512],
                in_=pk[rc % 2][:, 0:512], func=AF.Relu)),
                deps=[kt, marks[("attn", k, 0)], marks[("attn", k, 1)],
                      marks[("g1", k)]], inc=1)
            marks[("krelu", k, rc)] = krelu

        def kat_phase(k):
            pp = k % 2
            katok = P.add("V", (lambda e, pp=pp: e.tensor_tensor(
                out=kaT[pp], in0=B_obf[:], in1=B_obf[:], op=ALU.mult)),
                deps=[marks[("krelu", k, rc_)] for rc_ in range(4)]
                + [marks[("scan", k)]], inc=1)
            marks[("kaT", k)] = katok

        def rmm_phase(k, tp):
            def _rmm(e, tp=tp):
                for cc in range(NCH):
                    for nh in range(2):
                        e.matmul(pa[tp][:, nh * 512:(nh + 1) * 512],
                                 B_xrT[:, cc * SPAN + tp * 128:
                                       cc * SPAN + tp * 128 + 128],
                                 w_sb["wrt"][:, cc * DM + nh * 512:
                                             cc * DM + nh * 512 + 512],
                                 start=(cc == 0), stop=(cc == NCH - 1))
                    if cc % 2 == 1:
                        e.drain()
                return e.drain()

            rmm = P.add("T", _rmm,
                        deps=[marks[("mix", k)], wdma["wrt"],
                              marks[("uTev", k, tp)]], inc=1)
            marks[("rmm", k, tp)] = rmm
            rev = P.add("S", (lambda e, tp=tp: e.activation(
                out=B_R[tp][:], in_=pa[tp][:], func=AF.Tanh, scale=0.5)),
                deps=[rmm, marks.get(("f2", k - 1, tp))], inc=1)
            marks[("rev", k, tp)] = rev

        def fmm_phase(k, tp):
            t = k * TPS + tp
            pp = k % 2

            def _fmm(e, tp=tp, pp=pp):
                for cc in range(NCH):
                    for nh in range(2):
                        e.matmul(pa[tp][:, nh * 512:(nh + 1) * 512],
                                 kaT[pp][:, cc * SPAN + tp * 128:
                                         cc * SPAN + tp * 128 + 128],
                                 w_sb["wvt"][:, cc * DM + nh * 512:
                                             cc * DM + nh * 512 + 512],
                                 start=(cc == 0), stop=(cc == NCH - 1))
                    if cc % 2 == 1:
                        e.drain()
                return e.drain()

            fmm = P.add("T", _fmm, deps=[marks[("kaT", k)],
                                         marks[("rev", k, tp)], wdma["wvt"]],
                        inc=1)
            marks[("fmm", k, tp)] = fmm

            f2 = P.add("V", (lambda e, tp=tp: e.scalar_tensor_tensor(
                out=B_w[tp][:], in0=B_R[tp][:], scalar=1.0, in1=pa[tp][:],
                op0=ALU.add, op1=ALU.mult)), deps=[fmm], inc=1)
            marks[("f2", k, tp)] = f2
            P.add("V", (lambda e, tp=tp: e.scalar_tensor_tensor(
                out=B_w[tp][:], in0=B_w[tp][:], scalar=0.5, in1=B_u[tp][:],
                op0=ALU.mult, op1=ALU.add, accum_out=msum[:])), inc=1)
            P.add("V", (lambda e: e.tensor_scalar(
                out=mcol[:], in0=msum[:], scalar1=1.0 / DM, scalar2=None,
                op0=ALU.mult)), inc=1)
            xc2 = P.add("V", (lambda e, tp=tp: e.tensor_scalar(
                out=B_w[tp][:], in0=B_w[tp][:], scalar1=mcol[:],
                scalar2=None, op0=ALU.subtract)), inc=1)
            sqp3 = P.add("V", (lambda e, tp=tp: e.tensor_tensor(
                out=B_tr[:, 0:DM], in0=B_w[tp][:], in1=B_w[tp][:],
                op=ALU.mult)),
                deps=[xc2, f2], inc=1)
            sq3 = P.add("V", (lambda e: e.reduce_sum(
                out=msum[:], in_=B_tr[:, 0:DM], axis=mybir.AxisListType.X)),
                deps=[sqp3], inc=1)
            marks[("sq3red", k, tp)] = sq3
            ln2 = newton_rsqrt(
                (lambda e: e.tensor_scalar(
                    out=zf32[:, 0:1], in0=msum[:], scalar1=1.0 / DM,
                    scalar2=1e-5, op0=ALU.mult, op1=ALU.add)),
                1, [sq3])
            outts = P.add("V", (lambda e, tp=tp: e.tensor_scalar(
                out=B_u[tp][:], in0=B_w[tp][:], scalar1=rstd1[:, 0:1],
                scalar2=None, op0=ALU.mult)), deps=[ln2], inc=1)
            marks[("store", t)] = P.add("SY", (lambda e, t=t, tp=tp: e.dma_start(
                out=d_out[t * 128:(t + 1) * 128, :], in_=B_u[tp][:])),
                deps=[outts], inc=16)

        for g in range(3 * NCH):
            gate_round(0, g)
        tail_and_prefetch(0)
        gm_a(0)
        gm_b(0)
        for k in range(nspan):
            ssum_phase(k)

            gq = (list(range(3 * NCH)) if k + 1 < nspan else [])

            def take(n, k=k, gq=gq):
                for _ in range(n):
                    if gq:
                        gate_round(k + 1, gq.pop(0))

            wout_phase(k, 0)
            take(6)
            tr_phase(k, 0)
            take(6)
            wout_phase(k, 1)
            take(6)
            tr_phase(k, 1)
            take(6)
            mix_phase(k)
            while gq:
                gate_round(k + 1, gq.pop(0))
            if k + 1 < nspan:
                tail_and_prefetch(k + 1)
                gm_a(k + 1)
            kt_phase(k, 0)
            kt_phase(k, 1)
            rmm_phase(k, 0)
            kt_phase(k, 2)
            kt_phase(k, 3)
            kat_phase(k)
            rmm_phase(k, 1)
            fmm_phase(k, 0)
            fmm_phase(k, 1)
            if k + 1 < nspan:
                gm_b(k + 1)
            if k + 2 < nspan:
                xtdma[k + 2] = P.add("G", (lambda e, k=k: e.dma_start(
                    out=xt_bf[k % 2][:], in_=d_xt[k + 2, :, :])),
                    deps=[marks[("gates", k + 1)]], inc=16)

        if debug:
            dbgmap = {"By": B_y, "Bxp": B_xp, "Ba0": B_a[0], "Ba1": B_a[1],
                      "Bo": B_o, "Bobf": B_obf, "Bti": B_ti, "Btr": B_tr,
                      "rstd": rstd, "rstd1": rstd1, "hcl": sv["hcl"],
                      "q": sv["q"], "zf32": zf32, "zcv": zcv, "zsd": zsd,
                      "hcar0": hcar[0], "uT0": uT0, "msum": msum,
                      "dw": dbgw, "du": dbgu, "dr": dbgr,
                      "Bw0": B_w[0], "Bu0": B_u[0]}
            enddeps = [("t", P.cnt["t"]), ("v", P.cnt["v"]), ("s", P.cnt["s"]),
                       ("g", P.cnt["g"])]
            for nm, buf in dbgmap.items():
                th = nc.dram_tensor(f"dbg_{nm}", list(np.asarray(buf[:].shape)),
                                    buf[:].dtype, kind="ExternalOutput").ap()
                P.add("SY", (lambda e, th=th, buf=buf: e.dma_start(
                    out=th, in_=buf[:])), deps=enddeps, inc=16)

        final_val = P.cnt["sy"]
        P.add("SY", (lambda e: e.nop()), deps=[("sy", final_val)], inc=0)

        P.finalize()

        @blk.tensor
        def _(e):
            P.emit("T", e, semh)

        @blk.vector
        def _(e):
            P.emit("V", e, semh)

        @blk.scalar
        def _(e):
            P.emit("S", e, semh)

        @blk.gpsimd
        def _(e):
            P.emit("G", e, semh)

        @blk.sync
        def _(e):
            P.emit("SY", e, semh)

    return nc


_NC_CACHE = None


def _get_nc():
    global _NC_CACHE
    if _NC_CACHE is None:
        _NC_CACHE = build_nc()
    return _NC_CACHE


def _fm(v):
    return np.ascontiguousarray(np.asarray(v, np.float32).reshape(NCH, 128).T)


def kernel(**inputs):
    x = np.asarray(inputs["L0_attn_x"], np.float32)
    shared = {
        "wint": np.ascontiguousarray(np.asarray(inputs["L0_attn_W_in"], np.float32).T),
        "wigt": np.ascontiguousarray(np.asarray(inputs["L0_attn_W_ig"], np.float32).T),
        "wrgt": np.ascontiguousarray(np.asarray(inputs["L0_attn_W_rg"], np.float32).T),
        "woutt": np.ascontiguousarray(np.asarray(inputs["L0_attn_W_out"], np.float32).T),
        "wrt": np.ascontiguousarray(np.asarray(inputs["L0_ffn_W_r"], np.float32).T),
        "wkt": np.ascontiguousarray(np.asarray(inputs["L0_ffn_W_k"], np.float32).T),
        "wvt": np.ascontiguousarray(np.asarray(inputs["L0_ffn_W_v"], np.float32).T),
        "lam": _fm(inputs["L0_attn_lam"]),
        "big": _fm(inputs["L0_attn_b_ig"]),
        "brg": _fm(inputs["L0_attn_b_rg"]),
        "mur": _fm(inputs["L0_ffn_mu_r"]),
        "muk": _fm(inputs["L0_ffn_mu_k"]),
        "h0": _fm(np.asarray(inputs["L0_attn_h0"], np.float32).reshape(DM)),
        "nw": _fm(inputs["L0_attn_norm_w"]),
    }
    in_maps = []
    for b in range(NCORE):
        xb = x[b]
        xt = np.ascontiguousarray(
            xb.reshape(NSPAN, SPAN, NCH, 128).transpose(0, 3, 2, 1)
        ).reshape(NSPAN, 128, NCH * SPAN)
        m = dict(shared)
        m["xtm"] = np.ascontiguousarray(xb)
        m["xt"] = xt
        in_maps.append(m)

    nc = _get_nc()
    res = run_bass_kernel_spmd(nc, in_maps, core_ids=list(range(NCORE)))
    out = np.stack([np.asarray(r["out"], np.float32) for r in res.results])
    return out


# version 23
# speedup vs baseline: 1.0626x; 1.0134x over previous
import math
from contextlib import ExitStack

import numpy as np
import ml_dtypes

import concourse.bass as bass
import concourse.mybir as mybir
from concourse.bass_utils import run_bass_kernel_spmd

F32 = mybir.dt.float32
BF16 = mybir.dt.bfloat16
I32 = mybir.dt.int32
AF = mybir.ActivationFunctionType
ALU = mybir.AluOpType

NCORE = 8
S_TOK = 4096
DM = 1024
NCH = 8
SPAN = 256
NSPAN = S_TOK // SPAN
TPS = SPAN // 128
NTILE = S_TOK // 128

LN2 = math.log(2.0)
SEED_SCALE = -LN2 / 256.0
SEED_BIAS = 127.0 * LN2 / 2.0
NEWTON_ITERS = 2

ENG_SEM = {"T": "t", "V": "v", "S": "s", "SY": "sy", "G": "g"}


class _AutoDrain:

    _WRAP = frozenset({"tensor_scalar", "scalar_tensor_tensor",
                       "tensor_tensor_scan", "tensor_tensor",
                       "tensor_tensor_reduce", "reduce_sum", "tensor_reduce",
                       "tensor_copy"})

    def __init__(self, e, junk):
        self._e = e
        self._junk = junk

    def __getattr__(self, n):
        f = getattr(self._e, n)
        if n in self._WRAP:
            def g(*a, **k):
                f(*a, **k)
                return self._e.drain()
            return g
        return f


class Prog:

    def __init__(self):
        self.ops = []
        self.cnt = {s: 0 for s in ("t", "v", "s", "sy", "g")}
        self.batch_end = {"sy": {}, "g": {}}
        self.junk = None

    def add(self, eng, fn, deps=(), inc=0):
        sem = ENG_SEM[eng]
        tok = None
        if inc:
            self.cnt[sem] += inc
            tok = (sem, self.cnt[sem])
        self.ops.append([eng, fn, tuple(d for d in deps if d is not None), inc, sem,
                         self.cnt[sem]])
        return tok

    def finalize(self):
        cur = {"sy": [], "g": []}

        def close(q):
            if cur[q]:
                endv = cur[q][-1]
                for v in cur[q]:
                    self.batch_end[q][v] = endv
                cur[q] = []

        for eng, fn, deps, inc, sem, val in self.ops:
            if eng not in ("SY", "G"):
                continue
            q = ENG_SEM[eng]
            if deps:
                close(q)
            if inc:
                cur[q].append(val)
        close("sy")
        close("g")

    def emit(self, eng_name, e, semh):
        own = ENG_SEM[eng_name]
        skip_own = own in ("t", "v", "s")
        is_q = own in ("sy", "g")
        wait_e = e
        if eng_name == "V":
            e = _AutoDrain(e, self.junk)
        waited = {}
        for name, fn, deps, inc, sem, val in self.ops:
            if name != eng_name:
                continue
            emitted = False
            rdeps = []
            for (s, v) in deps:
                if s in self.batch_end:
                    v = self.batch_end[s].get(v, v)
                if skip_own and s == own:
                    continue
                rdeps.append((s, v))
            for (s, v) in rdeps:
                if waited.get(s, -1) >= v:
                    continue
                wait_e.wait_ge(semh[s], v)
                waited[s] = v
                emitted = True
            if is_q and inc and deps and not emitted and rdeps:
                s, v = rdeps[0]
                wait_e.wait_ge(semh[s], v)
            inst = fn(e)
            if inc:
                inst.then_inc(semh[sem], inc)


def build_nc(nspan=NSPAN, debug=False):
    nc = bass.Bass("TRN2", target_bir_lowering=False,
                   detect_race_conditions=False)

    d_xtm = nc.dram_tensor("xtm", [S_TOK, DM], F32, kind="ExternalInput").ap()
    d_xt = nc.dram_tensor("xt", [NSPAN, 128, NCH * SPAN], F32,
                          kind="ExternalInput").ap()
    dw = {}
    for wname in ("wint", "wigt", "wrgt", "woutt", "wrt", "wkt", "wvt"):
        dw[wname] = nc.dram_tensor(wname, [DM, DM], F32, kind="ExternalInput").ap()
    dvec = {}
    for vname in ("lam", "big", "brg", "mur", "muk", "h0", "nw"):
        dvec[vname] = nc.dram_tensor(vname, [128, NCH], F32,
                                     kind="ExternalInput").ap()
    d_out = nc.dram_tensor("out", [S_TOK, DM], F32, kind="ExternalOutput").ap()

    ident_h = nc.inline_tensor(np.eye(128, dtype=np.float32), "identf32")
    ones_h = nc.inline_tensor(np.ones((128, 128), dtype=ml_dtypes.bfloat16),
                              "onesbf")

    P = Prog()

    with ExitStack() as ctx:
        E = ctx.enter_context
        w_sb = {n: E(nc.sbuf_tensor(f"sb_{n}", [128, NCH * DM], BF16))
                for n in ("wint", "wigt", "wrgt", "woutt", "wrt", "wkt", "wvt")}
        xt_bf = [E(nc.sbuf_tensor(f"xtbf{i}", [128, NCH * SPAN], BF16))
                 for i in range(2)]
        xtm = [E(nc.sbuf_tensor(f"xtm{i}", [128, DM], F32)) for i in range(2)]
        B_tr = E(nc.sbuf_tensor("Btr", [128, NCH * SPAN], BF16))
        B_ti = E(nc.sbuf_tensor("Bti", [128, NCH * SPAN], BF16))
        B_xp = E(nc.sbuf_tensor("Bxp", [128, NCH * SPAN], BF16))
        B_y = E(nc.sbuf_tensor("By", [128, NCH * SPAN], BF16))
        B_a = [E(nc.sbuf_tensor(f"Ba{i}", [128, NCH * SPAN], F32))
               for i in range(2)]
        B_o = E(nc.sbuf_tensor("Bo", [128, NCH * SPAN], F32))
        B_obf = E(nc.sbuf_tensor("Bobf", [128, NCH * SPAN], BF16))
        uT0 = E(nc.sbuf_tensor("uT0", [128, NCH * (SPAN + 1)], BF16))
        B_xrT = E(nc.sbuf_tensor("BxrT", [128, NCH * SPAN], BF16))
        B_xkT = E(nc.sbuf_tensor("BxkT", [128, NCH * SPAN], BF16))
        B_w = [E(nc.sbuf_tensor(f"Bw{i}", [128, DM], F32)) for i in range(2)]
        B_u = [E(nc.sbuf_tensor(f"Bu{i}", [128, DM], F32)) for i in range(2)]
        B_R = [E(nc.sbuf_tensor(f"BR{i}", [128, DM], BF16)) for i in range(2)]
        sv = {n: E(nc.sbuf_tensor(f"v_{n}", [128, NCH], F32))
              for n in ("lam", "big", "brg", "mur", "muk", "h0", "nw",
                        "q", "hcl", "hbig", "hbrg", "ts1", "ts2")}
        hcar = [E(nc.sbuf_tensor(f"hcar{i}", [128, NCH], F32)) for i in range(2)]
        t_ident = E(nc.sbuf_tensor("identsb", [128, 128], F32))
        t_ones = E(nc.sbuf_tensor("onessb", [128, 128], BF16))
        zf32 = E(nc.sbuf_tensor("zf32", [128, 2], F32))
        zi32 = E(nc.sbuf_tensor("zi32", [128, 2], I32))
        zcv = E(nc.sbuf_tensor("zcv", [128, 2], F32))
        zsd = E(nc.sbuf_tensor("zsd", [128, 2], F32))
        znt = E(nc.sbuf_tensor("znt", [128, 2], F32))
        rstd = E(nc.sbuf_tensor("rstd", [128, 2], F32))
        rstd1 = E(nc.sbuf_tensor("rstd1", [128, 1], F32))
        msum = E(nc.sbuf_tensor("msum", [128, 1], F32))
        mcol = E(nc.sbuf_tensor("mcol", [128, 1], F32))
        junk = E(nc.sbuf_tensor("junk", [128, 2], F32))
        P.junk = junk
        c_sqb = E(nc.sbuf_tensor("c_sqb", [128, 1], F32))
        c_sdb = E(nc.sbuf_tensor("c_sdb", [128, 1], F32))
        if debug:
            dbgw = E(nc.sbuf_tensor("dbgw", [128, DM], BF16))
            dbgu = E(nc.sbuf_tensor("dbgu", [128, DM], BF16))
            dbgr = E(nc.sbuf_tensor("dbgr", [128, 1], F32))

        kaT = [B_a[i][:].bitcast(BF16)[:, 0:NCH * SPAN] for i in range(2)]

        pg = [E(nc.psum_tensor(f"pg{i}", [128, 512], F32)) for i in range(2)]
        pa = [E(nc.psum_tensor(f"pa{i}", [128, 1024], F32)) for i in range(2)]
        pk = [E(nc.psum_tensor(f"pk{i}", [128, 512], F32)) for i in range(2)]

        semh = {s: E(nc.semaphore(s)) for s in ("t", "v", "s", "sy", "g")}
        blk = E(nc.Block())

        wdma = {}
        for n in ("wrgt", "wigt", "wint"):
            s3 = dw[n].rearrange("(dc p) h -> p dc h", p=128)
            d3 = w_sb[n][:].rearrange("p (dc h) -> p dc h", dc=NCH)
            wdma[n] = P.add("G", (lambda e, s=s3, d=d3: e.dma_start(out=d, in_=s)),
                            inc=16)
        xtdma = {}
        for k in range(2):
            xtdma[k] = P.add(
                "G", (lambda e, k=k: e.dma_start(out=xt_bf[k][:],
                                                 in_=d_xt[k, :, :])),
                inc=16)
        for n in ("woutt", "wrt", "wkt", "wvt"):
            s3 = dw[n].rearrange("(dc p) h -> p dc h", p=128)
            d3 = w_sb[n][:].rearrange("p (dc h) -> p dc h", dc=NCH)
            wdma[n] = P.add("G", (lambda e, s=s3, d=d3: e.dma_start(out=d, in_=s)),
                            deps=[("g", P.cnt["g"])], inc=16)

        vdma = {}
        for n in ("lam", "big", "brg", "mur", "muk", "h0", "nw"):
            vdma[n] = P.add(
                "SY", (lambda e, n=n: e.dma_start(out=sv[n][:], in_=dvec[n])),
                inc=16)
        vdma["ident"] = P.add(
            "SY", (lambda e: e.dma_start(out=t_ident[:], in_=ident_h.ap())), inc=16)
        vdma["ones"] = P.add(
            "SY", (lambda e: e.dma_start(out=t_ones[:], in_=ones_h.ap())), inc=16)
        xtmdma = {}
        for t in range(2):
            xtmdma[t] = P.add(
                "SY", (lambda e, t=t: e.dma_start(
                    out=xtm[t % 2][:], in_=d_xtm[t * 128:(t + 1) * 128, :])),
                inc=16)

        wout_scaled = []
        for hc in range(NCH):
            tok = P.add("V", (lambda e, hc=hc: e.tensor_scalar(
                out=w_sb["woutt"][:, hc * DM:(hc + 1) * DM],
                in0=w_sb["woutt"][:, hc * DM:(hc + 1) * DM],
                scalar1=sv["nw"][:, hc:hc + 1], scalar2=None, op0=ALU.mult)),
                deps=[wdma["woutt"], vdma["nw"]], inc=1)
            wout_scaled.append(tok)

        qtok = P.add("S", (lambda e: e.activation(
            out=sv["q"][:], in_=sv["lam"][:], func=AF.Exp, scale=-1.0)),
            deps=[vdma["lam"]], inc=1)
        q, t1, t2 = sv["q"], sv["ts1"], sv["ts2"]

        def _tay(e):
            e.tensor_scalar(out=t1[:], in0=q[:], scalar1=-0.25, scalar2=1.0 / 3.0,
                            op0=ALU.mult, op1=ALU.add)
            e.tensor_tensor(out=t2[:], in0=t1[:], in1=q[:], op=ALU.mult)
            e.tensor_scalar(out=t1[:], in0=t2[:], scalar1=-1.0, scalar2=0.5,
                            op0=ALU.mult, op1=ALU.add)
            e.tensor_tensor(out=t2[:], in0=t1[:], in1=q[:], op=ALU.mult)
            e.tensor_scalar(out=t1[:], in0=t2[:], scalar1=-1.0, scalar2=1.0,
                            op0=ALU.mult, op1=ALU.add)
            e.tensor_tensor(out=t2[:], in0=t1[:], in1=q[:], op=ALU.mult)
            e.tensor_scalar(out=t1[:], in0=t2[:], scalar1=-1.0, scalar2=1e-8,
                            op0=ALU.mult, op1=ALU.add)
            e.scalar_tensor_tensor(out=t2[:], in0=q[:], scalar=1e-8, in1=t1[:],
                                   op0=ALU.mult, op1=ALU.add)
            return e.tensor_scalar(out=sv["hcl"][:], in0=t2[:], scalar1=4.0,
                                   scalar2=None, op0=ALU.mult)

        hcl_tok = P.add("V", _tay, deps=[qtok], inc=1)
        hbig_tok = P.add("V", (lambda e: e.tensor_scalar(
            out=sv["hbig"][:], in0=sv["big"][:], scalar1=0.5, scalar2=None,
            op0=ALU.mult)), deps=[vdma["big"]], inc=1)
        hbrg_tok = P.add("V", (lambda e: e.tensor_scalar(
            out=sv["hbrg"][:], in0=sv["brg"][:], scalar1=0.5, scalar2=None,
            op0=ALU.mult)), deps=[vdma["brg"]], inc=1)
        sqb_tok = P.add("V", (lambda e: e.memset(c_sqb[:], 1.000001)), inc=1)
        sdb_tok = P.add("V", (lambda e: e.memset(c_sdb[:], SEED_BIAS)), inc=1)
        ut0z = P.add("V", (lambda e: e.memset(
            uT0[:].rearrange("p (hc c) -> p hc c", hc=NCH)[:, :, 0:1], 0.0)),
            inc=1)

        marks = {}

        def newton_rsqrt(zsrc_fn, ncols, zdeps):
            dst = rstd[:, 0:ncols] if ncols == 2 else rstd1[:, 0:1]
            zt = P.add("V", zsrc_fn, deps=zdeps, inc=1)

            def _prep(e):
                e.tensor_scalar(out=zi32[:, 0:ncols],
                                in0=zf32[:, 0:ncols].bitcast(I32), scalar1=16,
                                scalar2=None, op0=ALU.logical_shift_right)
                return e.tensor_copy(out=zcv[:, 0:ncols], in_=zi32[:, 0:ncols])

            cv = P.add("V", _prep, deps=[zt], inc=1)
            sd = P.add("S", (lambda e: e.activation(
                out=zsd[:, 0:ncols], in_=zcv[:, 0:ncols], func=AF.Exp,
                scale=SEED_SCALE, bias=c_sdb[:])), deps=[cv, sdb_tok], inc=1)

            def _newton(e):
                y = dst
                e.tensor_copy(out=y, in_=zsd[:, 0:ncols])
                for _ in range(NEWTON_ITERS):
                    e.tensor_tensor(out=znt[:, 0:ncols], in0=y, in1=y, op=ALU.mult)
                    e.tensor_tensor(out=znt[:, 0:ncols], in0=znt[:, 0:ncols],
                                    in1=zf32[:, 0:ncols], op=ALU.mult)
                    e.tensor_scalar(out=znt[:, 0:ncols], in0=znt[:, 0:ncols],
                                    scalar1=-0.5, scalar2=1.5, op0=ALU.mult,
                                    op1=ALU.add)
                    e.tensor_tensor(out=y, in0=y, in1=znt[:, 0:ncols], op=ALU.mult)
                return e.nop()

            return P.add("V", _newton, deps=[sd], inc=1)

        def gate_round(k, grot):
            proj = grot // NCH
            hc = grot % NCH
            rot = grot % 2
            pp = k % 2
            wn = ("wrgt", "wigt", "wint")[proj]
            deps = [xtdma[k], wdma[wn],
                    marks.get(("gev", k, grot - 2)),
                    hbrg_tok if proj == 0 else None,
                    hbig_tok if proj == 1 else None]
            if grot < 2:
                deps.append(marks.get(("gev", k - 1, 22 + grot)))

            def _gmm(e, wn=wn, hc=hc, rot=rot, k=k):
                for dc in range(NCH):
                    e.matmul(pg[rot][:, 0:SPAN],
                             w_sb[wn][:, dc * DM + hc * 128:
                                      dc * DM + hc * 128 + 128],
                             xt_bf[k % 2][:, dc * SPAN:(dc + 1) * SPAN],
                             start=(dc == 0), stop=(dc == NCH - 1))
                return e.drain()

            mmtok = P.add("T", _gmm, deps=deps, inc=1)
            marks[("gates", k)] = mmtok

            if proj == 0:
                bdeps = [mmtok, marks.get(("scan", k - 2)),
                         marks.get(("fmm", k - 2, 0)),
                         marks.get(("fmm", k - 2, 1))]

                def _ev(e, hc=hc, rot=rot, pp=pp):
                    return e.activation(
                        out=B_a[pp][:, hc * SPAN:(hc + 1) * SPAN],
                        in_=pg[rot][:, 0:SPAN], func=AF.Tanh, scale=0.5,
                        bias=sv["hbrg"][:, hc:hc + 1])
            elif proj == 1:
                bdeps = [mmtok, marks.get(("g1", k - 1))]

                def _ev(e, hc=hc, rot=rot):
                    return e.activation(
                        out=B_ti[:, hc * SPAN:(hc + 1) * SPAN],
                        in_=pg[rot][:, 0:SPAN], func=AF.Tanh, scale=0.5,
                        bias=sv["hbig"][:, hc:hc + 1])
            else:
                bdeps = [mmtok, marks.get(("scan", k - 1))]

                def _ev(e, hc=hc, rot=rot):
                    return e.activation(
                        out=B_xp[:, hc * SPAN:(hc + 1) * SPAN],
                        in_=pg[rot][:, 0:SPAN], func=AF.Copy)

            marks[("gev", k, grot)] = P.add("S", _ev, deps=bdeps, inc=1)

        def tail_and_prefetch(k):
            if k == 0:
                marks[("tail", 0)] = ut0z
            else:
                u3 = uT0[:].rearrange("p (hc c) -> p hc c", hc=NCH)
                marks[("tail", k)] = P.add(
                    "V", (lambda e, u3=u3: e.tensor_copy(
                        out=u3[:, :, 0:1], in_=u3[:, :, SPAN:SPAN + 1])),
                    deps=[marks.get(("uTev", k - 1, 0)),
                          marks.get(("uTev", k - 1, 1)),
                          marks.get(("mix", k - 1))],
                    inc=1)

        def gm_a(k):
            pp = k % 2
            ytoks = []
            for hc in range(NCH):
                ytoks.append(P.add("V", (lambda e, hc=hc, pp=pp: e.tensor_scalar(
                    out=B_y[:, hc * SPAN:(hc + 1) * SPAN],
                    in0=B_a[pp][:, hc * SPAN:(hc + 1) * SPAN],
                    scalar1=sv["hcl"][:, hc:hc + 1],
                    scalar2=sv["hcl"][:, hc:hc + 1],
                    op0=ALU.mult, op1=ALU.add)),
                    deps=[marks[("gev", k, hc)], hcl_tok], inc=1))
            g1 = P.add("V", (lambda e: e.scalar_tensor_tensor(
                out=B_xp[:], in0=B_ti[:], scalar=1.0, in1=B_xp[:],
                op0=ALU.add, op1=ALU.mult)),
                deps=[marks[("gev", k, 15)], marks[("gev", k, 23)]], inc=1)
            marks[("g1", k)] = g1
            atok = P.add("S", (lambda e, pp=pp: e.activation(
                out=B_a[pp][:], in_=B_y[:], func=AF.Exp)),
                deps=ytoks, inc=1)
            e2y = P.add("S", (lambda e: e.activation(
                out=B_o[:], in_=B_y[:], func=AF.Exp, scale=2.0)),
                deps=[marks.get(("mix", k - 1))], inc=1)
            sqt = P.add("S", (lambda e: e.activation(
                out=B_y[:], in_=B_o[:], func=AF.Sqrt, scale=-1.0,
                bias=c_sqb[:])), deps=[e2y, sqb_tok], inc=1)
            gt = P.add("V", (lambda e: e.scalar_tensor_tensor(
                out=B_xp[:], in0=B_xp[:], scalar=0.5, in1=B_y[:],
                op0=ALU.mult, op1=ALU.mult)), deps=[sqt, g1], inc=1)

            carry_src = sv["h0"] if k == 0 else hcar[(k - 1) % 2]
            scans = []
            for hc in range(NCH):
                scans.append(P.add(
                    "V", (lambda e, hc=hc, c=carry_src, pp=pp:
                          e.tensor_tensor_scan(
                        out=B_o[:, hc * SPAN:(hc + 1) * SPAN],
                        data0=B_a[pp][:, hc * SPAN:(hc + 1) * SPAN],
                        data1=B_xp[:, hc * SPAN:(hc + 1) * SPAN],
                        initial=c[:, hc:hc + 1], op0=ALU.mult, op1=ALU.add)),
                    deps=[atok, gt]
                    + ([vdma["h0"]] if k == 0 and hc == 0 else []),
                    inc=1))
            marks[("scan", k)] = scans[-1]
            o3 = B_o[:].rearrange("p (hc c) -> p hc c", hc=NCH)
            P.add("V", (lambda e, k=k: e.tensor_copy(
                out=hcar[k % 2][:].rearrange("p (h o) -> p h o", o=1),
                in_=o3[:, :, SPAN - 1:SPAN])), deps=[], inc=1)

        def gm_b(k):
            obf = P.add("V", (lambda e: e.tensor_copy(out=B_obf[:], in_=B_o[:])),
                        deps=[marks.get(("kaT", k - 1))], inc=1)
            marks[("obf", k)] = obf
            o2 = P.add("V", (lambda e: e.tensor_tensor(
                out=B_tr[:], in0=B_obf[:], in1=B_obf[:], op=ALU.mult)),
                deps=[marks.get(("ssum", k - 1)),
                      marks.get(("sq3red", k - 1, 1))], inc=1)
            marks[("o2", k)] = o2

        def ssum_phase(k):
            def _ssum(e):
                for tp in range(TPS):
                    for hc in range(NCH):
                        e.matmul(pk[0][:, 256 + tp * 128:256 + tp * 128 + 128],
                                 B_tr[:, hc * SPAN + tp * 128:
                                      hc * SPAN + tp * 128 + 128],
                                 t_ones[:], start=(hc == 0),
                                 stop=(hc == NCH - 1))
                    e.drain()
                return e.drain()

            ssum = P.add("T", _ssum,
                         deps=[marks[("o2", k)], vdma["ones"],
                               marks.get(("krelu", k - 1, 2)),
                               marks.get(("rmsz", k - 1))], inc=1)
            marks[("ssum", k)] = ssum
            rms_tok = newton_rsqrt(
                (lambda e: e.tensor_scalar(
                    out=zf32[:, 0:2],
                    in0=pk[0][:, 256:512].rearrange(
                        "p (t c) -> p t c", t=2)[:, :, 0:1],
                    scalar1=1.0 / DM,
                    scalar2=1e-6, op0=ALU.mult, op1=ALU.add)),
                2, [ssum])
            marks[("rmsz", k)] = rms_tok

        def wout_phase(k, tp):
            t = k * TPS + tp

            def _wout(e, tp=tp):
                for hc in range(NCH):
                    for nh in range(2):
                        e.matmul(pa[tp][:, nh * 512:(nh + 1) * 512],
                                 B_obf[:, hc * SPAN + tp * 128:
                                       hc * SPAN + tp * 128 + 128],
                                 w_sb["woutt"][:, hc * DM + nh * 512:
                                               hc * DM + nh * 512 + 512],
                                 start=(hc == 0), stop=(hc == NCH - 1))
                    if hc % 2 == 1:
                        e.drain()
                return e.drain()

            attn = P.add("T", _wout,
                         deps=[marks[("obf", k)],
                               marks.get(("f2", k - 1, tp))] +
                              (wout_scaled if k == 0 else []),
                         inc=1)
            marks[("attn", k, tp)] = attn

            wstt = P.add("V", (lambda e, tp=tp: e.scalar_tensor_tensor(
                out=B_w[tp][:], in0=pa[tp][:], scalar=rstd[:, tp:tp + 1],
                in1=xtm[tp][:], op0=ALU.mult, op1=ALU.add,
                accum_out=msum[:])),
                deps=[attn, marks[("rmsz", k)], xtmdma[t]], inc=1)
            marks[("wstt", t)] = wstt
            if debug and t == 30:
                P.add("V", (lambda e: e.tensor_copy(out=dbgw[:], in_=B_w[0][:])),
                      deps=[wstt], inc=1)
            if t + 2 < 2 * nspan:
                xtmdma[t + 2] = P.add("SY", (lambda e, t=t: e.dma_start(
                    out=xtm[t % 2][:],
                    in_=d_xtm[(t + 2) * 128:(t + 3) * 128, :])),
                    deps=[wstt], inc=16)
            P.add("V", (lambda e: e.tensor_scalar(
                out=mcol[:], in0=msum[:], scalar1=1.0 / DM, scalar2=None,
                op0=ALU.mult)), inc=1)
            xc = P.add("V", (lambda e, tp=tp: e.tensor_scalar(
                out=B_w[tp][:], in0=B_w[tp][:], scalar1=mcol[:],
                scalar2=None, op0=ALU.subtract)), inc=1)
            sqp = P.add("V", (lambda e, tp=tp: e.tensor_tensor(
                out=B_tr[:, 0:DM], in0=B_w[tp][:], in1=B_w[tp][:],
                op=ALU.mult)),
                deps=[xc, marks[("ssum", k)]],
                inc=1)
            sq2 = P.add("V", (lambda e: e.reduce_sum(
                out=msum[:], in_=B_tr[:, 0:DM], axis=mybir.AxisListType.X)),
                deps=[sqp], inc=1)
            marks[("sq2red", k, tp)] = sq2
            ln1 = newton_rsqrt(
                (lambda e: e.tensor_scalar(
                    out=zf32[:, 0:1], in0=msum[:], scalar1=1.0 / DM,
                    scalar2=1e-5, op0=ALU.mult, op1=ALU.add)),
                1, [sq2])
            u = P.add("V", (lambda e, tp=tp: e.tensor_scalar(
                out=B_u[tp][:], in0=B_w[tp][:], scalar1=rstd1[:, 0:1],
                scalar2=None, op0=ALU.mult)),
                deps=[ln1, marks.get(("store", t - 2))], inc=1)
            marks[("u", k, tp)] = u
            if debug and t == 30:
                P.add("V", (lambda e: e.tensor_copy(out=dbgu[:], in_=B_u[0][:])),
                      deps=[], inc=1)
                P.add("V", (lambda e: e.tensor_copy(out=dbgr[:], in_=rstd1[:])),
                      deps=[], inc=1)

        def tr_phase(k, tp):
            def _transp(e, tp=tp):
                for hc in range(NCH):
                    e.transpose(pa[tp][:, hc * 128:(hc + 1) * 128],
                                B_u[tp][:, hc * 128:(hc + 1) * 128],
                                t_ident[:])
                return e.drain()

            tr = P.add("T", _transp,
                       deps=[marks[("u", k, tp)], marks[("wstt", k * TPS + tp)],
                             vdma["ident"]], inc=1)
            u3 = uT0[:].rearrange("p (hc c) -> p hc c", hc=NCH)
            pa3 = pa[tp][:].rearrange("p (hc c) -> p hc c", hc=NCH)
            uTev = P.add("S", (lambda e, u3=u3, pa3=pa3, tp=tp: e.activation(
                out=u3[:, :, 1 + tp * 128:1 + tp * 128 + 128],
                in_=pa3[:, :, 0:128], func=AF.Copy)),
                deps=[tr, marks[("tail", k)], marks.get(("mix", k - 1))], inc=1)
            marks[("uTev", k, tp)] = uTev

        def mix_phase(k):
            u3 = uT0[:].rearrange("p (hc c) -> p hc c", hc=NCH)
            d1 = B_o[:].bitcast(BF16)[:, 0:NCH * SPAN]
            d13 = d1.rearrange("p (hc c) -> p hc c", hc=NCH)
            xr3 = B_xrT[:].rearrange("p (hc c) -> p hc c", hc=NCH)
            xk3 = B_xkT[:].rearrange("p (hc c) -> p hc c", hc=NCH)
            mixdeps = [marks[("uTev", k, 0)], marks[("uTev", k, 1)],
                       marks[("tail", k)], vdma["mur"], vdma["muk"],
                       marks.get(("rmm", k - 1, 0)), marks.get(("rmm", k - 1, 1)),
                       marks.get(("kT", k - 1, 3)),
                       marks[("sq2red", k, 1)]]

            def _mix(e, u3=u3, d13=d13, xr3=xr3, xk3=xk3):
                e.tensor_tensor(out=d13[:, :, :],
                                in0=u3[:, :, 1:SPAN + 1],
                                in1=u3[:, :, 0:SPAN], op=ALU.subtract)
                for hc in range(NCH):
                    e.scalar_tensor_tensor(out=xr3[:, hc, :], in0=d13[:, hc, :],
                                           scalar=sv["mur"][:, hc:hc + 1],
                                           in1=u3[:, hc, 0:SPAN],
                                           op0=ALU.mult, op1=ALU.add)
                for hc in range(NCH):
                    e.scalar_tensor_tensor(out=xk3[:, hc, :], in0=d13[:, hc, :],
                                           scalar=sv["muk"][:, hc:hc + 1],
                                           in1=u3[:, hc, 0:SPAN],
                                           op0=ALU.mult, op1=ALU.add)
                return e.nop()

            marks[("mix", k)] = P.add("V", _mix, deps=mixdeps, inc=1)

        def kt_phase(k, rc):
            def _kt(e, rc=rc):
                for oi in range(2):
                    oc = 2 * rc + oi
                    for cc in range(NCH):
                        e.matmul(pk[rc % 2][:, oi * SPAN:(oi + 1) * SPAN],
                                 w_sb["wkt"][:, cc * DM + oc * 128:
                                             cc * DM + oc * 128 + 128],
                                 B_xkT[:, cc * SPAN:(cc + 1) * SPAN],
                                 start=(cc == 0), stop=(cc == NCH - 1))
                    e.drain()
                return e.drain()

            ktdeps = [marks[("mix", k)], wdma["wkt"]]
            if rc >= 2:
                ktdeps.append(marks.get(("krelu", k, rc - 2)))
            else:
                ktdeps.append(marks.get(("krelu", k - 1, rc + 2)))
            if rc == 0:
                ktdeps.append(marks[("rmsz", k)])
            kt = P.add("T", _kt, deps=ktdeps, inc=1)
            marks[("kT", k, rc)] = kt
            krelu = P.add("S", (lambda e, rc=rc: e.activation(
                out=B_obf[:, rc * 512:(rc + 1) * 512],
                in_=pk[rc % 2][:, 0:512], func=AF.Relu)),
                deps=[kt, marks[("attn", k, 0)], marks[("attn", k, 1)],
                      marks[("g1", k)]], inc=1)
            marks[("krelu", k, rc)] = krelu

        def kat_phase(k):
            pp = k % 2
            katok = P.add("V", (lambda e, pp=pp: e.tensor_tensor(
                out=kaT[pp], in0=B_obf[:], in1=B_obf[:], op=ALU.mult)),
                deps=[marks[("krelu", k, rc_)] for rc_ in range(4)]
                + [marks[("scan", k)]], inc=1)
            marks[("kaT", k)] = katok

        def rmm_phase(k, tp):
            def _rmm(e, tp=tp):
                for cc in range(NCH):
                    for nh in range(2):
                        e.matmul(pa[tp][:, nh * 512:(nh + 1) * 512],
                                 B_xrT[:, cc * SPAN + tp * 128:
                                       cc * SPAN + tp * 128 + 128],
                                 w_sb["wrt"][:, cc * DM + nh * 512:
                                             cc * DM + nh * 512 + 512],
                                 start=(cc == 0), stop=(cc == NCH - 1))
                    if cc % 2 == 1:
                        e.drain()
                return e.drain()

            rmm = P.add("T", _rmm,
                        deps=[marks[("mix", k)], wdma["wrt"],
                              marks[("uTev", k, tp)]], inc=1)
            marks[("rmm", k, tp)] = rmm
            rev = P.add("S", (lambda e, tp=tp: e.activation(
                out=B_R[tp][:], in_=pa[tp][:], func=AF.Tanh, scale=0.5)),
                deps=[rmm, marks.get(("f2", k - 1, tp))], inc=1)
            marks[("rev", k, tp)] = rev

        def fmm_phase(k, tp):
            t = k * TPS + tp
            pp = k % 2

            def _fmm(e, tp=tp, pp=pp):
                for cc in range(NCH):
                    for nh in range(2):
                        e.matmul(pa[tp][:, nh * 512:(nh + 1) * 512],
                                 kaT[pp][:, cc * SPAN + tp * 128:
                                         cc * SPAN + tp * 128 + 128],
                                 w_sb["wvt"][:, cc * DM + nh * 512:
                                             cc * DM + nh * 512 + 512],
                                 start=(cc == 0), stop=(cc == NCH - 1))
                    if cc % 2 == 1:
                        e.drain()
                return e.drain()

            fmm = P.add("T", _fmm, deps=[marks[("kaT", k)],
                                         marks[("rev", k, tp)], wdma["wvt"]],
                        inc=1)
            marks[("fmm", k, tp)] = fmm

            f2 = P.add("V", (lambda e, tp=tp: e.scalar_tensor_tensor(
                out=B_w[tp][:], in0=B_R[tp][:], scalar=1.0, in1=pa[tp][:],
                op0=ALU.add, op1=ALU.mult)), deps=[fmm], inc=1)
            marks[("f2", k, tp)] = f2
            P.add("V", (lambda e, tp=tp: e.scalar_tensor_tensor(
                out=B_w[tp][:], in0=B_w[tp][:], scalar=0.5, in1=B_u[tp][:],
                op0=ALU.mult, op1=ALU.add, accum_out=msum[:])), inc=1)
            P.add("V", (lambda e: e.tensor_scalar(
                out=mcol[:], in0=msum[:], scalar1=1.0 / DM, scalar2=None,
                op0=ALU.mult)), inc=1)
            xc2 = P.add("V", (lambda e, tp=tp: e.tensor_scalar(
                out=B_w[tp][:], in0=B_w[tp][:], scalar1=mcol[:],
                scalar2=None, op0=ALU.subtract)), inc=1)
            sqp3 = P.add("V", (lambda e, tp=tp: e.tensor_tensor(
                out=B_tr[:, 0:DM], in0=B_w[tp][:], in1=B_w[tp][:],
                op=ALU.mult)),
                deps=[xc2, f2], inc=1)
            sq3 = P.add("V", (lambda e: e.reduce_sum(
                out=msum[:], in_=B_tr[:, 0:DM], axis=mybir.AxisListType.X)),
                deps=[sqp3], inc=1)
            marks[("sq3red", k, tp)] = sq3
            ln2 = newton_rsqrt(
                (lambda e: e.tensor_scalar(
                    out=zf32[:, 0:1], in0=msum[:], scalar1=1.0 / DM,
                    scalar2=1e-5, op0=ALU.mult, op1=ALU.add)),
                1, [sq3])
            outts = P.add("V", (lambda e, tp=tp: e.tensor_scalar(
                out=B_u[tp][:], in0=B_w[tp][:], scalar1=rstd1[:, 0:1],
                scalar2=None, op0=ALU.mult)), deps=[ln2], inc=1)
            marks[("store", t)] = P.add("SY", (lambda e, t=t, tp=tp: e.dma_start(
                out=d_out[t * 128:(t + 1) * 128, :], in_=B_u[tp][:])),
                deps=[outts], inc=16)

        for g in range(3 * NCH):
            gate_round(0, g)
        tail_and_prefetch(0)
        gm_a(0)
        gm_b(0)
        for k in range(nspan):
            ssum_phase(k)

            gq = (list(range(3 * NCH)) if k + 1 < nspan else [])

            def take(n, k=k, gq=gq):
                for _ in range(n):
                    if gq:
                        gate_round(k + 1, gq.pop(0))

            wout_phase(k, 0)
            take(4)
            wout_phase(k, 1)
            take(8)
            tr_phase(k, 0)
            take(6)
            tr_phase(k, 1)
            take(6)
            mix_phase(k)
            while gq:
                gate_round(k + 1, gq.pop(0))
            if k + 1 < nspan:
                tail_and_prefetch(k + 1)
                gm_a(k + 1)
            kt_phase(k, 0)
            kt_phase(k, 1)
            rmm_phase(k, 0)
            kt_phase(k, 2)
            kt_phase(k, 3)
            kat_phase(k)
            rmm_phase(k, 1)
            fmm_phase(k, 0)
            fmm_phase(k, 1)
            if k + 1 < nspan:
                gm_b(k + 1)
            if k + 2 < nspan:
                xtdma[k + 2] = P.add("G", (lambda e, k=k: e.dma_start(
                    out=xt_bf[k % 2][:], in_=d_xt[k + 2, :, :])),
                    deps=[marks[("gates", k + 1)]], inc=16)

        if debug:
            dbgmap = {"By": B_y, "Bxp": B_xp, "Ba0": B_a[0], "Ba1": B_a[1],
                      "Bo": B_o, "Bobf": B_obf, "Bti": B_ti, "Btr": B_tr,
                      "rstd": rstd, "rstd1": rstd1, "hcl": sv["hcl"],
                      "q": sv["q"], "zf32": zf32, "zcv": zcv, "zsd": zsd,
                      "hcar0": hcar[0], "uT0": uT0, "msum": msum,
                      "dw": dbgw, "du": dbgu, "dr": dbgr,
                      "Bw0": B_w[0], "Bu0": B_u[0]}
            enddeps = [("t", P.cnt["t"]), ("v", P.cnt["v"]), ("s", P.cnt["s"]),
                       ("g", P.cnt["g"])]
            for nm, buf in dbgmap.items():
                th = nc.dram_tensor(f"dbg_{nm}", list(np.asarray(buf[:].shape)),
                                    buf[:].dtype, kind="ExternalOutput").ap()
                P.add("SY", (lambda e, th=th, buf=buf: e.dma_start(
                    out=th, in_=buf[:])), deps=enddeps, inc=16)

        final_val = P.cnt["sy"]
        P.add("SY", (lambda e: e.nop()), deps=[("sy", final_val)], inc=0)

        P.finalize()

        @blk.tensor
        def _(e):
            P.emit("T", e, semh)

        @blk.vector
        def _(e):
            P.emit("V", e, semh)

        @blk.scalar
        def _(e):
            P.emit("S", e, semh)

        @blk.gpsimd
        def _(e):
            P.emit("G", e, semh)

        @blk.sync
        def _(e):
            P.emit("SY", e, semh)

    return nc


_NC_CACHE = None


def _get_nc():
    global _NC_CACHE
    if _NC_CACHE is None:
        _NC_CACHE = build_nc()
    return _NC_CACHE


def _fm(v):
    return np.ascontiguousarray(np.asarray(v, np.float32).reshape(NCH, 128).T)


def kernel(**inputs):
    x = np.asarray(inputs["L0_attn_x"], np.float32)
    shared = {
        "wint": np.ascontiguousarray(np.asarray(inputs["L0_attn_W_in"], np.float32).T),
        "wigt": np.ascontiguousarray(np.asarray(inputs["L0_attn_W_ig"], np.float32).T),
        "wrgt": np.ascontiguousarray(np.asarray(inputs["L0_attn_W_rg"], np.float32).T),
        "woutt": np.ascontiguousarray(np.asarray(inputs["L0_attn_W_out"], np.float32).T),
        "wrt": np.ascontiguousarray(np.asarray(inputs["L0_ffn_W_r"], np.float32).T),
        "wkt": np.ascontiguousarray(np.asarray(inputs["L0_ffn_W_k"], np.float32).T),
        "wvt": np.ascontiguousarray(np.asarray(inputs["L0_ffn_W_v"], np.float32).T),
        "lam": _fm(inputs["L0_attn_lam"]),
        "big": _fm(inputs["L0_attn_b_ig"]),
        "brg": _fm(inputs["L0_attn_b_rg"]),
        "mur": _fm(inputs["L0_ffn_mu_r"]),
        "muk": _fm(inputs["L0_ffn_mu_k"]),
        "h0": _fm(np.asarray(inputs["L0_attn_h0"], np.float32).reshape(DM)),
        "nw": _fm(inputs["L0_attn_norm_w"]),
    }
    in_maps = []
    for b in range(NCORE):
        xb = x[b]
        xt = np.ascontiguousarray(
            xb.reshape(NSPAN, SPAN, NCH, 128).transpose(0, 3, 2, 1)
        ).reshape(NSPAN, 128, NCH * SPAN)
        m = dict(shared)
        m["xtm"] = np.ascontiguousarray(xb)
        m["xt"] = xt
        in_maps.append(m)

    nc = _get_nc()
    res = run_bass_kernel_spmd(nc, in_maps, core_ids=list(range(NCORE)))
    out = np.stack([np.asarray(r["out"], np.float32) for r in res.results])
    return out


# version 24
# speedup vs baseline: 1.0637x; 1.0010x over previous
import math
from contextlib import ExitStack

import numpy as np
import ml_dtypes

import concourse.bass as bass
import concourse.mybir as mybir
from concourse.bass_utils import run_bass_kernel_spmd

F32 = mybir.dt.float32
BF16 = mybir.dt.bfloat16
I32 = mybir.dt.int32
AF = mybir.ActivationFunctionType
ALU = mybir.AluOpType

NCORE = 8
S_TOK = 4096
DM = 1024
NCH = 8
SPAN = 256
NSPAN = S_TOK // SPAN
TPS = SPAN // 128
NTILE = S_TOK // 128

LN2 = math.log(2.0)
SEED_SCALE = -LN2 / 256.0
SEED_BIAS = 127.0 * LN2 / 2.0
NEWTON_ITERS = 2

ENG_SEM = {"T": "t", "V": "v", "S": "s", "SY": "sy", "G": "g"}


class _AutoDrain:

    _WRAP = frozenset({"tensor_scalar", "scalar_tensor_tensor",
                       "tensor_tensor_scan", "tensor_tensor",
                       "tensor_tensor_reduce", "reduce_sum", "tensor_reduce",
                       "tensor_copy"})

    def __init__(self, e, junk):
        self._e = e
        self._junk = junk

    def __getattr__(self, n):
        f = getattr(self._e, n)
        if n in self._WRAP:
            def g(*a, **k):
                f(*a, **k)
                return self._e.drain()
            return g
        return f


class Prog:

    def __init__(self):
        self.ops = []
        self.cnt = {s: 0 for s in ("t", "v", "s", "sy", "g")}
        self.batch_end = {"sy": {}, "g": {}}
        self.junk = None

    def add(self, eng, fn, deps=(), inc=0):
        sem = ENG_SEM[eng]
        tok = None
        if inc:
            self.cnt[sem] += inc
            tok = (sem, self.cnt[sem])
        self.ops.append([eng, fn, tuple(d for d in deps if d is not None), inc, sem,
                         self.cnt[sem]])
        return tok

    def finalize(self):
        cur = {"sy": [], "g": []}

        def close(q):
            if cur[q]:
                endv = cur[q][-1]
                for v in cur[q]:
                    self.batch_end[q][v] = endv
                cur[q] = []

        for eng, fn, deps, inc, sem, val in self.ops:
            if eng not in ("SY", "G"):
                continue
            q = ENG_SEM[eng]
            if deps:
                close(q)
            if inc:
                cur[q].append(val)
        close("sy")
        close("g")

    def emit(self, eng_name, e, semh):
        own = ENG_SEM[eng_name]
        skip_own = own in ("t", "v", "s")
        is_q = own in ("sy", "g")
        wait_e = e
        if eng_name == "V":
            e = _AutoDrain(e, self.junk)
        waited = {}
        for name, fn, deps, inc, sem, val in self.ops:
            if name != eng_name:
                continue
            emitted = False
            rdeps = []
            for (s, v) in deps:
                if s in self.batch_end:
                    v = self.batch_end[s].get(v, v)
                if skip_own and s == own:
                    continue
                rdeps.append((s, v))
            for (s, v) in rdeps:
                if waited.get(s, -1) >= v:
                    continue
                wait_e.wait_ge(semh[s], v)
                waited[s] = v
                emitted = True
            if is_q and inc and deps and not emitted and rdeps:
                s, v = rdeps[0]
                wait_e.wait_ge(semh[s], v)
            inst = fn(e)
            if inc:
                inst.then_inc(semh[sem], inc)


def build_nc(nspan=NSPAN, debug=False):
    nc = bass.Bass("TRN2", target_bir_lowering=False,
                   detect_race_conditions=False)

    d_xtm = nc.dram_tensor("xtm", [S_TOK, DM], F32, kind="ExternalInput").ap()
    d_xt = nc.dram_tensor("xt", [NSPAN, 128, NCH * SPAN], F32,
                          kind="ExternalInput").ap()
    dw = {}
    for wname in ("wint", "wigt", "wrgt", "woutt", "wrt", "wkt", "wvt"):
        dw[wname] = nc.dram_tensor(wname, [DM, DM], F32, kind="ExternalInput").ap()
    dvec = {}
    for vname in ("lam", "big", "brg", "mur", "muk", "h0", "nw"):
        dvec[vname] = nc.dram_tensor(vname, [128, NCH], F32,
                                     kind="ExternalInput").ap()
    d_out = nc.dram_tensor("out", [S_TOK, DM], F32, kind="ExternalOutput").ap()

    ident_h = nc.inline_tensor(np.eye(128, dtype=np.float32), "identf32")
    ones_h = nc.inline_tensor(np.ones((128, 128), dtype=ml_dtypes.bfloat16),
                              "onesbf")

    P = Prog()

    with ExitStack() as ctx:
        E = ctx.enter_context
        w_sb = {n: E(nc.sbuf_tensor(f"sb_{n}", [128, NCH * DM], BF16))
                for n in ("wint", "wigt", "wrgt", "woutt", "wrt", "wkt", "wvt")}
        xt_bf = [E(nc.sbuf_tensor(f"xtbf{i}", [128, NCH * SPAN], BF16))
                 for i in range(2)]
        xtm = [E(nc.sbuf_tensor(f"xtm{i}", [128, DM], F32)) for i in range(2)]
        B_tr = E(nc.sbuf_tensor("Btr", [128, NCH * SPAN], BF16))
        B_ti = E(nc.sbuf_tensor("Bti", [128, NCH * SPAN], BF16))
        B_xp = E(nc.sbuf_tensor("Bxp", [128, NCH * SPAN], BF16))
        B_y = E(nc.sbuf_tensor("By", [128, NCH * SPAN], BF16))
        B_a = [E(nc.sbuf_tensor(f"Ba{i}", [128, NCH * SPAN], F32))
               for i in range(2)]
        B_o = E(nc.sbuf_tensor("Bo", [128, NCH * SPAN], F32))
        B_obf = E(nc.sbuf_tensor("Bobf", [128, NCH * SPAN], BF16))
        uT0 = E(nc.sbuf_tensor("uT0", [128, NCH * (SPAN + 1)], BF16))
        B_xrT = E(nc.sbuf_tensor("BxrT", [128, NCH * SPAN], BF16))
        B_xkT = E(nc.sbuf_tensor("BxkT", [128, NCH * SPAN], BF16))
        B_w = [E(nc.sbuf_tensor(f"Bw{i}", [128, DM], F32)) for i in range(2)]
        B_u = [E(nc.sbuf_tensor(f"Bu{i}", [128, DM], F32)) for i in range(2)]
        B_R = [E(nc.sbuf_tensor(f"BR{i}", [128, DM], BF16)) for i in range(2)]
        sv = {n: E(nc.sbuf_tensor(f"v_{n}", [128, NCH], F32))
              for n in ("lam", "big", "brg", "mur", "muk", "h0", "nw",
                        "q", "hcl", "hbig", "hbrg", "ts1", "ts2")}
        hcar = [E(nc.sbuf_tensor(f"hcar{i}", [128, NCH], F32)) for i in range(2)]
        t_ident = E(nc.sbuf_tensor("identsb", [128, 128], F32))
        t_ones = E(nc.sbuf_tensor("onessb", [128, 128], BF16))
        zf32 = E(nc.sbuf_tensor("zf32", [128, 2], F32))
        zi32 = E(nc.sbuf_tensor("zi32", [128, 2], I32))
        zcv = E(nc.sbuf_tensor("zcv", [128, 2], F32))
        zsd = E(nc.sbuf_tensor("zsd", [128, 2], F32))
        znt = E(nc.sbuf_tensor("znt", [128, 2], F32))
        rstd = E(nc.sbuf_tensor("rstd", [128, 2], F32))
        rstd1 = E(nc.sbuf_tensor("rstd1", [128, 1], F32))
        msum = E(nc.sbuf_tensor("msum", [128, 1], F32))
        mcol = E(nc.sbuf_tensor("mcol", [128, 1], F32))
        junk = E(nc.sbuf_tensor("junk", [128, 2], F32))
        P.junk = junk
        c_sqb = E(nc.sbuf_tensor("c_sqb", [128, 1], F32))
        c_sdb = E(nc.sbuf_tensor("c_sdb", [128, 1], F32))
        if debug:
            dbgw = E(nc.sbuf_tensor("dbgw", [128, DM], BF16))
            dbgu = E(nc.sbuf_tensor("dbgu", [128, DM], BF16))
            dbgr = E(nc.sbuf_tensor("dbgr", [128, 1], F32))

        kaT = [B_a[i][:].bitcast(BF16)[:, 0:NCH * SPAN] for i in range(2)]

        pg = [E(nc.psum_tensor(f"pg{i}", [128, 512], F32)) for i in range(2)]
        pa = [E(nc.psum_tensor(f"pa{i}", [128, 1024], F32)) for i in range(2)]
        pk = [E(nc.psum_tensor(f"pk{i}", [128, 512], F32)) for i in range(2)]

        semh = {s: E(nc.semaphore(s)) for s in ("t", "v", "s", "sy", "g")}
        blk = E(nc.Block())

        wdma = {}
        for n in ("wrgt", "wigt", "wint"):
            s3 = dw[n].rearrange("(dc p) h -> p dc h", p=128)
            d3 = w_sb[n][:].rearrange("p (dc h) -> p dc h", dc=NCH)
            wdma[n] = P.add("G", (lambda e, s=s3, d=d3: e.dma_start(out=d, in_=s)),
                            inc=16)
        xtdma = {}
        for k in range(2):
            xtdma[k] = P.add(
                "G", (lambda e, k=k: e.dma_start(out=xt_bf[k][:],
                                                 in_=d_xt[k, :, :])),
                inc=16)
        for n in ("woutt", "wrt", "wkt", "wvt"):
            s3 = dw[n].rearrange("(dc p) h -> p dc h", p=128)
            d3 = w_sb[n][:].rearrange("p (dc h) -> p dc h", dc=NCH)
            wdma[n] = P.add("G", (lambda e, s=s3, d=d3: e.dma_start(out=d, in_=s)),
                            deps=[("g", P.cnt["g"])], inc=16)

        vdma = {}
        for n in ("lam", "big", "brg", "mur", "muk", "h0", "nw"):
            vdma[n] = P.add(
                "SY", (lambda e, n=n: e.dma_start(out=sv[n][:], in_=dvec[n])),
                inc=16)
        vdma["ident"] = P.add(
            "SY", (lambda e: e.dma_start(out=t_ident[:], in_=ident_h.ap())), inc=16)
        vdma["ones"] = P.add(
            "SY", (lambda e: e.dma_start(out=t_ones[:], in_=ones_h.ap())), inc=16)
        xtmdma = {}
        for t in range(2):
            xtmdma[t] = P.add(
                "SY", (lambda e, t=t: e.dma_start(
                    out=xtm[t % 2][:], in_=d_xtm[t * 128:(t + 1) * 128, :])),
                inc=16)

        wout_scaled = []
        for hc in range(NCH):
            tok = P.add("V", (lambda e, hc=hc: e.tensor_scalar(
                out=w_sb["woutt"][:, hc * DM:(hc + 1) * DM],
                in0=w_sb["woutt"][:, hc * DM:(hc + 1) * DM],
                scalar1=sv["nw"][:, hc:hc + 1], scalar2=None, op0=ALU.mult)),
                deps=[wdma["woutt"], vdma["nw"]], inc=1)
            wout_scaled.append(tok)

        qtok = P.add("S", (lambda e: e.activation(
            out=sv["q"][:], in_=sv["lam"][:], func=AF.Exp, scale=-1.0)),
            deps=[vdma["lam"]], inc=1)
        q, t1, t2 = sv["q"], sv["ts1"], sv["ts2"]

        def _tay(e):
            e.tensor_scalar(out=t1[:], in0=q[:], scalar1=-0.25, scalar2=1.0 / 3.0,
                            op0=ALU.mult, op1=ALU.add)
            e.tensor_tensor(out=t2[:], in0=t1[:], in1=q[:], op=ALU.mult)
            e.tensor_scalar(out=t1[:], in0=t2[:], scalar1=-1.0, scalar2=0.5,
                            op0=ALU.mult, op1=ALU.add)
            e.tensor_tensor(out=t2[:], in0=t1[:], in1=q[:], op=ALU.mult)
            e.tensor_scalar(out=t1[:], in0=t2[:], scalar1=-1.0, scalar2=1.0,
                            op0=ALU.mult, op1=ALU.add)
            e.tensor_tensor(out=t2[:], in0=t1[:], in1=q[:], op=ALU.mult)
            e.tensor_scalar(out=t1[:], in0=t2[:], scalar1=-1.0, scalar2=1e-8,
                            op0=ALU.mult, op1=ALU.add)
            e.scalar_tensor_tensor(out=t2[:], in0=q[:], scalar=1e-8, in1=t1[:],
                                   op0=ALU.mult, op1=ALU.add)
            return e.tensor_scalar(out=sv["hcl"][:], in0=t2[:], scalar1=4.0,
                                   scalar2=None, op0=ALU.mult)

        hcl_tok = P.add("V", _tay, deps=[qtok], inc=1)
        hbig_tok = P.add("V", (lambda e: e.tensor_scalar(
            out=sv["hbig"][:], in0=sv["big"][:], scalar1=0.5, scalar2=None,
            op0=ALU.mult)), deps=[vdma["big"]], inc=1)
        hbrg_tok = P.add("V", (lambda e: e.tensor_scalar(
            out=sv["hbrg"][:], in0=sv["brg"][:], scalar1=0.5, scalar2=None,
            op0=ALU.mult)), deps=[vdma["brg"]], inc=1)
        sqb_tok = P.add("V", (lambda e: e.memset(c_sqb[:], 1.000001)), inc=1)
        sdb_tok = P.add("V", (lambda e: e.memset(c_sdb[:], SEED_BIAS)), inc=1)
        ut0z = P.add("V", (lambda e: e.memset(
            uT0[:].rearrange("p (hc c) -> p hc c", hc=NCH)[:, :, 0:1], 0.0)),
            inc=1)

        marks = {}

        def newton_rsqrt(zsrc_fn, ncols, zdeps):
            dst = rstd[:, 0:ncols] if ncols == 2 else rstd1[:, 0:1]
            zt = P.add("V", zsrc_fn, deps=zdeps, inc=1)

            def _prep(e):
                e.tensor_scalar(out=zi32[:, 0:ncols],
                                in0=zf32[:, 0:ncols].bitcast(I32), scalar1=16,
                                scalar2=None, op0=ALU.logical_shift_right)
                return e.tensor_copy(out=zcv[:, 0:ncols], in_=zi32[:, 0:ncols])

            cv = P.add("V", _prep, deps=[zt], inc=1)
            sd = P.add("S", (lambda e: e.activation(
                out=zsd[:, 0:ncols], in_=zcv[:, 0:ncols], func=AF.Exp,
                scale=SEED_SCALE, bias=c_sdb[:])), deps=[cv, sdb_tok], inc=1)

            def _newton(e):
                y = dst
                e.tensor_copy(out=y, in_=zsd[:, 0:ncols])
                for _ in range(NEWTON_ITERS):
                    e.tensor_tensor(out=znt[:, 0:ncols], in0=y, in1=y, op=ALU.mult)
                    e.tensor_tensor(out=znt[:, 0:ncols], in0=znt[:, 0:ncols],
                                    in1=zf32[:, 0:ncols], op=ALU.mult)
                    e.tensor_scalar(out=znt[:, 0:ncols], in0=znt[:, 0:ncols],
                                    scalar1=-0.5, scalar2=1.5, op0=ALU.mult,
                                    op1=ALU.add)
                    e.tensor_tensor(out=y, in0=y, in1=znt[:, 0:ncols], op=ALU.mult)
                return e.nop()

            return P.add("V", _newton, deps=[sd], inc=1)

        def gate_round(k, grot):
            proj = grot // NCH
            hc = grot % NCH
            rot = grot % 2
            pp = k % 2
            wn = ("wrgt", "wigt", "wint")[proj]
            deps = [xtdma[k], wdma[wn],
                    marks.get(("gev", k, grot - 2)),
                    hbrg_tok if proj == 0 else None,
                    hbig_tok if proj == 1 else None]
            if grot < 2:
                deps.append(marks.get(("gev", k - 1, 22 + grot)))

            def _gmm(e, wn=wn, hc=hc, rot=rot, k=k):
                for dc in range(NCH):
                    e.matmul(pg[rot][:, 0:SPAN],
                             w_sb[wn][:, dc * DM + hc * 128:
                                      dc * DM + hc * 128 + 128],
                             xt_bf[k % 2][:, dc * SPAN:(dc + 1) * SPAN],
                             start=(dc == 0), stop=(dc == NCH - 1))
                return e.drain()

            mmtok = P.add("T", _gmm, deps=deps, inc=1)
            marks[("gates", k)] = mmtok

            if proj == 0:
                bdeps = [mmtok, marks.get(("scan", k - 2)),
                         marks.get(("fmm", k - 2, 0)),
                         marks.get(("fmm", k - 2, 1))]

                def _ev(e, hc=hc, rot=rot, pp=pp):
                    return e.activation(
                        out=B_a[pp][:, hc * SPAN:(hc + 1) * SPAN],
                        in_=pg[rot][:, 0:SPAN], func=AF.Tanh, scale=0.5,
                        bias=sv["hbrg"][:, hc:hc + 1])
            elif proj == 1:
                bdeps = [mmtok, marks.get(("g1", k - 1))]

                def _ev(e, hc=hc, rot=rot):
                    return e.activation(
                        out=B_ti[:, hc * SPAN:(hc + 1) * SPAN],
                        in_=pg[rot][:, 0:SPAN], func=AF.Tanh, scale=0.5,
                        bias=sv["hbig"][:, hc:hc + 1])
            else:
                bdeps = [mmtok, marks.get(("scan", k - 1))]

                def _ev(e, hc=hc, rot=rot):
                    return e.activation(
                        out=B_xp[:, hc * SPAN:(hc + 1) * SPAN],
                        in_=pg[rot][:, 0:SPAN], func=AF.Copy)

            marks[("gev", k, grot)] = P.add("S", _ev, deps=bdeps, inc=1)

        def tail_and_prefetch(k):
            if k == 0:
                marks[("tail", 0)] = ut0z
            else:
                u3 = uT0[:].rearrange("p (hc c) -> p hc c", hc=NCH)
                marks[("tail", k)] = P.add(
                    "V", (lambda e, u3=u3: e.tensor_copy(
                        out=u3[:, :, 0:1], in_=u3[:, :, SPAN:SPAN + 1])),
                    deps=[marks.get(("uTev", k - 1, 0)),
                          marks.get(("uTev", k - 1, 1)),
                          marks.get(("mix", k - 1))],
                    inc=1)

        def gm_a(k):
            pp = k % 2
            ytoks = []
            for hc in range(NCH):
                ytoks.append(P.add("V", (lambda e, hc=hc, pp=pp: e.tensor_scalar(
                    out=B_y[:, hc * SPAN:(hc + 1) * SPAN],
                    in0=B_a[pp][:, hc * SPAN:(hc + 1) * SPAN],
                    scalar1=sv["hcl"][:, hc:hc + 1],
                    scalar2=sv["hcl"][:, hc:hc + 1],
                    op0=ALU.mult, op1=ALU.add)),
                    deps=[marks[("gev", k, hc)], hcl_tok], inc=1))
            g1 = P.add("V", (lambda e: e.scalar_tensor_tensor(
                out=B_xp[:], in0=B_ti[:], scalar=1.0, in1=B_xp[:],
                op0=ALU.add, op1=ALU.mult)),
                deps=[marks[("gev", k, 15)], marks[("gev", k, 23)]], inc=1)
            marks[("g1", k)] = g1
            atok = P.add("S", (lambda e, pp=pp: e.activation(
                out=B_a[pp][:], in_=B_y[:], func=AF.Exp)),
                deps=ytoks, inc=1)
            e2y = P.add("S", (lambda e: e.activation(
                out=B_o[:], in_=B_y[:], func=AF.Exp, scale=2.0)),
                deps=[marks.get(("mix", k - 1))], inc=1)
            sqt = P.add("S", (lambda e: e.activation(
                out=B_y[:], in_=B_o[:], func=AF.Sqrt, scale=-1.0,
                bias=c_sqb[:])), deps=[e2y, sqb_tok], inc=1)
            gt = P.add("V", (lambda e: e.scalar_tensor_tensor(
                out=B_xp[:], in0=B_xp[:], scalar=0.5, in1=B_y[:],
                op0=ALU.mult, op1=ALU.mult)), deps=[sqt, g1], inc=1)

            carry_src = sv["h0"] if k == 0 else hcar[(k - 1) % 2]
            scans = []
            for hc in range(NCH):
                scans.append(P.add(
                    "V", (lambda e, hc=hc, c=carry_src, pp=pp:
                          e.tensor_tensor_scan(
                        out=B_o[:, hc * SPAN:(hc + 1) * SPAN],
                        data0=B_a[pp][:, hc * SPAN:(hc + 1) * SPAN],
                        data1=B_xp[:, hc * SPAN:(hc + 1) * SPAN],
                        initial=c[:, hc:hc + 1], op0=ALU.mult, op1=ALU.add)),
                    deps=[atok, gt]
                    + ([vdma["h0"]] if k == 0 and hc == 0 else []),
                    inc=1))
            marks[("scan", k)] = scans[-1]
            o3 = B_o[:].rearrange("p (hc c) -> p hc c", hc=NCH)
            P.add("V", (lambda e, k=k: e.tensor_copy(
                out=hcar[k % 2][:].rearrange("p (h o) -> p h o", o=1),
                in_=o3[:, :, SPAN - 1:SPAN])), deps=[], inc=1)

        def gm_b(k):
            obf = P.add("V", (lambda e: e.tensor_copy(out=B_obf[:], in_=B_o[:])),
                        deps=[marks.get(("kaT", k - 1))], inc=1)
            marks[("obf", k)] = obf
            o2 = P.add("V", (lambda e: e.tensor_tensor(
                out=B_tr[:], in0=B_obf[:], in1=B_obf[:], op=ALU.mult)),
                deps=[marks.get(("ssum", k - 1)),
                      marks.get(("sq3red", k - 1, 1))], inc=1)
            marks[("o2", k)] = o2

        def ssum_phase(k):
            def _ssum(e):
                for tp in range(TPS):
                    for hc in range(NCH):
                        e.matmul(pk[0][:, 256 + tp * 128:256 + tp * 128 + 128],
                                 B_tr[:, hc * SPAN + tp * 128:
                                      hc * SPAN + tp * 128 + 128],
                                 t_ones[:], start=(hc == 0),
                                 stop=(hc == NCH - 1))
                    e.drain()
                return e.drain()

            ssum = P.add("T", _ssum,
                         deps=[marks[("o2", k)], vdma["ones"],
                               marks.get(("krelu", k - 1, 2)),
                               marks.get(("rmsz", k - 1))], inc=1)
            marks[("ssum", k)] = ssum
            rms_tok = newton_rsqrt(
                (lambda e: e.tensor_scalar(
                    out=zf32[:, 0:2],
                    in0=pk[0][:, 256:512].rearrange(
                        "p (t c) -> p t c", t=2)[:, :, 0:1],
                    scalar1=1.0 / DM,
                    scalar2=1e-6, op0=ALU.mult, op1=ALU.add)),
                2, [ssum])
            marks[("rmsz", k)] = rms_tok

        def wout_phase(k, tp):
            t = k * TPS + tp

            def _wout(e, tp=tp):
                for hc in range(NCH):
                    for nh in range(2):
                        e.matmul(pa[tp][:, nh * 512:(nh + 1) * 512],
                                 B_obf[:, hc * SPAN + tp * 128:
                                       hc * SPAN + tp * 128 + 128],
                                 w_sb["woutt"][:, hc * DM + nh * 512:
                                               hc * DM + nh * 512 + 512],
                                 start=(hc == 0), stop=(hc == NCH - 1))
                    if hc % 2 == 1:
                        e.drain()
                return e.drain()

            attn = P.add("T", _wout,
                         deps=[marks[("obf", k)],
                               marks.get(("f2", k - 1, tp))] +
                              (wout_scaled if k == 0 else []),
                         inc=1)
            marks[("attn", k, tp)] = attn

            wstt = P.add("V", (lambda e, tp=tp: e.scalar_tensor_tensor(
                out=B_w[tp][:], in0=pa[tp][:], scalar=rstd[:, tp:tp + 1],
                in1=xtm[tp][:], op0=ALU.mult, op1=ALU.add,
                accum_out=msum[:])),
                deps=[attn, marks[("rmsz", k)], xtmdma[t]], inc=1)
            marks[("wstt", t)] = wstt
            if debug and t == 30:
                P.add("V", (lambda e: e.tensor_copy(out=dbgw[:], in_=B_w[0][:])),
                      deps=[wstt], inc=1)
            if t + 2 < 2 * nspan:
                xtmdma[t + 2] = P.add("SY", (lambda e, t=t: e.dma_start(
                    out=xtm[t % 2][:],
                    in_=d_xtm[(t + 2) * 128:(t + 3) * 128, :])),
                    deps=[wstt], inc=16)
            P.add("V", (lambda e: e.tensor_scalar(
                out=mcol[:], in0=msum[:], scalar1=1.0 / DM, scalar2=None,
                op0=ALU.mult)), inc=1)
            xc = P.add("V", (lambda e, tp=tp: e.tensor_scalar(
                out=B_w[tp][:], in0=B_w[tp][:], scalar1=mcol[:],
                scalar2=None, op0=ALU.subtract)), inc=1)
            sqp = P.add("V", (lambda e, tp=tp: e.tensor_tensor(
                out=B_tr[:, 0:DM], in0=B_w[tp][:], in1=B_w[tp][:],
                op=ALU.mult)),
                deps=[xc, marks[("ssum", k)]],
                inc=1)
            sq2 = P.add("V", (lambda e: e.reduce_sum(
                out=msum[:], in_=B_tr[:, 0:DM], axis=mybir.AxisListType.X)),
                deps=[sqp], inc=1)
            marks[("sq2red", k, tp)] = sq2
            ln1 = newton_rsqrt(
                (lambda e: e.tensor_scalar(
                    out=zf32[:, 0:1], in0=msum[:], scalar1=1.0 / DM,
                    scalar2=1e-5, op0=ALU.mult, op1=ALU.add)),
                1, [sq2])
            u = P.add("V", (lambda e, tp=tp: e.tensor_scalar(
                out=B_u[tp][:], in0=B_w[tp][:], scalar1=rstd1[:, 0:1],
                scalar2=None, op0=ALU.mult)),
                deps=[ln1, marks.get(("store", t - 2))], inc=1)
            marks[("u", k, tp)] = u
            if debug and t == 30:
                P.add("V", (lambda e: e.tensor_copy(out=dbgu[:], in_=B_u[0][:])),
                      deps=[], inc=1)
                P.add("V", (lambda e: e.tensor_copy(out=dbgr[:], in_=rstd1[:])),
                      deps=[], inc=1)

        def tr_phase(k, tp):
            def _transp(e, tp=tp):
                for hc in range(NCH):
                    e.transpose(pa[tp][:, hc * 128:(hc + 1) * 128],
                                B_u[tp][:, hc * 128:(hc + 1) * 128],
                                t_ident[:])
                return e.drain()

            tr = P.add("T", _transp,
                       deps=[marks[("u", k, tp)], marks[("wstt", k * TPS + tp)],
                             vdma["ident"]], inc=1)
            u3 = uT0[:].rearrange("p (hc c) -> p hc c", hc=NCH)
            pa3 = pa[tp][:].rearrange("p (hc c) -> p hc c", hc=NCH)
            uTev = P.add("S", (lambda e, u3=u3, pa3=pa3, tp=tp: e.activation(
                out=u3[:, :, 1 + tp * 128:1 + tp * 128 + 128],
                in_=pa3[:, :, 0:128], func=AF.Copy)),
                deps=[tr, marks[("tail", k)], marks.get(("mix", k - 1))], inc=1)
            marks[("uTev", k, tp)] = uTev

        def mix_phase(k):
            u3 = uT0[:].rearrange("p (hc c) -> p hc c", hc=NCH)
            d1 = B_o[:].bitcast(BF16)[:, 0:NCH * SPAN]
            d13 = d1.rearrange("p (hc c) -> p hc c", hc=NCH)
            xr3 = B_xrT[:].rearrange("p (hc c) -> p hc c", hc=NCH)
            xk3 = B_xkT[:].rearrange("p (hc c) -> p hc c", hc=NCH)
            mixdeps = [marks[("uTev", k, 0)], marks[("uTev", k, 1)],
                       marks[("tail", k)], vdma["mur"], vdma["muk"],
                       marks.get(("rmm", k - 1, 0)), marks.get(("rmm", k - 1, 1)),
                       marks.get(("kT", k - 1, 3)),
                       marks[("sq2red", k, 1)]]

            def _mix(e, u3=u3, d13=d13, xr3=xr3, xk3=xk3):
                e.tensor_tensor(out=d13[:, :, :],
                                in0=u3[:, :, 1:SPAN + 1],
                                in1=u3[:, :, 0:SPAN], op=ALU.subtract)
                for hc in range(NCH):
                    e.scalar_tensor_tensor(out=xr3[:, hc, :], in0=d13[:, hc, :],
                                           scalar=sv["mur"][:, hc:hc + 1],
                                           in1=u3[:, hc, 0:SPAN],
                                           op0=ALU.mult, op1=ALU.add)
                for hc in range(NCH):
                    e.scalar_tensor_tensor(out=xk3[:, hc, :], in0=d13[:, hc, :],
                                           scalar=sv["muk"][:, hc:hc + 1],
                                           in1=u3[:, hc, 0:SPAN],
                                           op0=ALU.mult, op1=ALU.add)
                return e.nop()

            marks[("mix", k)] = P.add("V", _mix, deps=mixdeps, inc=1)

        def kt_phase(k, rc):
            def _kt(e, rc=rc):
                for oi in range(2):
                    oc = 2 * rc + oi
                    for cc in range(NCH):
                        e.matmul(pk[rc % 2][:, oi * SPAN:(oi + 1) * SPAN],
                                 w_sb["wkt"][:, cc * DM + oc * 128:
                                             cc * DM + oc * 128 + 128],
                                 B_xkT[:, cc * SPAN:(cc + 1) * SPAN],
                                 start=(cc == 0), stop=(cc == NCH - 1))
                    e.drain()
                return e.drain()

            ktdeps = [marks[("mix", k)], wdma["wkt"]]
            if rc >= 2:
                ktdeps.append(marks.get(("krelu", k, rc - 2)))
            else:
                ktdeps.append(marks.get(("krelu", k - 1, rc + 2)))
            if rc == 0:
                ktdeps.append(marks[("rmsz", k)])
            kt = P.add("T", _kt, deps=ktdeps, inc=1)
            marks[("kT", k, rc)] = kt
            krelu = P.add("S", (lambda e, rc=rc: e.activation(
                out=B_obf[:, rc * 512:(rc + 1) * 512],
                in_=pk[rc % 2][:, 0:512], func=AF.Relu)),
                deps=[kt, marks[("attn", k, 0)], marks[("attn", k, 1)],
                      marks[("g1", k)]], inc=1)
            marks[("krelu", k, rc)] = krelu

        def kat_phase(k):
            pp = k % 2
            katok = P.add("V", (lambda e, pp=pp: e.tensor_tensor(
                out=kaT[pp], in0=B_obf[:], in1=B_obf[:], op=ALU.mult)),
                deps=[marks[("krelu", k, rc_)] for rc_ in range(4)]
                + [marks[("scan", k)]], inc=1)
            marks[("kaT", k)] = katok

        def rmm_phase(k, tp):
            def _rmm(e, tp=tp):
                for cc in range(NCH):
                    for nh in range(2):
                        e.matmul(pa[tp][:, nh * 512:(nh + 1) * 512],
                                 B_xrT[:, cc * SPAN + tp * 128:
                                       cc * SPAN + tp * 128 + 128],
                                 w_sb["wrt"][:, cc * DM + nh * 512:
                                             cc * DM + nh * 512 + 512],
                                 start=(cc == 0), stop=(cc == NCH - 1))
                    if cc % 2 == 1:
                        e.drain()
                return e.drain()

            rmm = P.add("T", _rmm,
                        deps=[marks[("mix", k)], wdma["wrt"],
                              marks[("uTev", k, tp)]], inc=1)
            marks[("rmm", k, tp)] = rmm
            rev = P.add("S", (lambda e, tp=tp: e.activation(
                out=B_R[tp][:], in_=pa[tp][:], func=AF.Tanh, scale=0.5)),
                deps=[rmm, marks.get(("f2", k - 1, tp))], inc=1)
            marks[("rev", k, tp)] = rev

        def fmm_phase(k, tp):
            t = k * TPS + tp
            pp = k % 2

            def _fmm(e, tp=tp, pp=pp):
                for cc in range(NCH):
                    for nh in range(2):
                        e.matmul(pa[tp][:, nh * 512:(nh + 1) * 512],
                                 kaT[pp][:, cc * SPAN + tp * 128:
                                         cc * SPAN + tp * 128 + 128],
                                 w_sb["wvt"][:, cc * DM + nh * 512:
                                             cc * DM + nh * 512 + 512],
                                 start=(cc == 0), stop=(cc == NCH - 1))
                    if cc % 2 == 1:
                        e.drain()
                return e.drain()

            fmm = P.add("T", _fmm, deps=[marks[("kaT", k)],
                                         marks[("rev", k, tp)], wdma["wvt"]],
                        inc=1)
            marks[("fmm", k, tp)] = fmm

            f2 = P.add("V", (lambda e, tp=tp: e.scalar_tensor_tensor(
                out=B_w[tp][:], in0=B_R[tp][:], scalar=1.0, in1=pa[tp][:],
                op0=ALU.add, op1=ALU.mult)), deps=[fmm], inc=1)
            marks[("f2", k, tp)] = f2
            P.add("V", (lambda e, tp=tp: e.scalar_tensor_tensor(
                out=B_w[tp][:], in0=B_w[tp][:], scalar=0.5, in1=B_u[tp][:],
                op0=ALU.mult, op1=ALU.add, accum_out=msum[:])), inc=1)
            P.add("V", (lambda e: e.tensor_scalar(
                out=mcol[:], in0=msum[:], scalar1=1.0 / DM, scalar2=None,
                op0=ALU.mult)), inc=1)
            xc2 = P.add("V", (lambda e, tp=tp: e.tensor_scalar(
                out=B_w[tp][:], in0=B_w[tp][:], scalar1=mcol[:],
                scalar2=None, op0=ALU.subtract)), inc=1)
            sqp3 = P.add("V", (lambda e, tp=tp: e.tensor_tensor(
                out=B_tr[:, 0:DM], in0=B_w[tp][:], in1=B_w[tp][:],
                op=ALU.mult)),
                deps=[xc2, f2], inc=1)
            sq3 = P.add("V", (lambda e: e.reduce_sum(
                out=msum[:], in_=B_tr[:, 0:DM], axis=mybir.AxisListType.X)),
                deps=[sqp3], inc=1)
            marks[("sq3red", k, tp)] = sq3
            ln2 = newton_rsqrt(
                (lambda e: e.tensor_scalar(
                    out=zf32[:, 0:1], in0=msum[:], scalar1=1.0 / DM,
                    scalar2=1e-5, op0=ALU.mult, op1=ALU.add)),
                1, [sq3])
            outts = P.add("V", (lambda e, tp=tp: e.tensor_scalar(
                out=B_u[tp][:], in0=B_w[tp][:], scalar1=rstd1[:, 0:1],
                scalar2=None, op0=ALU.mult)), deps=[ln2], inc=1)
            marks[("store", t)] = P.add("SY", (lambda e, t=t, tp=tp: e.dma_start(
                out=d_out[t * 128:(t + 1) * 128, :], in_=B_u[tp][:])),
                deps=[outts], inc=16)

        for g in range(3 * NCH):
            gate_round(0, g)
        tail_and_prefetch(0)
        gm_a(0)
        gm_b(0)
        for k in range(nspan):
            ssum_phase(k)

            gq = (list(range(3 * NCH)) if k + 1 < nspan else [])

            def take(n, k=k, gq=gq):
                for _ in range(n):
                    if gq:
                        gate_round(k + 1, gq.pop(0))

            wout_phase(k, 0)
            take(6)
            tr_phase(k, 0)
            take(6)
            wout_phase(k, 1)
            take(6)
            tr_phase(k, 1)
            take(6)
            mix_phase(k)
            while gq:
                gate_round(k + 1, gq.pop(0))
            if k + 1 < nspan:
                tail_and_prefetch(k + 1)
                gm_a(k + 1)
            kt_phase(k, 0)
            kt_phase(k, 1)
            rmm_phase(k, 0)
            kt_phase(k, 2)
            kt_phase(k, 3)
            kat_phase(k)
            rmm_phase(k, 1)
            fmm_phase(k, 0)
            fmm_phase(k, 1)
            if k + 1 < nspan:
                gm_b(k + 1)
            if k + 2 < nspan:
                xtdma[k + 2] = P.add("G", (lambda e, k=k: e.dma_start(
                    out=xt_bf[k % 2][:], in_=d_xt[k + 2, :, :])),
                    deps=[marks[("gates", k + 1)]], inc=16)

        if debug:
            dbgmap = {"By": B_y, "Bxp": B_xp, "Ba0": B_a[0], "Ba1": B_a[1],
                      "Bo": B_o, "Bobf": B_obf, "Bti": B_ti, "Btr": B_tr,
                      "rstd": rstd, "rstd1": rstd1, "hcl": sv["hcl"],
                      "q": sv["q"], "zf32": zf32, "zcv": zcv, "zsd": zsd,
                      "hcar0": hcar[0], "uT0": uT0, "msum": msum,
                      "dw": dbgw, "du": dbgu, "dr": dbgr,
                      "Bw0": B_w[0], "Bu0": B_u[0]}
            enddeps = [("t", P.cnt["t"]), ("v", P.cnt["v"]), ("s", P.cnt["s"]),
                       ("g", P.cnt["g"])]
            for nm, buf in dbgmap.items():
                th = nc.dram_tensor(f"dbg_{nm}", list(np.asarray(buf[:].shape)),
                                    buf[:].dtype, kind="ExternalOutput").ap()
                P.add("SY", (lambda e, th=th, buf=buf: e.dma_start(
                    out=th, in_=buf[:])), deps=enddeps, inc=16)

        final_val = P.cnt["sy"]
        P.add("SY", (lambda e: e.nop()), deps=[("sy", final_val)], inc=0)

        P.finalize()

        @blk.tensor
        def _(e):
            P.emit("T", e, semh)

        @blk.vector
        def _(e):
            P.emit("V", e, semh)

        @blk.scalar
        def _(e):
            P.emit("S", e, semh)

        @blk.gpsimd
        def _(e):
            P.emit("G", e, semh)

        @blk.sync
        def _(e):
            P.emit("SY", e, semh)

    return nc


_NC_CACHE = None


def _get_nc():
    global _NC_CACHE
    if _NC_CACHE is None:
        _NC_CACHE = build_nc()
    return _NC_CACHE


def _fm(v):
    return np.ascontiguousarray(np.asarray(v, np.float32).reshape(NCH, 128).T)


def kernel(**inputs):
    x = np.asarray(inputs["L0_attn_x"], np.float32)
    shared = {
        "wint": np.ascontiguousarray(np.asarray(inputs["L0_attn_W_in"], np.float32).T),
        "wigt": np.ascontiguousarray(np.asarray(inputs["L0_attn_W_ig"], np.float32).T),
        "wrgt": np.ascontiguousarray(np.asarray(inputs["L0_attn_W_rg"], np.float32).T),
        "woutt": np.ascontiguousarray(np.asarray(inputs["L0_attn_W_out"], np.float32).T),
        "wrt": np.ascontiguousarray(np.asarray(inputs["L0_ffn_W_r"], np.float32).T),
        "wkt": np.ascontiguousarray(np.asarray(inputs["L0_ffn_W_k"], np.float32).T),
        "wvt": np.ascontiguousarray(np.asarray(inputs["L0_ffn_W_v"], np.float32).T),
        "lam": _fm(inputs["L0_attn_lam"]),
        "big": _fm(inputs["L0_attn_b_ig"]),
        "brg": _fm(inputs["L0_attn_b_rg"]),
        "mur": _fm(inputs["L0_ffn_mu_r"]),
        "muk": _fm(inputs["L0_ffn_mu_k"]),
        "h0": _fm(np.asarray(inputs["L0_attn_h0"], np.float32).reshape(DM)),
        "nw": _fm(inputs["L0_attn_norm_w"]),
    }
    in_maps = []
    for b in range(NCORE):
        xb = x[b]
        xt = np.ascontiguousarray(
            xb.reshape(NSPAN, SPAN, NCH, 128).transpose(0, 3, 2, 1)
        ).reshape(NSPAN, 128, NCH * SPAN)
        m = dict(shared)
        m["xtm"] = np.ascontiguousarray(xb)
        m["xt"] = xt
        in_maps.append(m)

    nc = _get_nc()
    res = run_bass_kernel_spmd(nc, in_maps, core_ids=list(range(NCORE)))
    out = np.stack([np.asarray(r["out"], np.float32) for r in res.results])
    return out


# version 25
# speedup vs baseline: 1.0715x; 1.0074x over previous
import math
from contextlib import ExitStack

import numpy as np
import ml_dtypes

import concourse.bass as bass
import concourse.mybir as mybir
from concourse.bass_utils import run_bass_kernel_spmd

F32 = mybir.dt.float32
BF16 = mybir.dt.bfloat16
I32 = mybir.dt.int32
AF = mybir.ActivationFunctionType
ALU = mybir.AluOpType

NCORE = 8
S_TOK = 4096
DM = 1024
NCH = 8
SPAN = 256
NSPAN = S_TOK // SPAN
TPS = SPAN // 128
NTILE = S_TOK // 128

LN2 = math.log(2.0)
SEED_SCALE = -LN2 / 256.0
SEED_BIAS = 127.0 * LN2 / 2.0
NEWTON_ITERS = 2

ENG_SEM = {"T": "t", "V": "v", "S": "s", "SY": "sy", "G": "g"}


class _AutoDrain:

    _WRAP = frozenset({"tensor_scalar", "scalar_tensor_tensor",
                       "tensor_tensor_scan", "tensor_tensor",
                       "tensor_tensor_reduce", "reduce_sum", "tensor_reduce",
                       "tensor_copy"})

    def __init__(self, e, junk):
        self._e = e
        self._junk = junk

    def __getattr__(self, n):
        f = getattr(self._e, n)
        if n in self._WRAP:
            def g(*a, **k):
                f(*a, **k)
                return self._e.drain()
            return g
        return f


class Prog:

    def __init__(self):
        self.ops = []
        self.cnt = {s: 0 for s in ("t", "v", "s", "sy", "g")}
        self.batch_end = {"sy": {}, "g": {}}
        self.junk = None

    def add(self, eng, fn, deps=(), inc=0):
        sem = ENG_SEM[eng]
        tok = None
        if inc:
            self.cnt[sem] += inc
            tok = (sem, self.cnt[sem])
        self.ops.append([eng, fn, tuple(d for d in deps if d is not None), inc, sem,
                         self.cnt[sem]])
        return tok

    def finalize(self):
        cur = {"sy": [], "g": []}

        def close(q):
            if cur[q]:
                endv = cur[q][-1]
                for v in cur[q]:
                    self.batch_end[q][v] = endv
                cur[q] = []

        for eng, fn, deps, inc, sem, val in self.ops:
            if eng not in ("SY", "G"):
                continue
            q = ENG_SEM[eng]
            if deps:
                close(q)
            if inc:
                cur[q].append(val)
        close("sy")
        close("g")

    def emit(self, eng_name, e, semh):
        own = ENG_SEM[eng_name]
        skip_own = own in ("t", "v", "s")
        is_q = own in ("sy", "g")
        wait_e = e
        if eng_name == "V":
            e = _AutoDrain(e, self.junk)
        waited = {}
        for name, fn, deps, inc, sem, val in self.ops:
            if name != eng_name:
                continue
            emitted = False
            rdeps = []
            for (s, v) in deps:
                if s in self.batch_end:
                    v = self.batch_end[s].get(v, v)
                if skip_own and s == own:
                    continue
                rdeps.append((s, v))
            for (s, v) in rdeps:
                if waited.get(s, -1) >= v:
                    continue
                wait_e.wait_ge(semh[s], v)
                waited[s] = v
                emitted = True
            if is_q and inc and deps and not emitted and rdeps:
                s, v = rdeps[0]
                wait_e.wait_ge(semh[s], v)
            inst = fn(e)
            if inc:
                inst.then_inc(semh[sem], inc)


def build_nc(nspan=NSPAN, debug=False):
    nc = bass.Bass("TRN2", target_bir_lowering=False,
                   detect_race_conditions=False)

    d_xtm = nc.dram_tensor("xtm", [S_TOK, DM], F32, kind="ExternalInput").ap()
    d_xt = nc.dram_tensor("xt", [NSPAN, 128, NCH * SPAN], F32,
                          kind="ExternalInput").ap()
    dw = {}
    for wname in ("wint", "wigt", "wrgt", "woutt", "wrt", "wkt", "wvt"):
        dw[wname] = nc.dram_tensor(wname, [DM, DM], F32, kind="ExternalInput").ap()
    dvec = {}
    for vname in ("lam", "big", "brg", "mur", "muk", "h0", "nw"):
        dvec[vname] = nc.dram_tensor(vname, [128, NCH], F32,
                                     kind="ExternalInput").ap()
    d_out = nc.dram_tensor("out", [S_TOK, DM], F32, kind="ExternalOutput").ap()

    ident_h = nc.inline_tensor(np.eye(128, dtype=np.float32), "identf32")
    ones_h = nc.inline_tensor(np.ones((128, 128), dtype=ml_dtypes.bfloat16),
                              "onesbf")

    P = Prog()

    with ExitStack() as ctx:
        E = ctx.enter_context
        w_sb = {n: E(nc.sbuf_tensor(f"sb_{n}", [128, NCH * DM], BF16))
                for n in ("wint", "wigt", "wrgt", "woutt", "wrt", "wkt", "wvt")}
        xt_bf = [E(nc.sbuf_tensor(f"xtbf{i}", [128, NCH * SPAN], BF16))
                 for i in range(2)]
        xtm = [E(nc.sbuf_tensor(f"xtm{i}", [128, DM], F32)) for i in range(2)]
        B_tr = E(nc.sbuf_tensor("Btr", [128, NCH * SPAN], BF16))
        B_ti = E(nc.sbuf_tensor("Bti", [128, NCH * SPAN], BF16))
        B_xp = E(nc.sbuf_tensor("Bxp", [128, NCH * SPAN], BF16))
        B_y = E(nc.sbuf_tensor("By", [128, NCH * SPAN], BF16))
        B_a = [E(nc.sbuf_tensor(f"Ba{i}", [128, NCH * SPAN], F32))
               for i in range(2)]
        B_o = E(nc.sbuf_tensor("Bo", [128, NCH * SPAN], F32))
        B_obf = E(nc.sbuf_tensor("Bobf", [128, NCH * SPAN], BF16))
        uT0 = E(nc.sbuf_tensor("uT0", [128, NCH * (SPAN + 1)], BF16))
        B_xrT = E(nc.sbuf_tensor("BxrT", [128, NCH * SPAN], BF16))
        B_xkT = E(nc.sbuf_tensor("BxkT", [128, NCH * SPAN], BF16))
        B_w = [E(nc.sbuf_tensor(f"Bw{i}", [128, DM], F32)) for i in range(2)]
        B_u = [E(nc.sbuf_tensor(f"Bu{i}", [128, DM], F32)) for i in range(2)]
        B_R = [E(nc.sbuf_tensor(f"BR{i}", [128, DM], BF16)) for i in range(2)]
        sv = {n: E(nc.sbuf_tensor(f"v_{n}", [128, NCH], F32))
              for n in ("lam", "big", "brg", "mur", "muk", "h0", "nw",
                        "q", "hcl", "hbig", "hbrg", "ts1", "ts2")}
        hcar = [E(nc.sbuf_tensor(f"hcar{i}", [128, NCH], F32)) for i in range(2)]
        t_ident = E(nc.sbuf_tensor("identsb", [128, 128], F32))
        t_ones = E(nc.sbuf_tensor("onessb", [128, 128], BF16))
        zf32 = E(nc.sbuf_tensor("zf32", [128, 2], F32))
        zi32 = E(nc.sbuf_tensor("zi32", [128, 2], I32))
        zcv = E(nc.sbuf_tensor("zcv", [128, 2], F32))
        zsd = E(nc.sbuf_tensor("zsd", [128, 2], F32))
        znt = E(nc.sbuf_tensor("znt", [128, 2], F32))
        rstd = E(nc.sbuf_tensor("rstd", [128, 2], F32))
        rstd1 = E(nc.sbuf_tensor("rstd1", [128, 1], F32))
        msum = E(nc.sbuf_tensor("msum", [128, 1], F32))
        mcol = E(nc.sbuf_tensor("mcol", [128, 1], F32))
        junk = E(nc.sbuf_tensor("junk", [128, 2], F32))
        P.junk = junk
        c_sqb = E(nc.sbuf_tensor("c_sqb", [128, 1], F32))
        c_sdb = E(nc.sbuf_tensor("c_sdb", [128, 1], F32))
        if debug:
            dbgw = E(nc.sbuf_tensor("dbgw", [128, DM], BF16))
            dbgu = E(nc.sbuf_tensor("dbgu", [128, DM], BF16))
            dbgr = E(nc.sbuf_tensor("dbgr", [128, 1], F32))

        kaT = [B_a[i][:].bitcast(BF16)[:, 0:NCH * SPAN] for i in range(2)]

        pg = [E(nc.psum_tensor(f"pg{i}", [128, 512], F32)) for i in range(2)]
        pa = [E(nc.psum_tensor(f"pa{i}", [128, 1024], F32)) for i in range(2)]
        pk = [E(nc.psum_tensor(f"pk{i}", [128, 512], F32)) for i in range(2)]

        semh = {s: E(nc.semaphore(s)) for s in ("t", "v", "s", "sy", "g")}
        blk = E(nc.Block())

        wdma = {}
        for n in ("wrgt", "wigt", "wint"):
            s3 = dw[n].rearrange("(dc p) h -> p dc h", p=128)
            d3 = w_sb[n][:].rearrange("p (dc h) -> p dc h", dc=NCH)
            wdma[n] = P.add("G", (lambda e, s=s3, d=d3: e.dma_start(out=d, in_=s)),
                            inc=16)
        xtdma = {}
        for k in range(2):
            xtdma[k] = P.add(
                "G", (lambda e, k=k: e.dma_start(out=xt_bf[k][:],
                                                 in_=d_xt[k, :, :])),
                inc=16)
        for n in ("woutt", "wrt", "wkt", "wvt"):
            s3 = dw[n].rearrange("(dc p) h -> p dc h", p=128)
            d3 = w_sb[n][:].rearrange("p (dc h) -> p dc h", dc=NCH)
            wdma[n] = P.add("G", (lambda e, s=s3, d=d3: e.dma_start(out=d, in_=s)),
                            deps=[("g", P.cnt["g"])], inc=16)

        vdma = {}
        for n in ("lam", "big", "brg", "mur", "muk", "h0", "nw"):
            vdma[n] = P.add(
                "SY", (lambda e, n=n: e.dma_start(out=sv[n][:], in_=dvec[n])),
                inc=16)
        vdma["ident"] = P.add(
            "SY", (lambda e: e.dma_start(out=t_ident[:], in_=ident_h.ap())), inc=16)
        vdma["ones"] = P.add(
            "SY", (lambda e: e.dma_start(out=t_ones[:], in_=ones_h.ap())), inc=16)
        xtmdma = {}
        for t in range(2):
            xtmdma[t] = P.add(
                "SY", (lambda e, t=t: e.dma_start(
                    out=xtm[t % 2][:], in_=d_xtm[t * 128:(t + 1) * 128, :])),
                inc=16)

        wout_scaled = []
        for hc in range(NCH):
            tok = P.add("V", (lambda e, hc=hc: e.tensor_scalar(
                out=w_sb["woutt"][:, hc * DM:(hc + 1) * DM],
                in0=w_sb["woutt"][:, hc * DM:(hc + 1) * DM],
                scalar1=sv["nw"][:, hc:hc + 1], scalar2=None, op0=ALU.mult)),
                deps=[wdma["woutt"], vdma["nw"]], inc=1)
            wout_scaled.append(tok)

        qtok = P.add("S", (lambda e: e.activation(
            out=sv["q"][:], in_=sv["lam"][:], func=AF.Exp, scale=-1.0)),
            deps=[vdma["lam"]], inc=1)
        q, t1, t2 = sv["q"], sv["ts1"], sv["ts2"]

        def _tay(e):
            e.tensor_scalar(out=t1[:], in0=q[:], scalar1=-0.25, scalar2=1.0 / 3.0,
                            op0=ALU.mult, op1=ALU.add)
            e.tensor_tensor(out=t2[:], in0=t1[:], in1=q[:], op=ALU.mult)
            e.tensor_scalar(out=t1[:], in0=t2[:], scalar1=-1.0, scalar2=0.5,
                            op0=ALU.mult, op1=ALU.add)
            e.tensor_tensor(out=t2[:], in0=t1[:], in1=q[:], op=ALU.mult)
            e.tensor_scalar(out=t1[:], in0=t2[:], scalar1=-1.0, scalar2=1.0,
                            op0=ALU.mult, op1=ALU.add)
            e.tensor_tensor(out=t2[:], in0=t1[:], in1=q[:], op=ALU.mult)
            e.tensor_scalar(out=t1[:], in0=t2[:], scalar1=-1.0, scalar2=1e-8,
                            op0=ALU.mult, op1=ALU.add)
            e.scalar_tensor_tensor(out=t2[:], in0=q[:], scalar=1e-8, in1=t1[:],
                                   op0=ALU.mult, op1=ALU.add)
            return e.tensor_scalar(out=sv["hcl"][:], in0=t2[:], scalar1=4.0,
                                   scalar2=None, op0=ALU.mult)

        hcl_tok = P.add("V", _tay, deps=[qtok], inc=1)
        hbig_tok = P.add("V", (lambda e: e.tensor_scalar(
            out=sv["hbig"][:], in0=sv["big"][:], scalar1=0.5, scalar2=None,
            op0=ALU.mult)), deps=[vdma["big"]], inc=1)
        hbrg_tok = P.add("V", (lambda e: e.tensor_scalar(
            out=sv["hbrg"][:], in0=sv["brg"][:], scalar1=0.5, scalar2=None,
            op0=ALU.mult)), deps=[vdma["brg"]], inc=1)
        sqb_tok = P.add("V", (lambda e: e.memset(c_sqb[:], 1.000001)), inc=1)
        sdb_tok = P.add("V", (lambda e: e.memset(c_sdb[:], SEED_BIAS)), inc=1)
        ut0z = P.add("V", (lambda e: e.memset(
            uT0[:].rearrange("p (hc c) -> p hc c", hc=NCH)[:, :, 0:1], 0.0)),
            inc=1)

        marks = {}

        def newton_rsqrt(zsrc_fn, ncols, zdeps):
            dst = rstd[:, 0:ncols] if ncols == 2 else rstd1[:, 0:1]
            zt = P.add("V", zsrc_fn, deps=zdeps, inc=1)

            def _prep(e):
                e.tensor_scalar(out=zi32[:, 0:ncols],
                                in0=zf32[:, 0:ncols].bitcast(I32), scalar1=16,
                                scalar2=None, op0=ALU.logical_shift_right)
                return e.tensor_copy(out=zcv[:, 0:ncols], in_=zi32[:, 0:ncols])

            cv = P.add("V", _prep, deps=[zt], inc=1)
            sd = P.add("S", (lambda e: e.activation(
                out=zsd[:, 0:ncols], in_=zcv[:, 0:ncols], func=AF.Exp,
                scale=SEED_SCALE, bias=c_sdb[:])), deps=[cv, sdb_tok], inc=1)

            def _newton(e):
                y = dst
                prev = zsd[:, 0:ncols]
                for _ in range(NEWTON_ITERS):
                    e.tensor_tensor(out=znt[:, 0:ncols], in0=prev, in1=prev,
                                    op=ALU.mult)
                    e.tensor_tensor(out=znt[:, 0:ncols], in0=znt[:, 0:ncols],
                                    in1=zf32[:, 0:ncols], op=ALU.mult)
                    e.tensor_scalar(out=znt[:, 0:ncols], in0=znt[:, 0:ncols],
                                    scalar1=-0.5, scalar2=1.5, op0=ALU.mult,
                                    op1=ALU.add)
                    e.tensor_tensor(out=y, in0=prev, in1=znt[:, 0:ncols],
                                    op=ALU.mult)
                    prev = y
                return e.nop()

            return P.add("V", _newton, deps=[sd], inc=1)

        def gate_round(k, grot):
            proj = grot // NCH
            hc = grot % NCH
            rot = grot % 2
            pp = k % 2
            wn = ("wrgt", "wigt", "wint")[proj]
            deps = [xtdma[k], wdma[wn],
                    marks.get(("gev", k, grot - 2)),
                    hbrg_tok if proj == 0 else None,
                    hbig_tok if proj == 1 else None]
            if grot < 2:
                deps.append(marks.get(("gev", k - 1, 22 + grot)))

            def _gmm(e, wn=wn, hc=hc, rot=rot, k=k):
                for dc in range(NCH):
                    e.matmul(pg[rot][:, 0:SPAN],
                             w_sb[wn][:, dc * DM + hc * 128:
                                      dc * DM + hc * 128 + 128],
                             xt_bf[k % 2][:, dc * SPAN:(dc + 1) * SPAN],
                             start=(dc == 0), stop=(dc == NCH - 1))
                return e.drain()

            mmtok = P.add("T", _gmm, deps=deps, inc=1)
            marks[("gates", k)] = mmtok

            if proj == 0:
                bdeps = [mmtok, marks.get(("scan", k - 2)),
                         marks.get(("fmm", k - 2, 0)),
                         marks.get(("fmm", k - 2, 1))]

                def _ev(e, hc=hc, rot=rot, pp=pp):
                    return e.activation(
                        out=B_a[pp][:, hc * SPAN:(hc + 1) * SPAN],
                        in_=pg[rot][:, 0:SPAN], func=AF.Tanh, scale=0.5,
                        bias=sv["hbrg"][:, hc:hc + 1])
            elif proj == 1:
                bdeps = [mmtok, marks.get(("g1", k - 1))]

                def _ev(e, hc=hc, rot=rot):
                    return e.activation(
                        out=B_ti[:, hc * SPAN:(hc + 1) * SPAN],
                        in_=pg[rot][:, 0:SPAN], func=AF.Tanh, scale=0.5,
                        bias=sv["hbig"][:, hc:hc + 1])
            else:
                bdeps = [mmtok, marks.get(("scan", k - 1))]

                def _ev(e, hc=hc, rot=rot):
                    return e.activation(
                        out=B_xp[:, hc * SPAN:(hc + 1) * SPAN],
                        in_=pg[rot][:, 0:SPAN], func=AF.Copy)

            marks[("gev", k, grot)] = P.add("S", _ev, deps=bdeps, inc=1)

        def tail_and_prefetch(k):
            if k == 0:
                marks[("tail", 0)] = ut0z
            else:
                u3 = uT0[:].rearrange("p (hc c) -> p hc c", hc=NCH)
                marks[("tail", k)] = P.add(
                    "V", (lambda e, u3=u3: e.tensor_copy(
                        out=u3[:, :, 0:1], in_=u3[:, :, SPAN:SPAN + 1])),
                    deps=[marks.get(("uTev", k - 1, 0)),
                          marks.get(("uTev", k - 1, 1)),
                          marks.get(("mix", k - 1))],
                    inc=1)

        def gm_a(k):
            pp = k % 2
            ytoks = []
            for hc in range(NCH):
                ytoks.append(P.add("V", (lambda e, hc=hc, pp=pp: e.tensor_scalar(
                    out=B_y[:, hc * SPAN:(hc + 1) * SPAN],
                    in0=B_a[pp][:, hc * SPAN:(hc + 1) * SPAN],
                    scalar1=sv["hcl"][:, hc:hc + 1],
                    scalar2=sv["hcl"][:, hc:hc + 1],
                    op0=ALU.mult, op1=ALU.add)),
                    deps=[marks[("gev", k, hc)], hcl_tok], inc=1))
            g1 = P.add("V", (lambda e: e.scalar_tensor_tensor(
                out=B_xp[:], in0=B_ti[:], scalar=1.0, in1=B_xp[:],
                op0=ALU.add, op1=ALU.mult)),
                deps=[marks[("gev", k, 15)], marks[("gev", k, 23)]], inc=1)
            marks[("g1", k)] = g1
            atok = P.add("S", (lambda e, pp=pp: e.activation(
                out=B_a[pp][:], in_=B_y[:], func=AF.Exp)),
                deps=ytoks, inc=1)
            e2y = P.add("S", (lambda e: e.activation(
                out=B_o[:], in_=B_y[:], func=AF.Exp, scale=2.0)),
                deps=[marks.get(("mix", k - 1))], inc=1)
            sqt = P.add("S", (lambda e: e.activation(
                out=B_y[:], in_=B_o[:], func=AF.Sqrt, scale=-1.0,
                bias=c_sqb[:])), deps=[e2y, sqb_tok], inc=1)
            gt = P.add("V", (lambda e: e.scalar_tensor_tensor(
                out=B_xp[:], in0=B_xp[:], scalar=0.5, in1=B_y[:],
                op0=ALU.mult, op1=ALU.mult)), deps=[sqt, g1], inc=1)

            carry_src = sv["h0"] if k == 0 else hcar[(k - 1) % 2]
            scans = []
            for hc in range(NCH):
                scans.append(P.add(
                    "V", (lambda e, hc=hc, c=carry_src, pp=pp:
                          e.tensor_tensor_scan(
                        out=B_o[:, hc * SPAN:(hc + 1) * SPAN],
                        data0=B_a[pp][:, hc * SPAN:(hc + 1) * SPAN],
                        data1=B_xp[:, hc * SPAN:(hc + 1) * SPAN],
                        initial=c[:, hc:hc + 1], op0=ALU.mult, op1=ALU.add)),
                    deps=[atok, gt]
                    + ([vdma["h0"]] if k == 0 and hc == 0 else []),
                    inc=1))
            marks[("scan", k)] = scans[-1]
            o3 = B_o[:].rearrange("p (hc c) -> p hc c", hc=NCH)
            P.add("V", (lambda e, k=k: e.tensor_copy(
                out=hcar[k % 2][:].rearrange("p (h o) -> p h o", o=1),
                in_=o3[:, :, SPAN - 1:SPAN])), deps=[], inc=1)

        def gm_b(k):
            obf = P.add("V", (lambda e: e.tensor_copy(out=B_obf[:], in_=B_o[:])),
                        deps=[marks.get(("kaT", k - 1))], inc=1)
            marks[("obf", k)] = obf
            o2 = P.add("V", (lambda e: e.tensor_tensor(
                out=B_tr[:], in0=B_obf[:], in1=B_obf[:], op=ALU.mult)),
                deps=[marks.get(("ssum", k - 1)),
                      marks.get(("sq3red", k - 1, 1))], inc=1)
            marks[("o2", k)] = o2

        def ssum_phase(k):
            def _ssum(e):
                for tp in range(TPS):
                    for hc in range(NCH):
                        e.matmul(pk[0][:, 256 + tp * 128:256 + tp * 128 + 128],
                                 B_tr[:, hc * SPAN + tp * 128:
                                      hc * SPAN + tp * 128 + 128],
                                 t_ones[:], start=(hc == 0),
                                 stop=(hc == NCH - 1))
                    e.drain()
                return e.drain()

            ssum = P.add("T", _ssum,
                         deps=[marks[("o2", k)], vdma["ones"],
                               marks.get(("krelu", k - 1, 2)),
                               marks.get(("rmsz", k - 1))], inc=1)
            marks[("ssum", k)] = ssum
            rms_tok = newton_rsqrt(
                (lambda e: e.tensor_scalar(
                    out=zf32[:, 0:2],
                    in0=pk[0][:, 256:512].rearrange(
                        "p (t c) -> p t c", t=2)[:, :, 0:1],
                    scalar1=1.0 / DM,
                    scalar2=1e-6, op0=ALU.mult, op1=ALU.add)),
                2, [ssum])
            marks[("rmsz", k)] = rms_tok

        def wout_phase(k, tp):
            t = k * TPS + tp

            def _wout(e, tp=tp):
                for hc in range(NCH):
                    for nh in range(2):
                        e.matmul(pa[tp][:, nh * 512:(nh + 1) * 512],
                                 B_obf[:, hc * SPAN + tp * 128:
                                       hc * SPAN + tp * 128 + 128],
                                 w_sb["woutt"][:, hc * DM + nh * 512:
                                               hc * DM + nh * 512 + 512],
                                 start=(hc == 0), stop=(hc == NCH - 1))
                    if hc % 2 == 1:
                        e.drain()
                return e.drain()

            attn = P.add("T", _wout,
                         deps=[marks[("obf", k)],
                               marks.get(("f2", k - 1, tp))] +
                              (wout_scaled if k == 0 else []),
                         inc=1)
            marks[("attn", k, tp)] = attn

            wstt = P.add("V", (lambda e, tp=tp: e.scalar_tensor_tensor(
                out=B_w[tp][:], in0=pa[tp][:], scalar=rstd[:, tp:tp + 1],
                in1=xtm[tp][:], op0=ALU.mult, op1=ALU.add,
                accum_out=msum[:])),
                deps=[attn, marks[("rmsz", k)], xtmdma[t]], inc=1)
            marks[("wstt", t)] = wstt
            if debug and t == 30:
                P.add("V", (lambda e: e.tensor_copy(out=dbgw[:], in_=B_w[0][:])),
                      deps=[wstt], inc=1)
            if t + 2 < 2 * nspan:
                xtmdma[t + 2] = P.add("SY", (lambda e, t=t: e.dma_start(
                    out=xtm[t % 2][:],
                    in_=d_xtm[(t + 2) * 128:(t + 3) * 128, :])),
                    deps=[wstt], inc=16)
            P.add("V", (lambda e: e.tensor_scalar(
                out=mcol[:], in0=msum[:], scalar1=1.0 / DM, scalar2=None,
                op0=ALU.mult)), inc=1)
            xc = P.add("V", (lambda e, tp=tp: e.tensor_scalar(
                out=B_w[tp][:], in0=B_w[tp][:], scalar1=mcol[:],
                scalar2=None, op0=ALU.subtract)), inc=1)
            sqp = P.add("V", (lambda e, tp=tp: e.tensor_tensor(
                out=B_tr[:, 0:DM], in0=B_w[tp][:], in1=B_w[tp][:],
                op=ALU.mult)),
                deps=[xc, marks[("ssum", k)]],
                inc=1)
            sq2 = P.add("V", (lambda e: e.reduce_sum(
                out=msum[:], in_=B_tr[:, 0:DM], axis=mybir.AxisListType.X)),
                deps=[sqp], inc=1)
            marks[("sq2red", k, tp)] = sq2
            ln1 = newton_rsqrt(
                (lambda e: e.tensor_scalar(
                    out=zf32[:, 0:1], in0=msum[:], scalar1=1.0 / DM,
                    scalar2=1e-5, op0=ALU.mult, op1=ALU.add)),
                1, [sq2])
            u = P.add("V", (lambda e, tp=tp: e.tensor_scalar(
                out=B_u[tp][:], in0=B_w[tp][:], scalar1=rstd1[:, 0:1],
                scalar2=None, op0=ALU.mult)),
                deps=[ln1, marks.get(("store", t - 2))], inc=1)
            marks[("u", k, tp)] = u
            if debug and t == 30:
                P.add("V", (lambda e: e.tensor_copy(out=dbgu[:], in_=B_u[0][:])),
                      deps=[], inc=1)
                P.add("V", (lambda e: e.tensor_copy(out=dbgr[:], in_=rstd1[:])),
                      deps=[], inc=1)

        def tr_phase(k, tp):
            def _transp(e, tp=tp):
                for hc in range(NCH):
                    e.transpose(pa[tp][:, hc * 128:(hc + 1) * 128],
                                B_u[tp][:, hc * 128:(hc + 1) * 128],
                                t_ident[:])
                return e.drain()

            tr = P.add("T", _transp,
                       deps=[marks[("u", k, tp)], marks[("wstt", k * TPS + tp)],
                             vdma["ident"]], inc=1)
            u3 = uT0[:].rearrange("p (hc c) -> p hc c", hc=NCH)
            pa3 = pa[tp][:].rearrange("p (hc c) -> p hc c", hc=NCH)
            uTev = P.add("S", (lambda e, u3=u3, pa3=pa3, tp=tp: e.activation(
                out=u3[:, :, 1 + tp * 128:1 + tp * 128 + 128],
                in_=pa3[:, :, 0:128], func=AF.Copy)),
                deps=[tr, marks[("tail", k)], marks.get(("mix", k - 1))], inc=1)
            marks[("uTev", k, tp)] = uTev

        def mix_phase(k):
            u3 = uT0[:].rearrange("p (hc c) -> p hc c", hc=NCH)
            d1 = B_o[:].bitcast(BF16)[:, 0:NCH * SPAN]
            d13 = d1.rearrange("p (hc c) -> p hc c", hc=NCH)
            xr3 = B_xrT[:].rearrange("p (hc c) -> p hc c", hc=NCH)
            xk3 = B_xkT[:].rearrange("p (hc c) -> p hc c", hc=NCH)
            mixdeps = [marks[("uTev", k, 0)], marks[("uTev", k, 1)],
                       marks[("tail", k)], vdma["mur"], vdma["muk"],
                       marks.get(("rmm", k - 1, 0)), marks.get(("rmm", k - 1, 1)),
                       marks.get(("kT", k - 1, 3)),
                       marks[("sq2red", k, 1)]]

            def _mix(e, u3=u3, d13=d13, xr3=xr3, xk3=xk3):
                e.tensor_tensor(out=d13[:, :, :],
                                in0=u3[:, :, 1:SPAN + 1],
                                in1=u3[:, :, 0:SPAN], op=ALU.subtract)
                for hc in range(NCH):
                    e.scalar_tensor_tensor(out=xr3[:, hc, :], in0=d13[:, hc, :],
                                           scalar=sv["mur"][:, hc:hc + 1],
                                           in1=u3[:, hc, 0:SPAN],
                                           op0=ALU.mult, op1=ALU.add)
                for hc in range(NCH):
                    e.scalar_tensor_tensor(out=xk3[:, hc, :], in0=d13[:, hc, :],
                                           scalar=sv["muk"][:, hc:hc + 1],
                                           in1=u3[:, hc, 0:SPAN],
                                           op0=ALU.mult, op1=ALU.add)
                return e.nop()

            marks[("mix", k)] = P.add("V", _mix, deps=mixdeps, inc=1)

        def kt_phase(k, rc):
            def _kt(e, rc=rc):
                for oi in range(2):
                    oc = 2 * rc + oi
                    for cc in range(NCH):
                        e.matmul(pk[rc % 2][:, oi * SPAN:(oi + 1) * SPAN],
                                 w_sb["wkt"][:, cc * DM + oc * 128:
                                             cc * DM + oc * 128 + 128],
                                 B_xkT[:, cc * SPAN:(cc + 1) * SPAN],
                                 start=(cc == 0), stop=(cc == NCH - 1))
                    e.drain()
                return e.drain()

            ktdeps = [marks[("mix", k)], wdma["wkt"]]
            if rc >= 2:
                ktdeps.append(marks.get(("krelu", k, rc - 2)))
            else:
                ktdeps.append(marks.get(("krelu", k - 1, rc + 2)))
            if rc == 0:
                ktdeps.append(marks[("rmsz", k)])
            kt = P.add("T", _kt, deps=ktdeps, inc=1)
            marks[("kT", k, rc)] = kt
            krelu = P.add("S", (lambda e, rc=rc: e.activation(
                out=B_obf[:, rc * 512:(rc + 1) * 512],
                in_=pk[rc % 2][:, 0:512], func=AF.Relu)),
                deps=[kt, marks[("attn", k, 0)], marks[("attn", k, 1)],
                      marks[("g1", k)]], inc=1)
            marks[("krelu", k, rc)] = krelu

        def kat_phase(k):
            pp = k % 2
            katok = P.add("V", (lambda e, pp=pp: e.tensor_tensor(
                out=kaT[pp], in0=B_obf[:], in1=B_obf[:], op=ALU.mult)),
                deps=[marks[("krelu", k, rc_)] for rc_ in range(4)]
                + [marks[("scan", k)]], inc=1)
            marks[("kaT", k)] = katok

        def rmm_phase(k, tp):
            def _rmm(e, tp=tp):
                for cc in range(NCH):
                    for nh in range(2):
                        e.matmul(pa[tp][:, nh * 512:(nh + 1) * 512],
                                 B_xrT[:, cc * SPAN + tp * 128:
                                       cc * SPAN + tp * 128 + 128],
                                 w_sb["wrt"][:, cc * DM + nh * 512:
                                             cc * DM + nh * 512 + 512],
                                 start=(cc == 0), stop=(cc == NCH - 1))
                    if cc % 2 == 1:
                        e.drain()
                return e.drain()

            rmm = P.add("T", _rmm,
                        deps=[marks[("mix", k)], wdma["wrt"],
                              marks[("uTev", k, tp)]], inc=1)
            marks[("rmm", k, tp)] = rmm
            rev = P.add("S", (lambda e, tp=tp: e.activation(
                out=B_R[tp][:], in_=pa[tp][:], func=AF.Tanh, scale=0.5)),
                deps=[rmm, marks.get(("f2", k - 1, tp))], inc=1)
            marks[("rev", k, tp)] = rev

        def fmm_phase(k, tp):
            t = k * TPS + tp
            pp = k % 2

            def _fmm(e, tp=tp, pp=pp):
                for cc in range(NCH):
                    for nh in range(2):
                        e.matmul(pa[tp][:, nh * 512:(nh + 1) * 512],
                                 kaT[pp][:, cc * SPAN + tp * 128:
                                         cc * SPAN + tp * 128 + 128],
                                 w_sb["wvt"][:, cc * DM + nh * 512:
                                             cc * DM + nh * 512 + 512],
                                 start=(cc == 0), stop=(cc == NCH - 1))
                    if cc % 2 == 1:
                        e.drain()
                return e.drain()

            fmm = P.add("T", _fmm, deps=[marks[("kaT", k)],
                                         marks[("rev", k, tp)], wdma["wvt"]],
                        inc=1)
            marks[("fmm", k, tp)] = fmm

            f2 = P.add("V", (lambda e, tp=tp: e.scalar_tensor_tensor(
                out=B_w[tp][:], in0=B_R[tp][:], scalar=1.0, in1=pa[tp][:],
                op0=ALU.add, op1=ALU.mult)), deps=[fmm], inc=1)
            marks[("f2", k, tp)] = f2
            P.add("V", (lambda e, tp=tp: e.scalar_tensor_tensor(
                out=B_w[tp][:], in0=B_w[tp][:], scalar=0.5, in1=B_u[tp][:],
                op0=ALU.mult, op1=ALU.add, accum_out=msum[:])), inc=1)
            P.add("V", (lambda e: e.tensor_scalar(
                out=mcol[:], in0=msum[:], scalar1=1.0 / DM, scalar2=None,
                op0=ALU.mult)), inc=1)
            xc2 = P.add("V", (lambda e, tp=tp: e.tensor_scalar(
                out=B_w[tp][:], in0=B_w[tp][:], scalar1=mcol[:],
                scalar2=None, op0=ALU.subtract)), inc=1)
            sqp3 = P.add("V", (lambda e, tp=tp: e.tensor_tensor(
                out=B_tr[:, 0:DM], in0=B_w[tp][:], in1=B_w[tp][:],
                op=ALU.mult)),
                deps=[xc2, f2], inc=1)
            sq3 = P.add("V", (lambda e: e.reduce_sum(
                out=msum[:], in_=B_tr[:, 0:DM], axis=mybir.AxisListType.X)),
                deps=[sqp3], inc=1)
            marks[("sq3red", k, tp)] = sq3
            ln2 = newton_rsqrt(
                (lambda e: e.tensor_scalar(
                    out=zf32[:, 0:1], in0=msum[:], scalar1=1.0 / DM,
                    scalar2=1e-5, op0=ALU.mult, op1=ALU.add)),
                1, [sq3])
            outts = P.add("V", (lambda e, tp=tp: e.tensor_scalar(
                out=B_u[tp][:], in0=B_w[tp][:], scalar1=rstd1[:, 0:1],
                scalar2=None, op0=ALU.mult)), deps=[ln2], inc=1)
            marks[("store", t)] = P.add("SY", (lambda e, t=t, tp=tp: e.dma_start(
                out=d_out[t * 128:(t + 1) * 128, :], in_=B_u[tp][:])),
                deps=[outts], inc=16)

        for g in range(3 * NCH):
            gate_round(0, g)
        tail_and_prefetch(0)
        gm_a(0)
        gm_b(0)
        for k in range(nspan):
            ssum_phase(k)

            gq = (list(range(3 * NCH)) if k + 1 < nspan else [])

            def take(n, k=k, gq=gq):
                for _ in range(n):
                    if gq:
                        gate_round(k + 1, gq.pop(0))

            wout_phase(k, 0)
            take(6)
            tr_phase(k, 0)
            take(6)
            wout_phase(k, 1)
            take(6)
            tr_phase(k, 1)
            take(6)
            mix_phase(k)
            while gq:
                gate_round(k + 1, gq.pop(0))
            if k + 1 < nspan:
                tail_and_prefetch(k + 1)
                gm_a(k + 1)
            kt_phase(k, 0)
            kt_phase(k, 1)
            rmm_phase(k, 0)
            kt_phase(k, 2)
            kt_phase(k, 3)
            kat_phase(k)
            rmm_phase(k, 1)
            fmm_phase(k, 0)
            fmm_phase(k, 1)
            if k + 1 < nspan:
                gm_b(k + 1)
            if k + 2 < nspan:
                xtdma[k + 2] = P.add("G", (lambda e, k=k: e.dma_start(
                    out=xt_bf[k % 2][:], in_=d_xt[k + 2, :, :])),
                    deps=[marks[("gates", k + 1)]], inc=16)

        if debug:
            dbgmap = {"By": B_y, "Bxp": B_xp, "Ba0": B_a[0], "Ba1": B_a[1],
                      "Bo": B_o, "Bobf": B_obf, "Bti": B_ti, "Btr": B_tr,
                      "rstd": rstd, "rstd1": rstd1, "hcl": sv["hcl"],
                      "q": sv["q"], "zf32": zf32, "zcv": zcv, "zsd": zsd,
                      "hcar0": hcar[0], "uT0": uT0, "msum": msum,
                      "dw": dbgw, "du": dbgu, "dr": dbgr,
                      "Bw0": B_w[0], "Bu0": B_u[0]}
            enddeps = [("t", P.cnt["t"]), ("v", P.cnt["v"]), ("s", P.cnt["s"]),
                       ("g", P.cnt["g"])]
            for nm, buf in dbgmap.items():
                th = nc.dram_tensor(f"dbg_{nm}", list(np.asarray(buf[:].shape)),
                                    buf[:].dtype, kind="ExternalOutput").ap()
                P.add("SY", (lambda e, th=th, buf=buf: e.dma_start(
                    out=th, in_=buf[:])), deps=enddeps, inc=16)

        final_val = P.cnt["sy"]
        P.add("SY", (lambda e: e.nop()), deps=[("sy", final_val)], inc=0)

        P.finalize()

        @blk.tensor
        def _(e):
            P.emit("T", e, semh)

        @blk.vector
        def _(e):
            P.emit("V", e, semh)

        @blk.scalar
        def _(e):
            P.emit("S", e, semh)

        @blk.gpsimd
        def _(e):
            P.emit("G", e, semh)

        @blk.sync
        def _(e):
            P.emit("SY", e, semh)

    return nc


_NC_CACHE = None


def _get_nc():
    global _NC_CACHE
    if _NC_CACHE is None:
        _NC_CACHE = build_nc()
    return _NC_CACHE


def _fm(v):
    return np.ascontiguousarray(np.asarray(v, np.float32).reshape(NCH, 128).T)


def kernel(**inputs):
    x = np.asarray(inputs["L0_attn_x"], np.float32)
    shared = {
        "wint": np.ascontiguousarray(np.asarray(inputs["L0_attn_W_in"], np.float32).T),
        "wigt": np.ascontiguousarray(np.asarray(inputs["L0_attn_W_ig"], np.float32).T),
        "wrgt": np.ascontiguousarray(np.asarray(inputs["L0_attn_W_rg"], np.float32).T),
        "woutt": np.ascontiguousarray(np.asarray(inputs["L0_attn_W_out"], np.float32).T),
        "wrt": np.ascontiguousarray(np.asarray(inputs["L0_ffn_W_r"], np.float32).T),
        "wkt": np.ascontiguousarray(np.asarray(inputs["L0_ffn_W_k"], np.float32).T),
        "wvt": np.ascontiguousarray(np.asarray(inputs["L0_ffn_W_v"], np.float32).T),
        "lam": _fm(inputs["L0_attn_lam"]),
        "big": _fm(inputs["L0_attn_b_ig"]),
        "brg": _fm(inputs["L0_attn_b_rg"]),
        "mur": _fm(inputs["L0_ffn_mu_r"]),
        "muk": _fm(inputs["L0_ffn_mu_k"]),
        "h0": _fm(np.asarray(inputs["L0_attn_h0"], np.float32).reshape(DM)),
        "nw": _fm(inputs["L0_attn_norm_w"]),
    }
    in_maps = []
    for b in range(NCORE):
        xb = x[b]
        xt = np.ascontiguousarray(
            xb.reshape(NSPAN, SPAN, NCH, 128).transpose(0, 3, 2, 1)
        ).reshape(NSPAN, 128, NCH * SPAN)
        m = dict(shared)
        m["xtm"] = np.ascontiguousarray(xb)
        m["xt"] = xt
        in_maps.append(m)

    nc = _get_nc()
    res = run_bass_kernel_spmd(nc, in_maps, core_ids=list(range(NCORE)))
    out = np.stack([np.asarray(r["out"], np.float32) for r in res.results])
    return out
